# version 33
# baseline (speedup 1.0000x reference)
import contextlib
import numpy as np
import concourse.bass as bass
import concourse.mybir as mybir
from concourse.bass_utils import run_bass_kernel_spmd

F32 = mybir.dt.float32
BF16 = mybir.dt.bfloat16
AF = mybir.ActivationFunctionType
ALU = mybir.AluOpType
AX = mybir.AxisListType

NCORE = 8
TOK = 2048
HALO = 3
TT = TOK + HALO
W = 2052
D = 1024
NCH_D = 8
NCH_I = 16
NCH_F = 32
NE = 4
EPS_LN = 1e-5
EPS_AFF = 1e-12
INVD = 1.0 / 1024.0

CBS5 = [(0, 512), (512, 512), (1024, 512), (1536, 512), (2048, 3)]
CBS4 = [(3, 512), (515, 512), (1027, 512), (1539, 512)]
NBLK = 4
SBUF_BASE = 18432
SBUF_LIMIT = 228000


def pack_weights(inp):
    W_in = np.asarray(inp["L0_attn_W_in"], np.float32)
    conv_w = np.asarray(inp["L0_attn_conv_w"], np.float32)
    conv_b = np.asarray(inp["L0_attn_conv_b"], np.float32)
    W_x = np.asarray(inp["L0_attn_W_x"], np.float32)
    W_dt = np.asarray(inp["L0_attn_W_dt"], np.float32)
    b_dt = np.asarray(inp["L0_attn_b_dt"], np.float32)
    W_out = np.asarray(inp["L0_attn_W_out"], np.float32)
    W_r = np.asarray(inp["L0_ffn_W_router"], np.float32)
    W1 = np.asarray(inp["L0_ffn_W1"], np.float32)
    b1 = np.asarray(inp["L0_ffn_b1"], np.float32)
    Wo = np.asarray(inp["L0_ffn_Wo"], np.float32)
    bo = np.asarray(inp["L0_ffn_bo"], np.float32)
    gam = np.asarray(inp["L0_ffn_ln_gamma"], np.float32)
    bet = np.asarray(inp["L0_ffn_ln_beta"], np.float32)
    C = np.ascontiguousarray
    d = {}
    d["w_in_t"] = C(W_in.reshape(32, 128, 8, 128).transpose(0, 3, 2, 1))
    dg = np.zeros((16, 4, 128, 128), np.float32)
    ii = np.arange(128)
    for c in range(16):
        for t in range(4):
            dg[c, t, ii, ii] = conv_w[c * 128:(c + 1) * 128, 0, t]
    d["diag_t"] = C(dg.transpose(2, 0, 1, 3))
    d["conv_b_p"] = C(conv_b.reshape(16, 128).T)
    d["w_x_t"] = C(W_x[:32].T.reshape(16, 128, 32).transpose(1, 0, 2))
    d["w_dt_t"] = C(W_dt.T.reshape(32, 16, 128))
    d["b_dt_p"] = C(b_dt.reshape(16, 128).T)
    d["w_out_t"] = C(W_out.reshape(8, 128, 16, 128).transpose(0, 3, 2, 1))
    d["w_r_t"] = C(W_r.reshape(8, 128, 4).transpose(1, 0, 2))
    w1t = W1.reshape(4, 8, 128, 32, 128).transpose(3, 0, 1, 2, 4).reshape(32, 32, 128, 128)
    b1pad = np.zeros((32, 1, 128, 128), np.float32)
    b1pad[:, 0, 0:4, :] = b1.reshape(4, 32, 128).transpose(1, 0, 2)
    d["w1b_t"] = C(np.concatenate([w1t, b1pad], axis=1).transpose(0, 2, 1, 3))
    d["wo_t"] = C(Wo.reshape(32, 128, 8, 128).transpose(2, 1, 0, 3))
    d["bo_p"] = C(bo.reshape(8, 128).T)
    d["gam_p"] = C(gam.reshape(8, 128).T)
    d["bet_p"] = C(bet.reshape(8, 128).T)
    d["ident"] = np.eye(128, dtype=np.float32)
    return d


class Kern:
    def __init__(self, debug=False):
        self.debug = debug
        self.nc = bass.Bass("TRN2", target_bir_lowering=False)
        self._sem = {}
        self.T = {}
        self.psg = [None] * 8
        self.ps_held = set()
        self.rot = [0, 1, 2, 3, 4, 5]
        self.rot_i = 0
        self.cnt = dict(dx=0, mm=0, ve=0, se=0)
        self.ffn_slot_i = 0
        self.ffn_slot_user = [None, None, None]

    def sb(self, name, shape, dtype, off):
        t = self.nc.alloc_sbuf_tensor_at(name, shape, dtype, offset=off)
        self.T[name] = t
        return t

    def inc(self, ins, which, k=1):
        ins.then_inc(self._sem[which], k)
        self.cnt[which] += k
        return self.cnt[which]

    def w(self, eng, which, target):
        if target:
            eng.wait_ge(self._sem[which], target)

    def ps_acquire(self, te):
        for _ in range(len(self.rot) + 1):
            i = self.rot[self.rot_i % len(self.rot)]
            self.rot_i += 1
            if i not in self.ps_held:
                break
        else:
            raise RuntimeError("all psum banks held")
        g = self.psg[i]
        if g is not None:
            self.w(te, g[0], g[1])
            self.psg[i] = None
        return self.PS[i], i

    def ps_release(self, i, which):
        self.psg[i] = (which, self.cnt[which])
        self.ps_held.discard(i)

    def ps_guard_wait(self, te, i):
        g = self.psg[i]
        if g is not None:
            self.w(te, g[0], g[1])
            self.psg[i] = None

    def TE(self, f):
        self.block.tensor(f)

    def VE(self, f):
        self.block.vector(f)

    def SC(self, f):
        self.block.scalar(f)

    def GP(self, f):
        self.block.gpsimd(f)

    def SY(self, f):
        self.block.sync(f)

    def build(self):
        nc = self.nc
        self._es = contextlib.ExitStack()
        es = self._es
        dr = {}
        dr["x_loc"] = nc.dram_tensor("x_loc", [TT, D], F32, kind="ExternalInput").ap()
        dr["w_in_t"] = nc.dram_tensor("w_in_t", [32, 128, 8, 128], F32, kind="ExternalInput").ap()
        dr["diag_t"] = nc.dram_tensor("diag_t", [128, 16, 4, 128], F32, kind="ExternalInput").ap()
        dr["conv_b_p"] = nc.dram_tensor("conv_b_p", [128, 16], F32, kind="ExternalInput").ap()
        dr["w_x_t"] = nc.dram_tensor("w_x_t", [128, 16, 32], F32, kind="ExternalInput").ap()
        dr["w_dt_t"] = nc.dram_tensor("w_dt_t", [32, 16, 128], F32, kind="ExternalInput").ap()
        dr["b_dt_p"] = nc.dram_tensor("b_dt_p", [128, 16], F32, kind="ExternalInput").ap()
        dr["w_out_t"] = nc.dram_tensor("w_out_t", [8, 128, 16, 128], F32, kind="ExternalInput").ap()
        dr["w_r_t"] = nc.dram_tensor("w_r_t", [128, 8, 4], F32, kind="ExternalInput").ap()
        dr["w1b_t"] = nc.dram_tensor("w1b_t", [32, 128, 33, 128], F32, kind="ExternalInput").ap()
        dr["wo_t"] = nc.dram_tensor("wo_t", [8, 128, 32, 128], F32, kind="ExternalInput").ap()
        dr["bo_p"] = nc.dram_tensor("bo_p", [128, 8], F32, kind="ExternalInput").ap()
        dr["gam_p"] = nc.dram_tensor("gam_p", [128, 8], F32, kind="ExternalInput").ap()
        dr["bet_p"] = nc.dram_tensor("bet_p", [128, 8], F32, kind="ExternalInput").ap()
        dr["ident"] = nc.dram_tensor("ident", [128, 128], F32, kind="ExternalInput").ap()
        dr["y_out"] = nc.dram_tensor("y_out", [TOK, D], F32, kind="ExternalOutput").ap()
        self.dr = dr
        self.dbg = {}
        if self.debug:
            for nm, shp, dt in [("dbg_ln1", [128, NCH_D, TT], F32),
                                ("dbg_xc", [128, NCH_I, TOK], F32),
                                ("dbg_x2", [128, NCH_D, TOK], F32),
                                ("dbg_ln2", [128, NCH_D * 512], F32),
                                ("dbg_mask", [4, 512], F32),
                                ("dbg_hsel", [128, NCH_F * 512], F32),
                                ("dbg_y", [128, NCH_D * 512], F32)]:
                self.dbg[nm] = nc.dram_tensor(nm, shp, dt, kind="ExternalOutput").ap()

        o = SBUF_BASE
        def take(n):
            nonlocal o
            o = (o + 31) // 32 * 32
            r = o
            o += n
            return r
        self.sb("id32", [128, 128], F32, take(512))
        self.sb("ones_row", [1, 128], BF16, take(256))
        self.sb("inv_col", [128, 1], BF16, take(32))
        self.sb("inv_colf", [128, 1], F32, take(32))
        self.sb("conv_b_sb", [128, 16], F32, take(64))
        self.sb("b_dt_sb", [128, 16], F32, take(64))
        self.sb("bo_sb", [128, 8], F32, take(32))
        self.sb("gam_sb", [128, 8], F32, take(32))
        self.sb("bet_sb", [128, 8], F32, take(32))
        self.sb("eps5_c", [1, 8], F32, take(32))
        self.sb("eps12_c", [1, 8], F32, take(32))
        self.sb("wrt_sb", [128, 32], BF16, take(64))
        self.sb("ones_col", [128, 1], BF16, take(32))
        self.sb("cs_row", [1, 4], BF16, take(32))
        self.sb("wx_sb", [128, 512], BF16, take(1024))
        off_x = take(NCH_D * W * 4)
        off_l = take(NCH_D * W * 2)
        off_c = take(NCH_I * W * 2)
        self.sb("x_fm", [128, NCH_D * W], F32, off_x)
        self.sb("ln1", [128, NCH_D * W], BF16, off_l)
        self.sb("xcyz", [128, NCH_I * W], BF16, off_c)
        SB = take(0)
        self.SB = SB
        avail = SBUF_LIMIT - SB
        self.sb("stg0", [128, D], F32, SB + 16512)
        self.sb("stg1", [128, D], F32, SB + 20608)
        self.sb("stg2", [128, D], F32, SB + 24704)
        self.sb("r0w", [1, W], F32, SB + 0)
        self.sb("r1w", [1, W], F32, SB + 8256)
        self.sb("rb0w", [1, W], BF16, SB + 8256)
        self.sb("rb1w", [1, W], BF16, SB + 8256 + 4128)
        self.sb("r2w", [1, W], F32, SB + 16512)
        self.sb("r3w", [1, W], F32, SB + 24768)
        self.sb("sqs0", [128, 1024], BF16, SB + 33024)
        self.sb("sqs1", [128, 1024], BF16, SB + 35072)
        assert SB + 37120 <= SBUF_LIMIT, (SB, avail)
        self.sb("win0", [128, 1024], BF16, SB + 38976)
        self.sb("win1", [128, 1024], BF16, SB + 41024)
        assert SB + 43072 <= SBUF_LIMIT
        self.sb("xs_tmp", [128, W], BF16, SB + 4096)
        self.sb("sz_tmp", [128, TOK], BF16, SB + 8224)
        self.sb("diag_sb", [128, 64 * 128], BF16, SB + 12320)
        self.sb("wdt_sb", [32, 2048], BF16, SB + 28704)
        self.sb("a0", [128, 512], BF16, SB + 32800)
        self.sb("a1", [128, 512], BF16, SB + 33824)
        self.sb("dt_sb", [32, 2048], BF16, SB + 34848)
        assert SB + 38944 <= SBUF_LIMIT
        self.sb("wout0", [128, 4096], BF16, SB + 0)
        self.sb("wout1", [128, 4096], BF16, SB + 8192)
        self.sb("arcb", [128, NCH_D * 512], BF16, SB + 16512)
        self.sb("r1o", [1, 512], F32, SB + 24704)
        self.sb("rb0o", [1, 512], BF16, SB + 24704)
        self.sb("rb1o", [1, 512], BF16, SB + 25728)
        self.sb("sqo0", [128, 512], BF16, SB + 26752)
        self.sb("r2o", [1, 512], F32, SB + 38976)
        self.sb("r3o", [1, 512], F32, SB + 41024)
        self.sb("sqo1", [128, 512], BF16, SB + 42048)
        self.sb("wf0", [128, 33 * 128], BF16, SB + 0)
        self.sb("wf1", [128, 33 * 128], BF16, SB + 8448)
        self.sb("wf2", [128, 33 * 128], BF16, SB + 16896)
        self.sb("r0f", [1, 512], F32, SB + 25344)
        self.sb("r1f", [1, 512], F32, SB + 27392)
        self.sb("rb0f", [1, 512], BF16, SB + 27392)
        self.sb("rb1f", [1, 512], BF16, SB + 28416)
        self.sb("r2f", [1, 512], F32, SB + 29440)
        self.sb("r3f", [1, 512], F32, SB + 31488)
        self.sb("sqf0", [128, 512], BF16, SB + 25344)
        self.sb("sqf1", [128, 512], BF16, SB + 26368)
        self.sb("ln2b", [128, NCH_D * 512], BF16, SB + 33536)
        assert SB + 41728 <= SBUF_LIMIT
        assert SB + 35584 <= SBUF_LIMIT
        self.sb("xm", [128, NE * NCH_D * 512], BF16, off_x)
        self.sb("hsel", [128, NCH_F * 512], BF16, off_x + 32768)
        self.sb("ln2", [128, NCH_D * 512], BF16, off_l)
        self.sb("mrep", [128, NE * 512], BF16, off_l + 8192)
        self.sb("out_tm", [128, 4 * 1024], F32, off_l + 12288)
        self.sb("lt_tm", [128, 16], F32, off_l + 28672)
        self.sb("rmax", [128, 4], F32, off_l + 28736)
        self.sb("sel", [128, 16], F32, off_l + 28768)
        self.sb("notac", [128, 4], F32, off_l + 28832)
        self.sb("tmp4", [128, 4], F32, off_l + 28864)
        self.sb("mask4b", [4, 512], BF16, off_l + 28896)
        self.sb("lg_sb", [4, 512], F32, off_l + 29920)
        self.sb("x2", [128, NCH_D * W], BF16, off_c)
        self.sb("yffn", [128, NCH_D * 512], F32, off_c + 32832)
        self.sb("fret", [128, NCH_D * 512], BF16, off_c + 49216)
        self.sb("x3t", [128, 512], F32, off_c + 57408)
        self.sb("mrow4", [1, NE * 512], BF16, off_c + 59456)
        self.sb("aff_tap", [128, 512], F32, off_c + 63552)

        self.PS = [es.enter_context(nc.psum_tensor(f"ps{i}", [128, 512], F32))
                   for i in range(8)]
        for s in ("dx", "mm", "ve", "se", "gx", "stgA", "stgB", "stgC", "win0", "win1",
                  "wout0", "wout1", "wf0", "wf1", "wf2", "diag", "smallw",
                  "wdt", "dbg"):
            self._sem[s] = es.enter_context(nc.semaphore(s))
            self.cnt.setdefault(s, 0)
        self.block = es.enter_context(nc.Block())
        try:
            self.emit_setup()
            self.emit_small_weights()
            self.emit_ln1()
            self.emit_mix()
            self.emit_dt_a()
            self.emit_out_mid()
            self.emit_ffn_all()
            totals = {k: self.cnt[k] for k in self._sem if self.cnt.get(k)}
            def fin(s):
                for k, v in totals.items():
                    if k in ("mm", "ve", "se"):
                        continue
                    s.wait_ge(self._sem[k], v)
            self.SY(fin)
        finally:
            self._es.close()
        return nc

    def emit_setup(self):
        T, dr = self.T, self.dr
        def sy(s):
            for dst, srcn in [("id32", "ident"), ("conv_b_sb", "conv_b_p"),
                              ("b_dt_sb", "b_dt_p"), ("bo_sb", "bo_p"),
                              ("gam_sb", "gam_p"), ("bet_sb", "bet_p")]:
                ins = s.dma_start(out=T[dst][:], in_=dr[srcn][:])
                self.inc(ins, "dx", 16)
        self.SY(sy)
        self.const_dx = self.cnt["dx"]

        def ve0(v):
            v.memset(T["ones_row"][:], 1.0)
            v.memset(T["ones_col"][:], 1.0)
            v.memset(T["eps5_c"][:], EPS_LN)
            v.memset(T["eps12_c"][:], EPS_AFF)
            v.memset(T["inv_col"][:], INVD)
            ins = v.memset(T["inv_colf"][:], INVD)
            self.inc(ins, "ve")
        self.VE(ve0)
        self.ones_ready = self.cnt["ve"]

        stg = [T["stg0"], T["stg1"], T["stg2"]]
        self.xtile_ve = {}
        xv = T["x_fm"][:].rearrange("p (c w) -> p c w", w=W)
        tile_done_mm = [0] * 17
        for t in range(17):
            rows = 128 if t < 16 else 3
            stg_sem = ("stgA", "stgB", "stgC")[t % 3]
            def sy2(s, t=t, rows=rows, stg_sem=stg_sem):
                if t >= 3:
                    self.w(s, "mm", tile_done_mm[t - 3])
                ins = s.dma_start(out=stg[t % 3][0:rows, :],
                                  in_=dr["x_loc"][t * 128:t * 128 + rows, :])
                self.inc(ins, stg_sem, 16)
            self.SY(sy2)
            dx_t = (stg_sem, self.cnt[stg_sem])
            evi = []
            def te(te, t=t, rows=rows, dx_t=dx_t):
                if t == 0:
                    self.w(te, "dx", self.const_dx)
                self.w(te, dx_t[0], dx_t[1])
                for half in range(2):
                    P, pi = self.ps_acquire(te)
                    for j in range(4):
                        ch = half * 4 + j
                        ins = te.transpose(P[:, j * rows:(j + 1) * rows],
                                           stg[t % 3][0:rows, ch * 128:(ch + 1) * 128],
                                           T["id32"][0:rows, 0:rows])
                    self.inc(ins, "mm")
                    evi.append((pi, self.cnt["mm"], half))
            self.TE(te)
            tile_done_mm[t] = self.cnt["mm"]
            def ve(v, t=t, rows=rows, evi=evi):
                for pi, mmt, half in evi:
                    self.w(v, "mm", mmt)
                    src = self.PS[pi][:, 0:4 * rows].rearrange("p (c r) -> p c r", c=4)
                    dst = xv[:, half * 4:half * 4 + 4, t * 128:t * 128 + rows]
                    ins = v.tensor_copy(dst, src)
                    self.inc(ins, "ve")
                    self.ps_release(pi, "ve")
            self.VE(ve)
            self.xtile_ve[t] = self.cnt["ve"]
        self.xfm_ready = self.cnt["ve"]

    def emit_small_weights(self):
        T, dr = self.T, self.dr
        def gp(g):
            ins = g.dma_start(out=T["wrt_sb"][:].rearrange("p (k j) -> p k j", j=4),
                              in_=dr["w_r_t"][:])
            self.inc(ins, "smallw", 16)
            ins = g.dma_start(out=T["wx_sb"][:].rearrange("p (k f) -> p k f", f=32),
                              in_=dr["w_x_t"][:])
            self.inc(ins, "smallw", 16)
        self.GP(gp)
        self.small_w_dw = self.cnt["smallw"]
        res = {}
        def te_cs(te):
            self.w(te, "smallw", self.small_w_dw)
            self.w(te, "ve", self.ones_ready)
            P, pi = self.ps_acquire(te)
            for k in range(8):
                ins = te.matmul(P[0:4, 0:1], T["wrt_sb"][:, 4 * k:4 * k + 4],
                                T["ones_col"][:], start=(k == 0), stop=(k == 7))
            self.inc(ins, "mm")
            res["a"] = (pi, self.cnt["mm"])
        self.TE(te_cs)
        pi_a, mm_a = res["a"]
        def ve_cs(v):
            self.w(v, "mm", mm_a)
            ins = v.tensor_copy(T["lg_sb"][0:4, 0:1], self.PS[pi_a][0:4, 0:1])
            self.inc(ins, "ve")
            self.ps_release(pi_a, "ve")
        self.VE(ve_cs)
        cs_ve = self.cnt["ve"]
        def te_cs2(te):
            self.w(te, "ve", cs_ve)
            P, pi = self.ps_acquire(te)
            ins = te.transpose(P[0:1, 0:4], T["lg_sb"][0:4, 0:1], T["id32"][0:4, 0:4])
            self.inc(ins, "mm")
            res["b"] = (pi, self.cnt["mm"])
        self.TE(te_cs2)
        pi_b, mm_b = res["b"]
        def ve_cs2(v):
            self.w(v, "mm", mm_b)
            ins = v.tensor_copy(T["cs_row"][0:1, 0:4], self.PS[pi_b][0:1, 0:4])
            self.inc(ins, "ve")
            self.ps_release(pi_b, "ve")
        self.VE(ve_cs2)
        self.cs_mm = self.cnt["mm"]
        self.cs_ve = self.cnt["ve"]

    def ln_stats(self, cbs, nch, src_ap, sq_slots, copy_slots=None,
                 first_waits=(), rows=None, key=None, sq_vector=False,
                 keep_psum=False, cb_waits=None):
        r0, r1 = rows if rows is not None else (None, None)
        guards = self._slot_guard
        key_sq = key or tuple(id(s) for s in sq_slots)
        prev = guards.get(key_sq, [0, 0])
        for ci, (c0, n) in enumerate(cbs):
            mm_base = self.cnt["mm"]
            pred = [mm_base + ch + 1 for ch in range(nch)]
            ve_t = [0] * nch
            se_t = [0] * nch
            if copy_slots is not None:
                def ve(v, ci=ci, c0=c0, n=n, prev=list(prev), pred=pred):
                    for ch in range(nch):
                        sl = ch % 2
                        g = prev[sl] if ch < 2 else pred[ch - 2]
                        self.w(v, "mm", g)
                        if ci == 0 and ch == 0:
                            for wk, wt in first_waits:
                                self.w(v, wk, wt)
                        if ch == 0 and cb_waits is not None:
                            for wk, wt in cb_waits(ci):
                                self.w(v, wk, wt)
                        ins = v.tensor_copy(copy_slots[sl][:, 0:n], src_ap(ch, c0, n))
                        ve_t[ch] = self.inc(ins, "ve")
                self.VE(ve)
            if sq_vector:
                def vsq(v, ci=ci, c0=c0, n=n, prev=list(prev), pred=pred):
                    for ch in range(nch):
                        sl = ch % 2
                        g = prev[sl] if ch < 2 else pred[ch - 2]
                        self.w(v, "mm", g)
                        if ci == 0 and ch == 0:
                            for wk, wt in first_waits:
                                self.w(v, wk, wt)
                        sap = src_ap(ch, c0, n)
                        ins = v.tensor_mul(sq_slots[sl][:, 0:n], sap, sap)
                        se_t[ch] = 0
                        ve_t[ch] = self.inc(ins, "ve")
                self.VE(vsq)
            else:
                def sc(s, ci=ci, c0=c0, n=n, prev=list(prev), pred=pred):
                    for ch in range(nch):
                        sl = ch % 2
                        g = prev[sl] if ch < 2 else pred[ch - 2]
                        self.w(s, "mm", g)
                        if ci == 0 and ch == 0:
                            for wk, wt in first_waits:
                                self.w(s, wk, wt)
                        if ch == 0 and cb_waits is not None:
                            for wk, wt in cb_waits(ci):
                                self.w(s, wk, wt)
                        ins = s.activation(sq_slots[sl][:, 0:n], src_ap(ch, c0, n),
                                           AF.Square)
                        se_t[ch] = self.inc(ins, "se")
                self.SC(sc)
            def te(te, ci=ci, c0=c0, n=n, pred=pred):
                for ch in range(nch):
                    sl = ch % 2
                    if se_t[ch]:
                        self.w(te, "se", se_t[ch])
                    if ve_t[ch]:
                        self.w(te, "ve", ve_t[ch])
                    if ch == 0:
                        self.ps_guard_wait(te, 6)
                        self.ps_guard_wait(te, 7)
                    mean_rhs = (copy_slots[ch % 2][:, 0:n] if copy_slots is not None
                                else src_ap(ch, c0, n))
                    te.matmul(self.PS[6][0:1, 0:n], self.T["inv_col"][:], mean_rhs,
                              start=(ch == 0), stop=(ch == nch - 1), skip_group_check=True)
                    ins = te.matmul(self.PS[7][0:1, 0:n], self.T["inv_col"][:],
                                    sq_slots[sl][:, 0:n],
                                    start=(ch == 0), stop=(ch == nch - 1),
                                    skip_group_check=True)
                    self.inc(ins, "mm")
                assert self.cnt["mm"] == pred[-1], (self.cnt["mm"], pred[-1])
            self.TE(te)
            prev = [pred[nch - 2] if nch >= 2 else pred[-1], pred[nch - 1]]
            if keep_psum:
                self.psg[6] = ("mm", self.cnt["mm"])
                self.psg[7] = ("mm", self.cnt["mm"])
            else:
                mmt = self.cnt["mm"]
                def ve2(v, c0=c0, n=n, mmt=mmt):
                    self.w(v, "mm", mmt)
                    v.tensor_copy(r0[0:1, c0:c0 + n], self.PS[6][0:1, 0:n])
                    ins = v.tensor_copy(r1[0:1, c0:c0 + n], self.PS[7][0:1, 0:n])
                    self.inc(ins, "ve")
                    self.ps_release(6, "ve")
                    self.ps_release(7, "ve")
                self.VE(ve2)
        guards[key_sq] = prev

    def ln_smalls2(self, r2, r3, rb0, rb1, width, eps, stats_mm):
        N = width
        res = {}
        entry_ve = self.cnt["ve"]
        entry_se = self.cnt["se"]
        def sc1(s):
            self.w(s, "ve", entry_ve)
            self.w(s, "se", entry_se)
            self.w(s, "mm", stats_mm)
            ins = s.activation(r2[0:1, 0:N], self.PS[6][0:1, 0:N], AF.Square)
            res["s1"] = self.inc(ins, "se")
        self.SC(sc1)
        def ve(v):
            self.w(v, "se", res["s1"])
            ins = v.scalar_tensor_tensor(r3[0:1, 0:N], self.PS[7][0:1, 0:N], float(eps),
                                         r2[0:1, 0:N], ALU.add, ALU.subtract)
            res["v_stt"] = self.inc(ins, "ve")
            ins = v.tensor_copy(rb0[0:1, 0:N], self.PS[6][0:1, 0:N])
            res["v1"] = self.inc(ins, "ve")
            self.ps_release(6, "ve")
            self.ps_release(7, "ve")
        self.VE(ve)
        def sc2(s):
            self.w(s, "ve", res["v_stt"])
            ins = s.activation(r2[0:1, 0:N], r3[0:1, 0:N], AF.Ln)
            t = self.inc(ins, "se")
            self.w(s, "se", t)
            ins = s.activation(rb1[0:1, 0:N], r2[0:1, 0:N], AF.Exp, scale=-0.5)
            self.inc(ins, "se")
        self.SC(sc2)
        return ("se", self.cnt["se"], "ve", res["v1"])

    def ln_bcast2(self, rbs, c0, n, ready, stats_banks=False):
        rb0, rb1 = rbs
        res = {}
        def te(te):
            if stats_banks:
                Pm, im = self.PS[6], 6
                Pr, ir = self.PS[7], 7
                self.ps_guard_wait(te, 6)
                self.ps_guard_wait(te, 7)
            else:
                Pm, im = self.ps_acquire(te)
                Pr, ir = self.ps_acquire(te)
            self.ps_held.add(im)
            self.ps_held.add(ir)
            for j in range(0, len(ready), 2):
                self.w(te, ready[j], ready[j + 1])
            te.matmul(Pm[:, 0:n], self.T["ones_row"][:], rb0[0:1, c0:c0 + n],
                      start=True, stop=True)
            ins = te.matmul(Pr[:, 0:n], self.T["ones_row"][:], rb1[0:1, c0:c0 + n],
                            start=True, stop=True)
            self.inc(ins, "mm")
            res["r"] = (Pm, im, Pr, ir, self.cnt["mm"])
        self.TE(te)
        return res["r"]

    def ln_smalls(self, rows, rbs, width, eps):
        r0, r1, r2, r3 = rows
        rb0, rb1 = rbs
        N = width
        res = {}
        entry_ve = self.cnt["ve"]
        entry_se = self.cnt["se"]
        def sc1(s):
            self.w(s, "ve", entry_ve)
            self.w(s, "se", entry_se)
            ins = s.activation(r2[0:1, 0:N], r0[0:1, 0:N], AF.Square)
            res["s1"] = self.inc(ins, "se")
        self.SC(sc1)
        def ve(v):
            self.w(v, "se", res["s1"])
            ins = v.scalar_tensor_tensor(r3[0:1, 0:N], r1[0:1, 0:N], float(eps),
                                         r2[0:1, 0:N], ALU.add, ALU.subtract)
            res["v_stt"] = self.inc(ins, "ve")
            self.w(v, "ve", res["v_stt"])
            ins = v.tensor_copy(rb0[0:1, 0:N], r0[0:1, 0:N])
            res["v1"] = self.inc(ins, "ve")
        self.VE(ve)
        def sc2(s):
            self.w(s, "ve", res["v_stt"])
            ins = s.activation(r2[0:1, 0:N], r3[0:1, 0:N], AF.Ln)
            t = self.inc(ins, "se")
            self.w(s, "se", t)
            ins = s.activation(rb1[0:1, 0:N], r2[0:1, 0:N], AF.Exp, scale=-0.5)
            self.inc(ins, "se")
        self.SC(sc2)
        return ("se", self.cnt["se"], "ve", res["v1"])

    def ln_bcast(self, rbs, c0, n, ready_ve):
        rb0, rb1 = rbs
        res = {}
        def te(te):
            Pm, im = self.ps_acquire(te)
            Pr, ir = self.ps_acquire(te)
            self.ps_held.add(im)
            self.ps_held.add(ir)
            self.w(te, "ve", ready_ve)
            te.matmul(Pm[:, 0:n], self.T["ones_row"][:], rb0[0:1, c0:c0 + n],
                      start=True, stop=True)
            ins = te.matmul(Pr[:, 0:n], self.T["ones_row"][:], rb1[0:1, c0:c0 + n],
                            start=True, stop=True)
            self.inc(ins, "mm")
            res["r"] = (Pm, im, Pr, ir, self.cnt["mm"])
        self.TE(te)
        return res["r"]

    def emit_ln1(self):
        T = self.T
        xv = T["x_fm"][:].rearrange("p (c w) -> p c w", w=W)
        lv = T["ln1"][:].rearrange("p (c w) -> p c w", w=W)
        sq_slots = [self._view(T["sqs0"], 512), self._view(T["sqs1"], 512)]
        cp_slots = [self._view(T["sqs0"], 0), self._view(T["sqs1"], 0)]
        def cbw(ci):
            c0, n = CBS5[ci]
            last_tile = (c0 + n - 1) // 128
            return (("ve", self.xtile_ve[last_tile]),)
        self.ln_stats(CBS5, NCH_D,
                      lambda ch, c0, n: xv[:, ch, c0:c0 + n],
                      sq_slots, copy_slots=cp_slots,
                      first_waits=(("ve", self.ones_ready),),
                      rows=(T["r0w"], T["r1w"]), key="sqs", cb_waits=cbw)
        ready = self.ln_smalls((T["r0w"], T["r1w"], T["r2w"], T["r3w"]),
                               (T["rb0w"], T["rb1w"]), TT, EPS_LN)
        self.ln1_cb_ve = {}
        for (c0, n) in CBS5:
            Pm, im, Pr, ir, mmt = self.ln_bcast2((T["rb0w"], T["rb1w"]), c0, n, ready)
            def ve(v, c0=c0, n=n, Pm=Pm, Pr=Pr, im=im, ir=ir, mmt=mmt):
                self.w(v, "mm", mmt)
                for cc in range(0, NCH_D, 2):
                    ts = []
                    for ch in (cc, cc + 1):
                        ins = v.tensor_sub(lv[:, ch, c0:c0 + n], xv[:, ch, c0:c0 + n],
                                           Pm[:, 0:n])
                        ts.append(self.inc(ins, "ve"))
                    for ch, t in zip((cc, cc + 1), ts):
                        self.w(v, "ve", t)
                        ins = v.tensor_mul(lv[:, ch, c0:c0 + n], lv[:, ch, c0:c0 + n],
                                           Pr[:, 0:n])
                        self.inc(ins, "ve")
                self.ps_release(im, "ve")
                self.ps_release(ir, "ve")
            self.VE(ve)
            self.ln1_cb_ve[c0] = self.cnt["ve"]
        self.ln1_ready = self.cnt["ve"]
        if self.debug:
            def gp(g):
                self.w(g, "ve", self.ln1_ready)
                self.w(g, "dbg", self.cnt["dbg"])
                ins = g.dma_start(out=self.dbg["dbg_ln1"],
                                  in_=T["ln1"][:].rearrange("p (c w) -> p c w", w=W)[:, :, 0:TT])
                self.inc(ins, "dbg", 16)
            self.GP(gp)

    def emit_mix(self):
        T = self.T
        lv = T["ln1"][:].rearrange("p (c w) -> p c w", w=W)
        cv = T["xcyz"][:].rearrange("p (c w) -> p c w", w=W)
        win = [T["win0"], T["win1"]]
        slot_user = [None, None]
        self.xc_se = {}
        self.pz_ve = {}
        rot_save = self.rot
        self.rot = [0, 1, 2]
        xp_started = False
        prev_conv_mm = 0
        dma_i = 0

        for ch in range(16):
            dws = {}
            for mi, m in enumerate((ch, 16 + ch)):
                sl = dma_i % 2
                ssem = "win0" if sl == 0 else "win1"
                def gp(g, m=m, sl=sl, ssem=ssem):
                    if slot_user[sl] is not None:
                        self.w(g, "mm", slot_user[sl])
                    ins = g.dma_start(out=win[sl][:].rearrange("p (k j) -> p k j", j=128),
                                      in_=self.dr["w_in_t"][m])
                    self.inc(ins, ssem, 16)
                self.GP(gp)
                dws[m] = (sl, ssem, self.cnt[ssem])
                dma_i += 1
            if ch == 0:
                def gp_d(g):
                    self.w(g, "ve", self.ln1_ready)
                    ins = g.dma_start(out=T["diag_sb"][:].rearrange(
                        "p (c t j) -> p c t j", c=16, t=4),
                                      in_=self.dr["diag_t"][:])
                    self.inc(ins, "diag", 16)
                self.GP(gp_d)
                self.diag_dw = self.cnt["diag"]

            sl_xs, ssem_xs, dw_xs = dws[ch]
            evacs = []
            V0 = self.cnt["ve"]
            def te_xs(te, sl_xs=sl_xs, ssem_xs=ssem_xs, dw_xs=dw_xs, V0=V0):
                self.w(te, ssem_xs, dw_xs)
                for j, (c0, n) in enumerate(CBS5):
                    self.w(te, "ve", self.ln1_cb_ve[c0])
                    P, pi = self.ps_acquire(te)
                    for k in range(8):
                        ins = te.matmul(P[:, 0:n], win[sl_xs][:, k * 128:(k + 1) * 128],
                                        lv[:, k, c0:c0 + n], start=(k == 0), stop=(k == 7))
                    self.inc(ins, "mm")
                    evacs.append((pi, self.cnt["mm"], c0, n))
                    self.psg[pi] = ("ve", V0 + j + 1)
            self.TE(te_xs)
            slot_user[sl_xs] = self.cnt["mm"]
            def ve_xs(v, evacs=evacs, prev_conv_mm=prev_conv_mm, V0=V0):
                self.w(v, "mm", prev_conv_mm)
                for pi, mmt, c0, n in evacs:
                    self.w(v, "mm", mmt)
                    ins = v.tensor_copy(T["xs_tmp"][:, c0:c0 + n], self.PS[pi][:, 0:n])
                    self.inc(ins, "ve")
                assert self.cnt["ve"] == V0 + len(evacs)
            self.VE(ve_xs)
            xs_ve = self.cnt["ve"]

            cevacs = []
            S0 = self.cnt["se"]
            def te_cv(te, ch=ch, xs_ve=xs_ve, S0=S0):
                self.w(te, "ve", xs_ve)
                self.w(te, "diag", self.diag_dw)
                for j, (c0, n) in enumerate(CBS4):
                    P, pi = self.ps_acquire(te)
                    for tap in range(4):
                        dcol = (ch * 4 + tap) * 128
                        ins = te.matmul(P[:, 0:n], T["diag_sb"][:, dcol:dcol + 128],
                                        T["xs_tmp"][:, c0 - 3 + tap:c0 - 3 + tap + n],
                                        start=(tap == 0), stop=(tap == 3))
                    self.inc(ins, "mm")
                    cevacs.append((pi, self.cnt["mm"], c0, n))
                    self.psg[pi] = ("se", S0 + j + 1)
            self.TE(te_cv)
            prev_conv_mm = self.cnt["mm"]
            self.conv_last_mm = prev_conv_mm
            def sc_cv(s, ch=ch, cevacs=cevacs, S0=S0):
                for pi, mmt, c0, n in cevacs:
                    self.w(s, "mm", mmt)
                    ins = s.activation(cv[:, ch, c0:c0 + n], self.PS[pi][:, 0:n], AF.Silu,
                                       bias=T["conv_b_sb"][:, ch:ch + 1])
                    self.inc(ins, "se")
                assert self.cnt["se"] == S0 + len(cevacs)
            self.SC(sc_cv)
            xc_se = self.cnt["se"]
            self.xc_se[ch] = xc_se

            def te_xp(te, ch=ch, xc_se=xc_se, xp_started=xp_started):
                self.w(te, "se", xc_se)
                for i, (c0, n) in enumerate(CBS4):
                    pi = 3 + i
                    if not xp_started:
                        self.ps_guard_wait(te, pi)
                    wcol = ch * 32
                    ins = te.matmul(self.PS[pi][0:32, 0:n], T["wx_sb"][:, wcol:wcol + 32],
                                    cv[:, ch, c0:c0 + n], start=(ch == 0), stop=(ch == 15),
                                    skip_group_check=True)
                    if ch == 15:
                        self.inc(ins, "mm")
            if ch == 0:
                def te_xp0(te, f=te_xp):
                    self.w(te, "smallw", self.small_w_dw)
                    f(te)
                self.TE(te_xp0)
            else:
                self.TE(te_xp)
            xp_started = True
            if ch == 15:
                self.xp_done_mm = self.cnt["mm"]

            sl_z, ssem_z, dw_z = dws[16 + ch]
            zevacs = []
            S0z = self.cnt["se"]
            def te_z(te, sl_z=sl_z, ssem_z=ssem_z, dw_z=dw_z, S0z=S0z):
                self.w(te, ssem_z, dw_z)
                for j, (c0, n) in enumerate(CBS4):
                    self.w(te, "ve", self.ln1_cb_ve[c0 - 3 + 512])
                    P, pi = self.ps_acquire(te)
                    for k in range(8):
                        ins = te.matmul(P[:, 0:n], win[sl_z][:, k * 128:(k + 1) * 128],
                                        lv[:, k, c0:c0 + n], start=(k == 0), stop=(k == 7))
                    self.inc(ins, "mm")
                    zevacs.append((pi, self.cnt["mm"], c0, n))
                    self.psg[pi] = ("se", S0z + j + 1)
            self.TE(te_z)
            slot_user[sl_z] = self.cnt["mm"]
            prev_pz = self.pz_ve.get(ch - 1, 0)
            def sc_z(s, zevacs=zevacs, prev_pz=prev_pz, S0z=S0z):
                self.w(s, "ve", prev_pz)
                for pi, mmt, c0, n in zevacs:
                    self.w(s, "mm", mmt)
                    ins = s.activation(T["sz_tmp"][:, c0 - 3:c0 - 3 + n],
                                       self.PS[pi][:, 0:n], AF.Silu)
                    self.inc(ins, "se")
                assert self.cnt["se"] == S0z + len(zevacs)
            self.SC(sc_z)
            sz_se = self.cnt["se"]
            def ve_pz(v, ch=ch, sz_se=sz_se):
                self.w(v, "se", sz_se)
                for (c0, n) in CBS4:
                    ins = v.tensor_mul(cv[:, ch, c0:c0 + n], cv[:, ch, c0:c0 + n],
                                       T["sz_tmp"][:, c0 - 3:c0 - 3 + n])
                    self.inc(ins, "ve")
            self.VE(ve_pz)
            self.pz_ve[ch] = self.cnt["ve"]
        self.rot = rot_save
        self.mix_done_mm = self.cnt["mm"]
        self.mix_done_ve = self.cnt["ve"]
        self._dbg_xc_t = 0
        if self.debug:
            def gp(g):
                self.w(g, "ve", self.pz_ve[15])
                self.w(g, "dbg", self.cnt["dbg"])
                ins = g.dma_start(out=self.dbg["dbg_xc"],
                                  in_=T["xcyz"][:].rearrange("p (c w) -> p c w", w=W)[:, :, HALO:TT])
                self._dbg_xc_t = self.inc(ins, "dbg", 16)
            self.GP(gp)

    def emit_dt_a(self):
        T = self.T
        cv = T["xcyz"][:].rearrange("p (c w) -> p c w", w=W)
        def gp(g):
            self.w(g, "mm", self.conv_last_mm)
            ins = g.dma_start(out=T["wdt_sb"][:].rearrange("p (m j) -> p m j", j=128),
                              in_=self.dr["w_dt_t"][:])
            self.inc(ins, "wdt", 16)
        self.GP(gp)
        self.wdt_dw = self.cnt["wdt"]

        def ve_dt(v):
            self.w(v, "mm", self.xp_done_mm)
            for i in range(4):
                ins = v.tensor_copy(T["dt_sb"][0:32, i * 512:(i + 1) * 512],
                                    self.PS[3 + i][0:32, 0:512])
                self.inc(ins, "ve")
                self.ps_release(3 + i, "ve")
        self.VE(ve_dt)
        self.dt_ve = self.cnt["ve"]
        self._a_slot_ve = [0, 0]
        self._a_si = 0

    def emit_a_pair(self, i, c0, n, ch0):
        T = self.T
        cv = T["xcyz"][:].rearrange("p (c w) -> p c w", w=W)
        at = [T["a0"], T["a1"]]
        info = []
        for ch in (ch0, ch0 + 1):
            sl = self._a_si % 2
            self._a_si += 1
            res = {}
            def te(te, ch=ch, i=i, n=n, res=res):
                self.w(te, "ve", self.dt_ve)
                self.w(te, "wdt", self.wdt_dw)
                P, pi = self.ps_acquire(te)
                ins = te.matmul(P[:, 0:n], T["wdt_sb"][:, ch * 128:(ch + 1) * 128],
                                T["dt_sb"][0:32, i * 512:i * 512 + n],
                                start=True, stop=True)
                self.inc(ins, "mm")
                res["r"] = (pi, self.cnt["mm"])
            self.TE(te)
            info.append((ch, sl, res["r"]))
        def sc(s, info=info, n=n):
            pes = []
            for ch, sl, (pi, mmt) in info:
                self.w(s, "mm", mmt)
                self.w(s, "ve", self._a_slot_ve[sl])
                ins = s.activation(self.PS[pi][:, 0:n], self.PS[pi][:, 0:n], AF.Exp,
                                   bias=T["b_dt_sb"][:, ch:ch + 1])
                t = self.inc(ins, "se")
                pes.append((pi, t))
            for (ch, sl, (pi, mmt)), (_, t) in zip(info, pes):
                self.w(s, "se", t)
                ins = s.activation(at[sl][:, 0:n], self.PS[pi][:, 0:n], AF.Ln, bias=1.0)
                self.inc(ins, "se")
                self.ps_release(pi, "se")
        self.SC(sc)
        set_ = self.cnt["se"]
        def ve(v, info=info, c0=c0, n=n, set_=set_):
            self.w(v, "se", set_)
            if self.debug and self._dbg_xc_t:
                self.w(v, "dbg", self._dbg_xc_t)
            for ch, sl, _ in info:
                ins = v.tensor_mul(cv[:, ch, c0:c0 + n], cv[:, ch, c0:c0 + n],
                                   at[sl][:, 0:n])
                self.inc(ins, "ve")
        self.VE(ve)
        for ch, sl, _ in info:
            self._a_slot_ve[sl] = self.cnt["ve"]
        return self.cnt["ve"]

    def emit_op_evac(self, info):
        T = self.T
        lv = T["ln1"][:].rearrange("p (c w) -> p c w", w=W)
        arv = T["arcb"][:].rearrange("p (c w) -> p c w", w=512)
        pi, mmt, m, c0, n = info
        def ve(v):
            self.w(v, "mm", mmt)
            ins = v.tensor_add(arv[:, m, 0:n], self.PS[pi][:, 0:n],
                               lv[:, m, c0:c0 + n])
            self.inc(ins, "ve")
            self.ps_release(pi, "ve")
        self.VE(ve)
        return self.cnt["ve"]

    def emit_op_m(self, ci, c0, n, m, yz_cb_ve, defer_evac=False):
        T = self.T
        cv = T["xcyz"][:].rearrange("p (c w) -> p c w", w=W)
        lv = T["ln1"][:].rearrange("p (c w) -> p c w", w=W)
        arv = T["arcb"][:].rearrange("p (c w) -> p c w", w=512)
        wout = [T["wout0"], T["wout1"]]
        pair = (ci * 8 + m) // 2
        sl = pair % 2
        half = m % 2
        osem = "wout0" if sl == 0 else "wout1"
        if half == 0:
            def gp(g, sl=sl, m=m, osem=osem):
                if self.wout_slot_user[sl] is None:
                    self.w(g, "mm", self.mix_done_mm)
                    self.w(g, "ve", self.mix_done_ve)
                else:
                    self.w(g, "mm", self.wout_slot_user[sl])
                m2 = min(m + 2, 8)
                ins = g.dma_start(
                    out=wout[sl][:].rearrange("p (q k j) -> p q k j", q=2, j=128)[
                        :, 0:m2 - m],
                    in_=self.dr["w_out_t"][m:m2].rearrange("q p k j -> p q k j"))
                self.inc(ins, osem, 16)
            self.GP(gp)
        dw_t = self.cnt[osem]
        res = {}
        def te(te, sl=sl, half=half, c0=c0, n=n, dw_t=dw_t, osem=osem,
               yz_cb_ve=yz_cb_ve):
            self.w(te, osem, dw_t)
            self.w(te, "ve", yz_cb_ve)
            P, pi = self.ps_acquire(te)
            for k in range(16):
                ins = te.matmul(P[:, 0:n],
                                wout[sl][:, half * 2048 + k * 128:half * 2048 + (k + 1) * 128],
                                cv[:, k, c0:c0 + n], start=(k == 0), stop=(k == 15))
            self.inc(ins, "mm")
            res["r"] = (pi, self.cnt["mm"])
        self.TE(te)
        if half == 1:
            self.wout_slot_user[sl] = self.cnt["mm"]
        pi, mmt = res["r"]
        if defer_evac:
            self.ps_held.add(pi)
            return (pi, mmt, m, c0, n)
        return self.emit_op_evac((pi, mmt, m, c0, n))

    def emit_lnmid_stats(self, ci, c0, n, ar_ve):
        T = self.T
        arv = T["arcb"][:].rearrange("p (c w) -> p c w", w=512)
        sq_slots = [self._view(T["sqo0"]), self._view(T["sqo1"])]
        self.ln_stats([(0, n)], NCH_D,
                      lambda ch, cc0, nn: arv[:, ch, 0:nn],
                      sq_slots, copy_slots=None,
                      first_waits=(("ve", ar_ve),),
                      rows=None, key="sqo", sq_vector=True, keep_psum=True)
        stats_done_mm = self.cnt["mm"]
        return self.ln_smalls2(T["r2o"], T["r3o"], T["rb0o"], T["rb1o"], n, EPS_LN,
                               stats_done_mm)

    def emit_lnmid_apply(self, ci, c0, n, ready):
        T = self.T
        xv = T["x_fm"][:].rearrange("p (c w) -> p c w", w=W)
        x2v = T["x2"][:].rearrange("p (c w) -> p c w", w=W)
        arv = T["arcb"][:].rearrange("p (c w) -> p c w", w=512)
        Pm, im, Pr, ir, mmt = self.ln_bcast2((T["rb0o"], T["rb1o"]), 0, n, ready,
                                             stats_banks=True)
        def ve2(v, ci=ci, c0=c0, n=n, Pm=Pm, Pr=Pr, im=im, ir=ir, mmt=mmt):
            self.w(v, "mm", mmt)
            for cc in range(0, NCH_D, 2):
                x2a = x2v[:, cc, ci * 512:ci * 512 + n]
                x2b = x2v[:, cc + 1, ci * 512:ci * 512 + n]
                t0_ = self.inc(v.tensor_sub(x2a, arv[:, cc, 0:n], Pm[:, 0:n]), "ve")
                t1_ = self.inc(v.tensor_sub(x2b, arv[:, cc + 1, 0:n], Pm[:, 0:n]), "ve")
                self.w(v, "ve", t0_)
                t0_ = self.inc(v.tensor_mul(x2a, x2a, Pr[:, 0:n]), "ve")
                self.w(v, "ve", t1_)
                t1_ = self.inc(v.tensor_mul(x2b, x2b, Pr[:, 0:n]), "ve")
                self.w(v, "ve", t0_)
                self.inc(v.tensor_add(x2a, xv[:, cc, c0:c0 + n], x2a), "ve")
                self.w(v, "ve", t1_)
                self.inc(v.tensor_add(x2b, xv[:, cc + 1, c0:c0 + n], x2b), "ve")
            self.ps_release(im, "ve")
            self.ps_release(ir, "ve")
        self.VE(ve2)
        return self.cnt["ve"]

    def emit_out_mid(self):
        T = self.T
        self.wout_slot_user = [None, None]
        self._tap_prev = 0
        dbg_yz_t = getattr(self, "_dbg_yz_t", 0) if self.debug else 0
        if dbg_yz_t:
            def vew(v):
                self.w(v, "dbg", dbg_yz_t)
            self.VE(vew)
        ar_ve = {}
        yz_ve = {}
        rp = {}
        for ci in range(5):
            pends = []
            for m in range(8):
                if ci < 4:
                    c0, n = CBS4[ci]
                    yz_ve[ci] = self.emit_a_pair(ci, c0, n, 2 * m)
                if ci >= 1:
                    pc0, pn = CBS4[ci - 1]
                    if m == 0 and ci >= 2:
                        rp[ci - 2] = self.emit_lnmid_stats(ci - 2, *CBS4[ci - 2],
                                                           ar_ve[ci - 2])
                    if m == 1 and ci >= 2:
                        self.emit_lnmid_apply(ci - 2, *CBS4[ci - 2], rp[ci - 2])
                        for pend in pends:
                            self.ps_held.discard(pend[0])
                            self.emit_op_evac(pend)
                        pends = []
                    r = self.emit_op_m(ci - 1, pc0, pn, m, yz_ve[ci - 1],
                                       defer_evac=(m == 0 and ci >= 2))
                    if m == 0 and ci >= 2:
                        pends.append(r)
                    else:
                        ar_ve[ci - 1] = r
        rp[3] = self.emit_lnmid_stats(3, CBS4[3][0], CBS4[3][1], ar_ve[3])
        self.emit_lnmid_apply(3, CBS4[3][0], CBS4[3][1], rp[3])
        self.x2_ready = self.cnt["ve"]
        if self.debug:
            def gp(g):
                self.w(g, "ve", self.x2_ready)
                self.w(g, "dbg", self.cnt["dbg"])
                ins = g.dma_start(out=self.dbg["dbg_x2"],
                                  in_=T["x2"][:].rearrange("p (c w) -> p c w", w=W)[:, :, 0:TOK])
                self.inc(ins, "dbg", 16)
            self.GP(gp)

    @staticmethod
    def _view(t, base=0):
        class V:
            def __getitem__(self, key):
                p, cc = key
                return t[p, base + (cc.start or 0):base + cc.stop]
        return V()

    def emit_ffn_head(self, b):
        for p in self.make_ffn_head(b):
            p()

    def make_ffn_head(self, b):
        T = self.T
        x2v = T["x2"][:].rearrange("p (c w) -> p c w", w=W)
        l2buf = T["ln2"] if b % 2 == 0 else T["ln2b"]
        self._l2buf = getattr(self, "_l2buf", {})
        self._l2buf[b] = l2buf
        l2v = l2buf[:].rearrange("p (c w) -> p c w", w=512)
        xmv = T["xm"][:].rearrange("p (e c w) -> p e c w", e=NE, c=NCH_D)
        rbs = (T["rb0f"], T["rb1f"])
        t0 = b * 512
        N = 512
        pieces = []
        hst = {}
        if self.debug and b == 1:
            dtot = self.cnt["dbg"]
            def vew(v):
                self.w(v, "dbg", dtot)
            self.VE(vew)
            def scw(s):
                self.w(s, "dbg", dtot)
            self.SC(scw)

        def p_stats():
            sq_slots = [self._view(T["sqf0"]), self._view(T["sqf1"])]
            self.ln_stats([(0, N)], NCH_D,
                          lambda ch, cc0, nn: x2v[:, ch, t0:t0 + nn],
                          sq_slots, copy_slots=None,
                          first_waits=(("ve", self.x2_ready),),
                          rows=None, key="sqf", sq_vector=True, keep_psum=True)
            hst["stats_mm"] = self.cnt["mm"]
        pieces.append(p_stats)
        def p_smalls():
            hst["ready"] = self.ln_smalls2(T["r2f"], T["r3f"], T["rb0f"], T["rb1f"],
                                           N, EPS_LN, hst["stats_mm"])
        pieces.append(p_smalls)
        def p_bcast_apply():
            ready = hst["ready"]
            hst["ready2"] = ready
            Pm, im, Pr, ir, mmt = self.ln_bcast2(rbs, 0, N, ready,
                                                 stats_banks=True)
            def ve_l2(v):
                self.w(v, "mm", mmt)
                for c0_ in range(0, NCH_D, 2):
                    ts = []
                    for ch in (c0_, c0_ + 1):
                        ins = v.tensor_sub(l2v[:, ch, 0:N], x2v[:, ch, t0:t0 + N],
                                           Pm[:, 0:N])
                        ts.append(self.inc(ins, "ve"))
                    for ch, t in zip((c0_, c0_ + 1), ts):
                        self.w(v, "ve", t)
                        ins = v.tensor_mul(l2v[:, ch, 0:N], l2v[:, ch, 0:N],
                                           Pr[:, 0:N])
                        self.inc(ins, "ve")
                self.ps_release(im, "ve")
                self.ps_release(ir, "ve")
            self.VE(ve_l2)
            hst["ln2_ve"] = self.cnt["ve"]
        pieces.append(p_bcast_apply)

        def p_router():
            ready = hst["ready2"]
            res = {}
            def te_r(te):
                self.w(te, "ve", self.x2_ready)
                self.w(te, "smallw", self.small_w_dw)
                P, pi = self.ps_acquire(te)
                for k in range(8):
                    ins = te.matmul(P[0:4, 0:N], T["wrt_sb"][:, 4 * k:4 * k + 4],
                                    x2v[:, k, t0:t0 + N], start=(k == 0), stop=(k == 7))
                self.inc(ins, "mm")
                P2, pi2 = self.ps_acquire(te)
                for j in range(0, len(ready), 2):
                    self.w(te, ready[j], ready[j + 1])
                self.w(te, "ve", self.cs_ve)
                ins = te.matmul(P2[0:4, 0:N], T["cs_row"][0:1, 0:4], rbs[0][0:1, 0:N],
                                start=True, stop=True)
                self.inc(ins, "mm")
                res["r"] = (pi, pi2, self.cnt["mm"])
            self.TE(te_r)
            pi_lg, pi_adj, mmt = res["r"]
            def ve_lg(v):
                self.w(v, "mm", mmt)
                ins = v.tensor_copy(T["lg_sb"][:], self.PS[pi_lg][0:4, 0:N])
                t = self.inc(ins, "ve")
                self.w(v, "ve", t)
                ins = v.tensor_sub(T["lg_sb"][:], T["lg_sb"][:],
                                   self.PS[pi_adj][0:4, 0:N])
                self.inc(ins, "ve")
                self.ps_release(pi_lg, "ve")
                self.ps_release(pi_adj, "ve")
            self.VE(ve_lg)
            hst["lg_ve"] = self.cnt["ve"]
        pieces.append(p_router)

        def p_cascade():
            lg_ve = hst["lg_ve"]
            res2 = {}
            def te_t(te):
                self.w(te, "ve", lg_ve)
                P, pi = self.ps_acquire(te)
                for g in range(4):
                    ins = te.transpose(P[:, 4 * g:4 * g + 4],
                                       T["lg_sb"][:, 128 * g:128 * (g + 1)],
                                       T["id32"][0:4, 0:4])
                self.inc(ins, "mm")
                res2["r"] = (pi, self.cnt["mm"])
            self.TE(te_t)
            pi_tm, mmt = res2["r"]
            def ve_c(v):
                def st(ins):
                    t = self.inc(ins, "ve")
                    self.w(v, "ve", t)
                self.w(v, "mm", mmt)
                st(v.tensor_copy(T["lt_tm"][:], self.PS[pi_tm][:, 0:16]))
                self.ps_release(pi_tm, "ve")
                lt3 = T["lt_tm"][:].rearrange("p (g e) -> p g e", e=4)
                sel3 = T["sel"][:].rearrange("p (g e) -> p g e", e=4)
                st(v.reduce_max(T["rmax"][:], lt3, axis=AX.X))
                st(v.memset(T["notac"][:], 1.0))
                for e in range(4):
                    st(v.tensor_tensor(T["tmp4"][:], lt3[:, :, e], T["rmax"][:],
                                       ALU.is_ge))
                    st(v.tensor_mul(sel3[:, :, e], T["tmp4"][:], T["notac"][:]))
                    if e < 3:
                        st(v.tensor_sub(T["notac"][:], T["notac"][:], sel3[:, :, e]))
            self.VE(ve_c)
            hst["sel_ve"] = self.cnt["ve"]
        pieces.append(p_cascade)

        def p_mask4():
            sel_ve = hst["sel_ve"]
            res3 = {}
            def te_m(te):
                self.w(te, "ve", sel_ve)
                P, pi = self.ps_acquire(te)
                for g in range(4):
                    ins = te.transpose(P[0:4, 128 * g:128 * (g + 1)],
                                       T["sel"][:, 4 * g:4 * g + 4], T["id32"][:])
                self.inc(ins, "mm")
                res3["r"] = (pi, self.cnt["mm"])
            self.TE(te_m)
            pi_m4, mmt = res3["r"]
            def ve_m4(v):
                self.w(v, "mm", mmt)
                ins = v.tensor_copy(T["mask4b"][:], self.PS[pi_m4][0:4, 0:N])
                self.inc(ins, "ve")
                self.ps_release(pi_m4, "ve")
            self.VE(ve_m4)
            hst["mask4_ve"] = self.cnt["ve"]
        pieces.append(p_mask4)

        def mk_mrow(e):
            def p_mrow():
                sel3 = T["sel"][:].rearrange("p (g e) -> p g e", e=4)
                sel_ve = hst["sel_ve"]
                res4 = {}
                def te_e(te, e=e):
                    self.w(te, "ve", sel_ve)
                    P, pi = self.ps_acquire(te)
                    for g in range(4):
                        ins = te.transpose(P[0:1, g * 128:(g + 1) * 128],
                                           sel3[:, g:g + 1, e], T["id32"][:])
                    self.inc(ins, "mm")
                    res4["r"] = (pi, self.cnt["mm"])
                self.TE(te_e)
                pi, mmt = res4["r"]
                def ve_e(v, e=e, pi=pi, mmt=mmt):
                    self.w(v, "mm", mmt)
                    ins = v.tensor_copy(T["mrow4"][0:1, e * 512:(e + 1) * 512],
                                        self.PS[pi][0:1, 0:N])
                    self.inc(ins, "ve")
                    self.ps_release(pi, "ve")
                self.VE(ve_e)
            return p_mrow
        for e in range(4):
            pieces.append(mk_mrow(e))
        def mk_mrep(e):
            def p_mrep():
                mrow_ve = self.cnt["ve"]
                res5 = {}
                def te_e(te, e=e, mrow_ve=mrow_ve):
                    self.w(te, "ve", mrow_ve)
                    P, pi = self.ps_acquire(te)
                    ins = te.matmul(P[:, 0:N], T["ones_row"][:],
                                    T["mrow4"][0:1, e * 512:(e + 1) * 512],
                                    start=True, stop=True)
                    self.inc(ins, "mm")
                    res5["r"] = (pi, self.cnt["mm"])
                self.TE(te_e)
                pi, mmt = res5["r"]
                def ve_e(v, e=e, pi=pi, mmt=mmt):
                    self.w(v, "mm", mmt)
                    ins = v.tensor_copy(T["mrep"][:, e * 512:(e + 1) * 512],
                                        self.PS[pi][:, 0:N])
                    self.inc(ins, "ve")
                    self.ps_release(pi, "ve")
                self.VE(ve_e)
            return p_mrep
        for e in range(4):
            pieces.append(mk_mrep(e))

        def mk_xm(half):
            def p_xm():
                mrep_ve = self.cnt["ve"]
                def ve_xm(v):
                    self.w(v, "ve", mrep_ve)
                    self.w(v, "ve", hst["ln2_ve"])
                    for e in range(2 * half, 2 * half + 2):
                        for ch in range(NCH_D):
                            ins = v.tensor_mul(xmv[:, e, ch, 0:N], l2v[:, ch, 0:N],
                                               T["mrep"][:, e * 512:(e + 1) * 512])
                            self._xm_tile_t[(b, e, ch)] = self.inc(ins, "ve")
                self.VE(ve_xm)
            return p_xm
        pieces.append(mk_xm(0))
        pieces.append(mk_xm(1))
        def p_fin():
            self._ffn_head_t[b] = (self.cnt["ve"], hst["mask4_ve"], 0)
            if self.debug and b == 0:
                def gp(g):
                    self.w(g, "ve", self.cnt["ve"])
                    self.w(g, "dbg", self.cnt["dbg"])
                    ins = g.dma_start(out=self.dbg["dbg_ln2"], in_=l2buf[:])
                    self.inc(ins, "dbg", 16)
                    ins = g.dma_start(out=self.dbg["dbg_mask"], in_=T["mask4b"][:])
                    self.inc(ins, "dbg", 16)
                self.GP(gp)
        pieces.append(p_fin)
        return pieces

    def issue_w1_dma(self, b, m, sl_override=None):
        wf = [self.T["wf0"], self.T["wf1"], self.T["wf2"]]
        if sl_override is None:
            sl = self.ffn_slot_i % 3
            self.ffn_slot_i += 1
        else:
            sl = sl_override
        fsem = f"wf{sl}"
        def gp(g, sl=sl, m=m, fsem=fsem):
            if self.ffn_slot_user[sl] is None:
                self.w(g, "ve", self.x2_ready)
            else:
                self.w(g, "mm", self.ffn_slot_user[sl])
            wv = wf[sl][:].rearrange("p (j c) -> p j c", c=128)
            ins = g.dma_start(out=wv[:, 0:17], in_=self.dr["w1b_t"][m][:, 0:17])
            self.inc(ins, fsem, 16)
            ins = g.dma_start(out=wv[:, 17:33], in_=self.dr["w1b_t"][m][:, 17:33])
            self.inc(ins, fsem, 16)
        self.GP(gp)
        return sl, fsem, self.cnt[fsem]

    def emit_ffn_expert_m(self, b, m):
        T = self.T
        xmv = T["xm"][:].rearrange("p (e c w) -> p e c w", e=NE, c=NCH_D)
        hv = T["hsel"][:].rearrange("p (c w) -> p c w", w=512)
        wf = [T["wf0"], T["wf1"], T["wf2"]]
        N = 512
        xm_ve, mask4_ve, xm_gx = self._ffn_head_t[b]
        if (b, m) in self._wf_pre:
            sl, fsem, dw_t = self._wf_pre.pop((b, m))
        else:
            sl, fsem, dw_t = self.issue_w1_dma(b, m)
        res = {}
        def te(te, sl=sl, dw_t=dw_t, fsem=fsem, m=m, b=b):
            self.w(te, fsem, dw_t)
            if m == 0:
                self.w(te, "ve", mask4_ve)
            else:
                self.w(te, "ve", max(xm_ve, mask4_ve))
            self.w(te, "gx", xm_gx)
            P, pi = self.ps_acquire(te)
            first = True
            last_w = mask4_ve
            for e in range(4):
                for k in range(8):
                    j = e * 8 + k
                    if m == 0:
                        t = self._xm_tile_t[(b, e, k)]
                        if t > last_w:
                            self.w(te, "ve", t)
                            last_w = t
                    te.matmul(P[:, 0:N], wf[sl][:, j * 128:(j + 1) * 128],
                              xmv[:, e, k, 0:N], start=first, stop=False)
                    first = False
            ins = te.matmul(P[:, 0:N], wf[sl][0:4, 32 * 128:32 * 128 + 128],
                            T["mask4b"][0:4, 0:N], start=False, stop=True)
            self.inc(ins, "mm")
            res["r"] = (pi, self.cnt["mm"])
        self.TE(te)
        self.ffn_slot_user[sl] = self.cnt["mm"]
        pi, mmt = res["r"]
        def sc(s, pi=pi, mmt=mmt, m=m):
            self.w(s, "mm", mmt)
            ins = s.activation(hv[:, m, 0:N], self.PS[pi][:, 0:N], AF.Gelu)
            self.inc(ins, "se")
            self.ps_release(pi, "se")
        self.SC(sc)

    def emit_ffn_wo(self, b, head_pieces=None):
        T = self.T
        hv = T["hsel"][:].rearrange("p (c w) -> p c w", w=512)
        yv = T["yffn"][:].rearrange("p (c w) -> p c w", w=512)
        l2v = self._l2buf[b][:].rearrange("p (c w) -> p c w", w=512)
        wf = [T["wf0"], T["wf1"], T["wf2"]]
        N = 512
        hsel_se = self.cnt["se"]
        if self.debug and b == 0:
            def gpd(g):
                self.w(g, "se", hsel_se)
                self.w(g, "dbg", self.cnt["dbg"])
                ins = g.dma_start(out=self.dbg["dbg_hsel"], in_=T["hsel"][:])
                self.inc(ins, "dbg", 16)
            self.GP(gpd)
        i0 = self.ffn_slot_i
        OFF = [0, 1, 2, 0, 1, 2, 1, 2]
        for mo in range(8):
            sl = (i0 + OFF[mo]) % 3
            fsem = f"wf{sl}"
            def gp(g, sl=sl, mo=mo, fsem=fsem):
                self.w(g, "mm", self.ffn_slot_user[sl])
                wv = wf[sl][:, 0:32 * 128].rearrange("p (j c) -> p j c", c=128)
                ins = g.dma_start(out=wv[:, 0:16], in_=self.dr["wo_t"][mo][:, 0:16])
                self.inc(ins, fsem, 16)
                ins = g.dma_start(out=wv[:, 16:32], in_=self.dr["wo_t"][mo][:, 16:32])
                self.inc(ins, fsem, 16)
            self.GP(gp)
            dw_t = self.cnt[fsem]
            res = {}
            def te(te, sl=sl, dw_t=dw_t, fsem=fsem):
                self.w(te, fsem, dw_t)
                self.w(te, "se", hsel_se)
                P, pi = self.ps_acquire(te)
                for k in range(32):
                    ins = te.matmul(P[:, 0:N], wf[sl][:, k * 128:(k + 1) * 128],
                                    hv[:, k, 0:N], start=(k == 0), stop=(k == 31))
                self.inc(ins, "mm")
                res["r"] = (pi, self.cnt["mm"])
            self.TE(te)
            self.ffn_slot_user[sl] = self.cnt["mm"]
            pi, mmt = res["r"]
            dbg_now = self.cnt["dbg"] if self.debug else 0
            def ve(v, pi=pi, mmt=mmt, mo=mo, dbg_now=dbg_now):
                self.w(v, "mm", mmt)
                if mo == 0 and dbg_now:
                    self.w(v, "dbg", dbg_now)
                ins = v.scalar_tensor_tensor(yv[:, mo, 0:N], self.PS[pi][:, 0:N],
                                             T["bo_sb"][:, mo:mo + 1], l2v[:, mo, 0:N],
                                             ALU.add, ALU.add)
                self.inc(ins, "ve")
                self.ps_release(pi, "ve")
            self.VE(ve)
            if head_pieces:
                head_pieces.pop(0)()
            if mo >= 1 and head_pieces:
                head_pieces.pop(0)()
        if b + 1 < NBLK:
            for m in range(3):
                self._wf_pre[(b + 1, m)] = self.issue_w1_dma(
                    b + 1, m, sl_override=(i0 + m) % 3)
        self.ffn_slot_i = i0 + 9
        self._ffn_y_ve[b] = self.cnt["ve"]
        if self.debug and b == 0:
            def gpd2(g):
                self.w(g, "ve", self._ffn_y_ve[b])
                self.w(g, "dbg", self.cnt["dbg"])
                ins = g.dma_start(out=self.dbg["dbg_y"], in_=T["yffn"][:])
                self.inc(ins, "dbg", 16)
            self.GP(gpd2)

    def make_ffn_tail(self, b):
        T = self.T
        yv = T["yffn"][:].rearrange("p (c w) -> p c w", w=512)
        frv = T["fret"][:].rearrange("p (c w) -> p c w", w=512)
        x2v = T["x2"][:].rearrange("p (c w) -> p c w", w=W)
        ov = T["out_tm"][:].rearrange("p (g f) -> p g f", f=1024)
        rows = (T["r0f"], T["r1f"], T["r2f"], T["r3f"])
        rbs = (T["rb0f"], T["rb1f"])
        t0 = b * 512
        N = 512
        y_ve = self._ffn_y_ve[b]
        st = {}
        pieces = []

        sqf = [T["sqf0"], T["sqf1"]]
        st["aff_sq_t"] = {}
        st["aff_sqmm"] = {}
        def mk_aff_feed(cc):
            def p(cc=cc):
                prev = self._slot_guard.get("sqf", [0, 0])
                def ve_f(v):
                    for ch in (cc, cc + 1):
                        g = prev[ch % 2] if ch < 2 else st["aff_sqmm"][ch - 2]
                        self.w(v, "mm", g)
                        if ch == 0:
                            self.w(v, "ve", y_ve)
                        ins = v.tensor_mul(sqf[ch % 2][:, 0:N], yv[:, ch, 0:N],
                                           yv[:, ch, 0:N])
                        st["aff_sq_t"][ch] = self.inc(ins, "ve")
                self.VE(ve_f)
            return p
        def mk_aff_mm(cc):
            def p(cc=cc):
                def te_a(te):
                    for ch in (cc, cc + 1):
                        self.w(te, "ve", st["aff_sq_t"][ch])
                        if ch == 0:
                            self.ps_guard_wait(te, 6)
                            self.ps_guard_wait(te, 7)
                        te.matmul(self.PS[6][0:1, 0:N], T["inv_colf"][:],
                                  yv[:, ch, 0:N],
                                  start=(ch == 0), stop=(ch == NCH_D - 1),
                                  skip_group_check=True)
                        ins = te.matmul(self.PS[7][0:1, 0:N], T["inv_col"][:],
                                        sqf[ch % 2][:, 0:N],
                                        start=(ch == 0), stop=(ch == NCH_D - 1),
                                        skip_group_check=True)
                        st["aff_sqmm"][ch] = self.inc(ins, "mm")
                self.TE(te_a)
                if cc == NCH_D - 2:
                    self._slot_guard["sqf"] = [st["aff_sqmm"][NCH_D - 2],
                                               st["aff_sqmm"][NCH_D - 1]]
                    st["aff_stats_mm"] = self.cnt["mm"]
            return p
        for cc in range(0, NCH_D, 2):
            pieces.append(mk_aff_feed(cc))
            pieces.append(mk_aff_mm(cc))

        def p_aff_smalls():
            st["aff_ready"] = self.ln_smalls2(T["r2f"], T["r3f"], T["rb0f"], T["rb1f"],
                                              N, EPS_AFF, st["aff_stats_mm"])
        pieces.append(p_aff_smalls)

        def p_aff_bcast():
            st["aff_bc"] = self.ln_bcast2(rbs, 0, N, st["aff_ready"],
                                          stats_banks=True)
        pieces.append(p_aff_bcast)

        def p_aff_apply():
            Pm2, im2, Pr2, ir2, mmt2 = st["aff_bc"]
            mul_t = [0] * NCH_D
            def ve_all(v):
                self.w(v, "mm", mmt2)
                for c0_ in range(0, NCH_D, 2):
                    ts = []
                    for ch in (c0_, c0_ + 1):
                        ins = v.tensor_sub(frv[:, ch, 0:N], yv[:, ch, 0:N],
                                           Pm2[:, 0:N])
                        ts.append(self.inc(ins, "ve"))
                    for ch, t in zip((c0_, c0_ + 1), ts):
                        self.w(v, "ve", t)
                        ins = v.tensor_mul(frv[:, ch, 0:N], frv[:, ch, 0:N],
                                           Pr2[:, 0:N])
                        mul_t[ch] = self.inc(ins, "ve")
                self.ps_release(im2, "ve")
                self.ps_release(ir2, "ve")
            self.VE(ve_all)
            def sc_all(s):
                for ch in range(NCH_D):
                    self.w(s, "ve", mul_t[ch])
                    ins = s.activation(frv[:, ch, 0:N], frv[:, ch, 0:N], AF.Identity,
                                       scale=T["gam_sb"][:, ch:ch + 1],
                                       bias=T["bet_sb"][:, ch:ch + 1])
                    self.inc(ins, "se")
            self.SC(sc_all)
            st["fr_se"] = self.cnt["se"]
        pieces.append(p_aff_apply)

        st["out_sq_t"] = {}
        st["out_sqmm"] = {}
        def mk_out_feed(cc):
            def p(cc=cc):
                prev = self._slot_guard.get("sqf", [0, 0])
                def ve_f(v):
                    for ch in (cc, cc + 1):
                        g = prev[ch % 2] if ch < 2 else st["out_sqmm"][ch - 2]
                        self.w(v, "mm", g)
                        if ch == 0:
                            self.w(v, "se", st["fr_se"])
                        ins = v.tensor_mul(sqf[ch % 2][:, 0:N], frv[:, ch, 0:N],
                                           frv[:, ch, 0:N])
                        st["out_sq_t"][ch] = self.inc(ins, "ve")
                self.VE(ve_f)
            return p
        def mk_out_mm(cc):
            def p(cc=cc):
                def te_o(te):
                    for ch in (cc, cc + 1):
                        self.w(te, "ve", st["out_sq_t"][ch])
                        if ch == 0:
                            self.ps_guard_wait(te, 6)
                            self.ps_guard_wait(te, 7)
                        te.matmul(self.PS[6][0:1, 0:N], T["inv_col"][:],
                                  frv[:, ch, 0:N],
                                  start=(ch == 0), stop=(ch == NCH_D - 1),
                                  skip_group_check=True)
                        ins = te.matmul(self.PS[7][0:1, 0:N], T["inv_col"][:],
                                        sqf[ch % 2][:, 0:N],
                                        start=(ch == 0), stop=(ch == NCH_D - 1),
                                        skip_group_check=True)
                        st["out_sqmm"][ch] = self.inc(ins, "mm")
                self.TE(te_o)
                if cc == NCH_D - 2:
                    self._slot_guard["sqf"] = [st["out_sqmm"][NCH_D - 2],
                                               st["out_sqmm"][NCH_D - 1]]
                    st["out_stats_mm"] = self.cnt["mm"]
            return p
        for cc in range(0, NCH_D, 2):
            pieces.append(mk_out_feed(cc))
            pieces.append(mk_out_mm(cc))

        def p_out_smalls():
            st["out_ready"] = self.ln_smalls2(T["r2f"], T["r3f"], T["rb0f"], T["rb1f"],
                                              N, EPS_LN, st["out_stats_mm"])
        pieces.append(p_out_smalls)

        def p_out_bcast():
            st["out_bc"] = self.ln_bcast2(rbs, 0, N, st["out_ready"],
                                          stats_banks=True)
        pieces.append(p_out_bcast)

        prev_tp = [0, 0]
        def mk_out_pair(cc):
            def p_out_pair():
                Pm3, im3, Pr3, ir3, mmt4 = st["out_bc"]
                bufs = (T["x3t"], T["aff_tap"])
                res9 = {}
                def ve(v, cc=cc):
                    if cc == 0:
                        self.w(v, "mm", mmt4)
                        if b > 0:
                            self.w(v, "dx", self.out_dx_prev)
                    if prev_tp[0]:
                        self.w(v, "mm", prev_tp[0])
                    if prev_tp[1]:
                        self.w(v, "mm", prev_tp[1])
                    ts = []
                    for j, ch in enumerate((cc, cc + 1)):
                        ins = v.tensor_sub(bufs[j][:, 0:N], frv[:, ch, 0:N],
                                           Pm3[:, 0:N])
                        ts.append(self.inc(ins, "ve"))
                    for j in range(2):
                        self.w(v, "ve", ts[j])
                        ins = v.tensor_mul(bufs[j][:, 0:N], bufs[j][:, 0:N],
                                           Pr3[:, 0:N])
                        ts[j] = self.inc(ins, "ve")
                    for j, ch in enumerate((cc, cc + 1)):
                        self.w(v, "ve", ts[j])
                        ins = v.tensor_add(bufs[j][:, 0:N], x2v[:, ch, t0:t0 + N],
                                           bufs[j][:, 0:N])
                        ts[j] = self.inc(ins, "ve")
                    res9["ve"] = tuple(ts)
                    if cc == NCH_D - 2:
                        self.ps_release(im3, "ve")
                        self.ps_release(ir3, "ve")
                self.VE(ve)
                tps = []
                for j, ch in enumerate((cc, cc + 1)):
                    res10 = {}
                    def te(te, j=j, res10=res10):
                        self.w(te, "ve", res9["ve"][j])
                        P, pi = self.ps_acquire(te)
                        for g in range(4):
                            ins = te.transpose(P[:, g * 128:(g + 1) * 128],
                                               bufs[j][:, g * 128:(g + 1) * 128],
                                               T["id32"][:])
                        self.inc(ins, "mm")
                        res10["mm"] = (pi, self.cnt["mm"])
                    self.TE(te)
                    pi, mmt5 = res10["mm"]
                    prev_tp[j] = mmt5
                    def ve2(v, ch=ch, pi=pi, mmt5=mmt5):
                        self.w(v, "mm", mmt5)
                        src = self.PS[pi][:, 0:512].rearrange("p (g f) -> p g f", g=4)
                        ins = v.tensor_copy(ov[:, :, ch * 128:(ch + 1) * 128], src)
                        self.inc(ins, "ve")
                        self.ps_release(pi, "ve")
                    self.VE(ve2)
            return p_out_pair
        for cc in range(0, NCH_D, 2):
            pieces.append(mk_out_pair(cc))

        def p_store():
            out_ve = self.cnt["ve"]
            def sy(s, b=b, out_ve=out_ve):
                self.w(s, "ve", out_ve)
                for g in range(4):
                    r0_ = b * 512 + g * 128
                    ins = s.dma_start(out=self.dr["y_out"][r0_:r0_ + 128, :],
                                      in_=self.T["out_tm"][:, g * 1024:(g + 1) * 1024])
                    self.inc(ins, "dx", 16)
            self.SY(sy)
            self.out_dx_prev = self.cnt["dx"]
        pieces.append(p_store)
        return pieces

    def emit_ffn_all(self):
        self._ffn_head_t = {}
        self._ffn_y_ve = {}
        self._wf_pre = {}
        self._xm_tile_t = {}
        for m in range(3):
            self._wf_pre[(0, m)] = self.issue_w1_dma(0, m)
        self.emit_ffn_head(0)
        tail = []
        for b in range(NBLK):
            for m in range(NCH_F):
                self.emit_ffn_expert_m(b, m)
                if tail:
                    tail.pop(0)()
            while tail:
                tail.pop(0)()
            heads = self.make_ffn_head(b + 1) if b + 1 < NBLK else []
            self.emit_ffn_wo(b, heads)
            while heads:
                heads.pop(0)()
            tail = self.make_ffn_tail(b)
        while tail:
            tail.pop(0)()

    _slot_guard = None


Kern._slot_guard = {}


def _run(debug, inputs):
    packed = pack_weights(inputs)
    x = np.asarray(inputs["L0_attn_x"], np.float32).reshape(NCORE * TOK, D)
    in_maps = []
    for c in range(NCORE):
        lo = c * TOK
        xl = np.zeros((TT, D), np.float32)
        if lo % 8192 != 0:
            xl[:HALO] = x[lo - HALO:lo]
        xl[HALO:] = x[lo:lo + TOK]
        m = {"x_loc": np.ascontiguousarray(xl)}
        m.update(packed)
        in_maps.append(m)
    kb = Kern(debug=debug)
    kb._slot_guard = {}
    nc = kb.build()
    res = run_bass_kernel_spmd(nc, in_maps, core_ids=list(range(NCORE)))
    return res


def kernel(**inputs):
    res = _run(False, inputs)
    out = np.concatenate([r["y_out"] for r in res.results], axis=0)
    return out.reshape(2, 8192, D).astype(np.float32)


# version 34 (final)
# speedup vs baseline: 1.0098x; 1.0098x over previous
import contextlib
import numpy as np
import concourse.bass as bass
import concourse.mybir as mybir
from concourse.bass_utils import run_bass_kernel_spmd

F32 = mybir.dt.float32
BF16 = mybir.dt.bfloat16
AF = mybir.ActivationFunctionType
ALU = mybir.AluOpType
AX = mybir.AxisListType

NCORE = 8
TOK = 2048
HALO = 3
TT = TOK + HALO
W = 2052
D = 1024
NCH_D = 8
NCH_I = 16
NCH_F = 32
NE = 4
EPS_LN = 1e-5
EPS_AFF = 1e-12
INVD = 1.0 / 1024.0

CBS5 = [(0, 512), (512, 512), (1024, 512), (1536, 512), (2048, 3)]
CBS4 = [(3, 512), (515, 512), (1027, 512), (1539, 512)]
NBLK = 4
SBUF_BASE = 18432
SBUF_LIMIT = 228000


def pack_weights(inp):
    W_in = np.asarray(inp["L0_attn_W_in"], np.float32)
    conv_w = np.asarray(inp["L0_attn_conv_w"], np.float32)
    conv_b = np.asarray(inp["L0_attn_conv_b"], np.float32)
    W_x = np.asarray(inp["L0_attn_W_x"], np.float32)
    W_dt = np.asarray(inp["L0_attn_W_dt"], np.float32)
    b_dt = np.asarray(inp["L0_attn_b_dt"], np.float32)
    W_out = np.asarray(inp["L0_attn_W_out"], np.float32)
    W_r = np.asarray(inp["L0_ffn_W_router"], np.float32)
    W1 = np.asarray(inp["L0_ffn_W1"], np.float32)
    b1 = np.asarray(inp["L0_ffn_b1"], np.float32)
    Wo = np.asarray(inp["L0_ffn_Wo"], np.float32)
    bo = np.asarray(inp["L0_ffn_bo"], np.float32)
    gam = np.asarray(inp["L0_ffn_ln_gamma"], np.float32)
    bet = np.asarray(inp["L0_ffn_ln_beta"], np.float32)
    C = np.ascontiguousarray
    d = {}
    d["w_in_t"] = C(W_in.reshape(32, 128, 8, 128).transpose(0, 3, 2, 1))
    dg = np.zeros((16, 4, 128, 128), np.float32)
    ii = np.arange(128)
    for c in range(16):
        for t in range(4):
            dg[c, t, ii, ii] = conv_w[c * 128:(c + 1) * 128, 0, t]
    d["diag_t"] = C(dg.transpose(2, 0, 1, 3))
    d["conv_b_p"] = C(conv_b.reshape(16, 128).T)
    d["w_x_t"] = C(W_x[:32].T.reshape(16, 128, 32).transpose(1, 0, 2))
    d["w_dt_t"] = C(W_dt.T.reshape(32, 16, 128))
    d["b_dt_p"] = C(b_dt.reshape(16, 128).T)
    d["w_out_t"] = C(W_out.reshape(8, 128, 16, 128).transpose(0, 3, 2, 1))
    d["w_r_t"] = C(W_r.reshape(8, 128, 4).transpose(1, 0, 2))
    w1t = W1.reshape(4, 8, 128, 32, 128).transpose(3, 0, 1, 2, 4).reshape(32, 32, 128, 128)
    b1pad = np.zeros((32, 1, 128, 128), np.float32)
    b1pad[:, 0, 0:4, :] = b1.reshape(4, 32, 128).transpose(1, 0, 2)
    d["w1b_t"] = C(np.concatenate([w1t, b1pad], axis=1).transpose(0, 2, 1, 3))
    d["wo_t"] = C(Wo.reshape(32, 128, 8, 128).transpose(2, 1, 0, 3))
    d["bo_p"] = C(bo.reshape(8, 128).T)
    d["gam_p"] = C(gam.reshape(8, 128).T)
    d["bet_p"] = C(bet.reshape(8, 128).T)
    d["ident"] = np.eye(128, dtype=np.float32)
    return d


class Kern:
    def __init__(self, debug=False):
        self.debug = debug
        self.nc = bass.Bass("TRN2", target_bir_lowering=False)
        self._sem = {}
        self.T = {}
        self.psg = [None] * 8
        self.ps_held = set()
        self.rot = [0, 1, 2, 3, 4, 5]
        self.rot_i = 0
        self.cnt = dict(dx=0, mm=0, ve=0, se=0)
        self.ffn_slot_i = 0
        self.ffn_slot_user = [None, None, None]

    def sb(self, name, shape, dtype, off):
        t = self.nc.alloc_sbuf_tensor_at(name, shape, dtype, offset=off)
        self.T[name] = t
        return t

    def inc(self, ins, which, k=1):
        ins.then_inc(self._sem[which], k)
        self.cnt[which] += k
        return self.cnt[which]

    def w(self, eng, which, target):
        if target:
            eng.wait_ge(self._sem[which], target)

    def ps_acquire(self, te):
        for _ in range(len(self.rot) + 1):
            i = self.rot[self.rot_i % len(self.rot)]
            self.rot_i += 1
            if i not in self.ps_held:
                break
        else:
            raise RuntimeError("all psum banks held")
        g = self.psg[i]
        if g is not None:
            self.w(te, g[0], g[1])
            self.psg[i] = None
        return self.PS[i], i

    def ps_release(self, i, which):
        self.psg[i] = (which, self.cnt[which])
        self.ps_held.discard(i)

    def ps_guard_wait(self, te, i):
        g = self.psg[i]
        if g is not None:
            self.w(te, g[0], g[1])
            self.psg[i] = None

    def TE(self, f):
        self.block.tensor(f)

    def VE(self, f):
        self.block.vector(f)

    def SC(self, f):
        self.block.scalar(f)

    def GP(self, f):
        self.block.gpsimd(f)

    def SY(self, f):
        self.block.sync(f)

    def build(self):
        nc = self.nc
        self._es = contextlib.ExitStack()
        es = self._es
        dr = {}
        dr["x_loc"] = nc.dram_tensor("x_loc", [TT, D], F32, kind="ExternalInput").ap()
        dr["w_in_t"] = nc.dram_tensor("w_in_t", [32, 128, 8, 128], F32, kind="ExternalInput").ap()
        dr["diag_t"] = nc.dram_tensor("diag_t", [128, 16, 4, 128], F32, kind="ExternalInput").ap()
        dr["conv_b_p"] = nc.dram_tensor("conv_b_p", [128, 16], F32, kind="ExternalInput").ap()
        dr["w_x_t"] = nc.dram_tensor("w_x_t", [128, 16, 32], F32, kind="ExternalInput").ap()
        dr["w_dt_t"] = nc.dram_tensor("w_dt_t", [32, 16, 128], F32, kind="ExternalInput").ap()
        dr["b_dt_p"] = nc.dram_tensor("b_dt_p", [128, 16], F32, kind="ExternalInput").ap()
        dr["w_out_t"] = nc.dram_tensor("w_out_t", [8, 128, 16, 128], F32, kind="ExternalInput").ap()
        dr["w_r_t"] = nc.dram_tensor("w_r_t", [128, 8, 4], F32, kind="ExternalInput").ap()
        dr["w1b_t"] = nc.dram_tensor("w1b_t", [32, 128, 33, 128], F32, kind="ExternalInput").ap()
        dr["wo_t"] = nc.dram_tensor("wo_t", [8, 128, 32, 128], F32, kind="ExternalInput").ap()
        dr["bo_p"] = nc.dram_tensor("bo_p", [128, 8], F32, kind="ExternalInput").ap()
        dr["gam_p"] = nc.dram_tensor("gam_p", [128, 8], F32, kind="ExternalInput").ap()
        dr["bet_p"] = nc.dram_tensor("bet_p", [128, 8], F32, kind="ExternalInput").ap()
        dr["ident"] = nc.dram_tensor("ident", [128, 128], F32, kind="ExternalInput").ap()
        dr["y_out"] = nc.dram_tensor("y_out", [TOK, D], F32, kind="ExternalOutput").ap()
        self.dr = dr
        self.dbg = {}
        if self.debug:
            for nm, shp, dt in [("dbg_ln1", [128, NCH_D, TT], F32),
                                ("dbg_xc", [128, NCH_I, TOK], F32),
                                ("dbg_x2", [128, NCH_D, TOK], F32),
                                ("dbg_ln2", [128, NCH_D * 512], F32),
                                ("dbg_mask", [4, 512], F32),
                                ("dbg_hsel", [128, NCH_F * 512], F32),
                                ("dbg_y", [128, NCH_D * 512], F32)]:
                self.dbg[nm] = nc.dram_tensor(nm, shp, dt, kind="ExternalOutput").ap()

        o = SBUF_BASE
        def take(n):
            nonlocal o
            o = (o + 31) // 32 * 32
            r = o
            o += n
            return r
        self.sb("id32", [128, 128], F32, take(512))
        self.sb("ones_row", [1, 128], BF16, take(256))
        self.sb("inv_col", [128, 1], BF16, take(32))
        self.sb("inv_colf", [128, 1], F32, take(32))
        self.sb("conv_b_sb", [128, 16], F32, take(64))
        self.sb("b_dt_sb", [128, 16], F32, take(64))
        self.sb("bo_sb", [128, 8], F32, take(32))
        self.sb("gam_sb", [128, 8], F32, take(32))
        self.sb("bet_sb", [128, 8], F32, take(32))
        self.sb("eps5_c", [1, 8], F32, take(32))
        self.sb("eps12_c", [1, 8], F32, take(32))
        self.sb("wrt_sb", [128, 32], BF16, take(64))
        self.sb("ones_col", [128, 1], BF16, take(32))
        self.sb("cs_row", [1, 4], BF16, take(32))
        self.sb("wx_sb", [128, 512], BF16, take(1024))
        off_x = take(NCH_D * W * 4)
        off_l = take(NCH_D * W * 2)
        off_c = take(NCH_I * W * 2)
        self.sb("x_fm", [128, NCH_D * W], F32, off_x)
        self.sb("ln1", [128, NCH_D * W], BF16, off_l)
        self.sb("xcyz", [128, NCH_I * W], BF16, off_c)
        SB = take(0)
        self.SB = SB
        avail = SBUF_LIMIT - SB
        self.sb("stg0", [128, D], F32, SB + 16512)
        self.sb("stg1", [128, D], F32, SB + 20608)
        self.sb("stg2", [128, D], F32, SB + 24704)
        self.sb("r0w", [1, W], F32, SB + 0)
        self.sb("r1w", [1, W], F32, SB + 8256)
        self.sb("rb0w", [1, W], BF16, SB + 8256)
        self.sb("rb1w", [1, W], BF16, SB + 8256 + 4128)
        self.sb("r2w", [1, W], F32, SB + 16512)
        self.sb("r3w", [1, W], F32, SB + 24768)
        self.sb("sqs0", [128, 1024], BF16, SB + 33024)
        self.sb("sqs1", [128, 1024], BF16, SB + 35072)
        assert SB + 37120 <= SBUF_LIMIT, (SB, avail)
        self.sb("win0", [128, 1024], BF16, SB + 38976)
        self.sb("win1", [128, 1024], BF16, SB + 41024)
        assert SB + 43072 <= SBUF_LIMIT
        self.sb("xs_tmp", [128, W], BF16, SB + 4096)
        self.sb("sz_tmp", [128, TOK], BF16, SB + 8224)
        self.sb("diag_sb", [128, 64 * 128], BF16, SB + 12320)
        self.sb("wdt_sb", [32, 2048], BF16, SB + 28704)
        self.sb("a0", [128, 512], BF16, SB + 32800)
        self.sb("a1", [128, 512], BF16, SB + 33824)
        self.sb("dt_sb", [32, 2048], BF16, SB + 34848)
        assert SB + 38944 <= SBUF_LIMIT
        self.sb("wout0", [128, 4096], BF16, SB + 0)
        self.sb("wout1", [128, 4096], BF16, SB + 8192)
        self.sb("arcb", [128, NCH_D * 512], BF16, SB + 16512)
        self.sb("r1o", [1, 512], F32, SB + 24704)
        self.sb("rb0o", [1, 512], BF16, SB + 24704)
        self.sb("rb1o", [1, 512], BF16, SB + 25728)
        self.sb("sqo0", [128, 512], BF16, SB + 26752)
        self.sb("r2o", [1, 512], F32, SB + 38976)
        self.sb("r3o", [1, 512], F32, SB + 41024)
        self.sb("sqo1", [128, 512], BF16, SB + 42048)
        self.sb("wf0", [128, 33 * 128], BF16, SB + 0)
        self.sb("wf1", [128, 33 * 128], BF16, SB + 8448)
        self.sb("wf2", [128, 33 * 128], BF16, SB + 16896)
        self.sb("r0f", [1, 512], F32, SB + 25344)
        self.sb("r1f", [1, 512], F32, SB + 27392)
        self.sb("rb0f", [1, 512], BF16, SB + 27392)
        self.sb("rb1f", [1, 512], BF16, SB + 28416)
        self.sb("r2f", [1, 512], F32, SB + 29440)
        self.sb("r3f", [1, 512], F32, SB + 31488)
        self.sb("sqf0", [128, 512], BF16, SB + 25344)
        self.sb("sqf1", [128, 512], BF16, SB + 26368)
        self.sb("ln2b", [128, NCH_D * 512], BF16, SB + 33536)
        assert SB + 41728 <= SBUF_LIMIT
        assert SB + 35584 <= SBUF_LIMIT
        self.sb("xm", [128, NE * NCH_D * 512], BF16, off_x)
        self.sb("hsel", [128, NCH_F * 512], BF16, off_x + 32768)
        self.sb("ln2", [128, NCH_D * 512], BF16, off_l)
        self.sb("mrep", [128, NE * 512], BF16, off_l + 8192)
        self.sb("out_tm", [128, 4 * 1024], F32, off_l + 12288)
        self.sb("lt_tm", [128, 16], F32, off_l + 28672)
        self.sb("rmax", [128, 4], F32, off_l + 28736)
        self.sb("sel", [128, 16], F32, off_l + 28768)
        self.sb("notac", [128, 4], F32, off_l + 28832)
        self.sb("tmp4", [128, 4], F32, off_l + 28864)
        self.sb("mask4b", [4, 512], BF16, off_l + 28896)
        self.sb("lg_sb", [4, 512], F32, off_l + 29920)
        self.sb("x2", [128, NCH_D * W], BF16, off_c)
        self.sb("yffn", [128, NCH_D * 512], F32, off_c + 32832)
        self.sb("fret", [128, NCH_D * 512], BF16, off_c + 49216)
        self.sb("x3t", [128, 512], F32, off_c + 57408)
        self.sb("mrow4", [1, NE * 512], BF16, off_c + 59456)
        self.sb("aff_tap", [128, 512], F32, off_c + 63552)

        self.PS = [es.enter_context(nc.psum_tensor(f"ps{i}", [128, 512], F32))
                   for i in range(8)]
        for s in ("dx", "mm", "ve", "se", "gx", "stgA", "stgB", "stgC", "win0", "win1",
                  "wout0", "wout1", "wf0", "wf1", "wf2", "diag", "smallw",
                  "wdt", "dbg"):
            self._sem[s] = es.enter_context(nc.semaphore(s))
            self.cnt.setdefault(s, 0)
        self.block = es.enter_context(nc.Block())
        try:
            self.emit_setup()
            self.emit_small_weights()
            self.emit_ln1()
            self.emit_mix()
            self.emit_dt_a()
            self.emit_out_mid()
            self.emit_ffn_all()
            totals = {k: self.cnt[k] for k in self._sem if self.cnt.get(k)}
            def fin(s):
                for k, v in totals.items():
                    if k in ("mm", "ve", "se"):
                        continue
                    s.wait_ge(self._sem[k], v)
            self.SY(fin)
        finally:
            self._es.close()
        return nc

    def emit_setup(self):
        T, dr = self.T, self.dr
        def sy(s):
            for dst, srcn in [("id32", "ident"), ("conv_b_sb", "conv_b_p"),
                              ("b_dt_sb", "b_dt_p"), ("bo_sb", "bo_p"),
                              ("gam_sb", "gam_p"), ("bet_sb", "bet_p")]:
                ins = s.dma_start(out=T[dst][:], in_=dr[srcn][:])
                self.inc(ins, "dx", 16)
        self.SY(sy)
        self.const_dx = self.cnt["dx"]

        def ve0(v):
            v.memset(T["ones_row"][:], 1.0)
            v.memset(T["ones_col"][:], 1.0)
            v.memset(T["eps5_c"][:], EPS_LN)
            v.memset(T["eps12_c"][:], EPS_AFF)
            v.memset(T["inv_col"][:], INVD)
            ins = v.memset(T["inv_colf"][:], INVD)
            self.inc(ins, "ve")
        self.VE(ve0)
        self.ones_ready = self.cnt["ve"]

        stg = [T["stg0"], T["stg1"], T["stg2"]]
        self.xtile_ve = {}
        xv = T["x_fm"][:].rearrange("p (c w) -> p c w", w=W)
        tile_done_mm = [0] * 17
        for t in range(17):
            rows = 128 if t < 16 else 3
            stg_sem = ("stgA", "stgB", "stgC")[t % 3]
            def sy2(s, t=t, rows=rows, stg_sem=stg_sem):
                if t >= 3:
                    self.w(s, "mm", tile_done_mm[t - 3])
                ins = s.dma_start(out=stg[t % 3][0:rows, :],
                                  in_=dr["x_loc"][t * 128:t * 128 + rows, :])
                self.inc(ins, stg_sem, 16)
            self.SY(sy2)
            dx_t = (stg_sem, self.cnt[stg_sem])
            evi = []
            def te(te, t=t, rows=rows, dx_t=dx_t):
                if t == 0:
                    self.w(te, "dx", self.const_dx)
                self.w(te, dx_t[0], dx_t[1])
                for half in range(2):
                    P, pi = self.ps_acquire(te)
                    for j in range(4):
                        ch = half * 4 + j
                        ins = te.transpose(P[:, j * rows:(j + 1) * rows],
                                           stg[t % 3][0:rows, ch * 128:(ch + 1) * 128],
                                           T["id32"][0:rows, 0:rows])
                    self.inc(ins, "mm")
                    evi.append((pi, self.cnt["mm"], half))
            self.TE(te)
            tile_done_mm[t] = self.cnt["mm"]
            def ve(v, t=t, rows=rows, evi=evi):
                for pi, mmt, half in evi:
                    self.w(v, "mm", mmt)
                    src = self.PS[pi][:, 0:4 * rows].rearrange("p (c r) -> p c r", c=4)
                    dst = xv[:, half * 4:half * 4 + 4, t * 128:t * 128 + rows]
                    ins = v.tensor_copy(dst, src)
                    self.inc(ins, "ve")
                    self.ps_release(pi, "ve")
            self.VE(ve)
            self.xtile_ve[t] = self.cnt["ve"]
        self.xfm_ready = self.cnt["ve"]

    def emit_small_weights(self):
        T, dr = self.T, self.dr
        def gp(g):
            ins = g.dma_start(out=T["wrt_sb"][:].rearrange("p (k j) -> p k j", j=4),
                              in_=dr["w_r_t"][:])
            self.inc(ins, "smallw", 16)
            ins = g.dma_start(out=T["wx_sb"][:].rearrange("p (k f) -> p k f", f=32),
                              in_=dr["w_x_t"][:])
            self.inc(ins, "smallw", 16)
        self.GP(gp)
        self.small_w_dw = self.cnt["smallw"]
        res = {}
        def te_cs(te):
            self.w(te, "smallw", self.small_w_dw)
            self.w(te, "ve", self.ones_ready)
            P, pi = self.ps_acquire(te)
            for k in range(8):
                ins = te.matmul(P[0:4, 0:1], T["wrt_sb"][:, 4 * k:4 * k + 4],
                                T["ones_col"][:], start=(k == 0), stop=(k == 7))
            self.inc(ins, "mm")
            res["a"] = (pi, self.cnt["mm"])
        self.TE(te_cs)
        pi_a, mm_a = res["a"]
        def ve_cs(v):
            self.w(v, "mm", mm_a)
            ins = v.tensor_copy(T["lg_sb"][0:4, 0:1], self.PS[pi_a][0:4, 0:1])
            self.inc(ins, "ve")
            self.ps_release(pi_a, "ve")
        self.VE(ve_cs)
        cs_ve = self.cnt["ve"]
        def te_cs2(te):
            self.w(te, "ve", cs_ve)
            P, pi = self.ps_acquire(te)
            ins = te.transpose(P[0:1, 0:4], T["lg_sb"][0:4, 0:1], T["id32"][0:4, 0:4])
            self.inc(ins, "mm")
            res["b"] = (pi, self.cnt["mm"])
        self.TE(te_cs2)
        pi_b, mm_b = res["b"]
        def ve_cs2(v):
            self.w(v, "mm", mm_b)
            ins = v.tensor_copy(T["cs_row"][0:1, 0:4], self.PS[pi_b][0:1, 0:4])
            self.inc(ins, "ve")
            self.ps_release(pi_b, "ve")
        self.VE(ve_cs2)
        self.cs_mm = self.cnt["mm"]
        self.cs_ve = self.cnt["ve"]

    def ln_stats(self, cbs, nch, src_ap, sq_slots, copy_slots=None,
                 first_waits=(), rows=None, key=None, sq_vector=False,
                 keep_psum=False, cb_waits=None):
        r0, r1 = rows if rows is not None else (None, None)
        guards = self._slot_guard
        key_sq = key or tuple(id(s) for s in sq_slots)
        prev = guards.get(key_sq, [0, 0])
        for ci, (c0, n) in enumerate(cbs):
            mm_base = self.cnt["mm"]
            pred = [mm_base + ch + 1 for ch in range(nch)]
            ve_t = [0] * nch
            se_t = [0] * nch
            if copy_slots is not None:
                def ve(v, ci=ci, c0=c0, n=n, prev=list(prev), pred=pred):
                    for ch in range(nch):
                        sl = ch % 2
                        g = prev[sl] if ch < 2 else pred[ch - 2]
                        self.w(v, "mm", g)
                        if ci == 0 and ch == 0:
                            for wk, wt in first_waits:
                                self.w(v, wk, wt)
                        if ch == 0 and cb_waits is not None:
                            for wk, wt in cb_waits(ci):
                                self.w(v, wk, wt)
                        ins = v.tensor_copy(copy_slots[sl][:, 0:n], src_ap(ch, c0, n))
                        ve_t[ch] = self.inc(ins, "ve")
                self.VE(ve)
            if sq_vector:
                def vsq(v, ci=ci, c0=c0, n=n, prev=list(prev), pred=pred):
                    for ch in range(nch):
                        sl = ch % 2
                        g = prev[sl] if ch < 2 else pred[ch - 2]
                        self.w(v, "mm", g)
                        if ci == 0 and ch == 0:
                            for wk, wt in first_waits:
                                self.w(v, wk, wt)
                        sap = src_ap(ch, c0, n)
                        ins = v.tensor_mul(sq_slots[sl][:, 0:n], sap, sap)
                        se_t[ch] = 0
                        ve_t[ch] = self.inc(ins, "ve")
                self.VE(vsq)
            else:
                def sc(s, ci=ci, c0=c0, n=n, prev=list(prev), pred=pred):
                    for ch in range(nch):
                        sl = ch % 2
                        g = prev[sl] if ch < 2 else pred[ch - 2]
                        self.w(s, "mm", g)
                        if ci == 0 and ch == 0:
                            for wk, wt in first_waits:
                                self.w(s, wk, wt)
                        if ch == 0 and cb_waits is not None:
                            for wk, wt in cb_waits(ci):
                                self.w(s, wk, wt)
                        ins = s.activation(sq_slots[sl][:, 0:n], src_ap(ch, c0, n),
                                           AF.Square)
                        se_t[ch] = self.inc(ins, "se")
                self.SC(sc)
            def te(te, ci=ci, c0=c0, n=n, pred=pred):
                for ch in range(nch):
                    sl = ch % 2
                    if se_t[ch]:
                        self.w(te, "se", se_t[ch])
                    if ve_t[ch]:
                        self.w(te, "ve", ve_t[ch])
                    if ch == 0:
                        self.ps_guard_wait(te, 6)
                        self.ps_guard_wait(te, 7)
                    mean_rhs = (copy_slots[ch % 2][:, 0:n] if copy_slots is not None
                                else src_ap(ch, c0, n))
                    te.matmul(self.PS[6][0:1, 0:n], self.T["inv_col"][:], mean_rhs,
                              start=(ch == 0), stop=(ch == nch - 1), skip_group_check=True)
                    ins = te.matmul(self.PS[7][0:1, 0:n], self.T["inv_col"][:],
                                    sq_slots[sl][:, 0:n],
                                    start=(ch == 0), stop=(ch == nch - 1),
                                    skip_group_check=True)
                    self.inc(ins, "mm")
                assert self.cnt["mm"] == pred[-1], (self.cnt["mm"], pred[-1])
            self.TE(te)
            prev = [pred[nch - 2] if nch >= 2 else pred[-1], pred[nch - 1]]
            if keep_psum:
                self.psg[6] = ("mm", self.cnt["mm"])
                self.psg[7] = ("mm", self.cnt["mm"])
            else:
                mmt = self.cnt["mm"]
                def ve2(v, c0=c0, n=n, mmt=mmt):
                    self.w(v, "mm", mmt)
                    v.tensor_copy(r0[0:1, c0:c0 + n], self.PS[6][0:1, 0:n])
                    ins = v.tensor_copy(r1[0:1, c0:c0 + n], self.PS[7][0:1, 0:n])
                    self.inc(ins, "ve")
                    self.ps_release(6, "ve")
                    self.ps_release(7, "ve")
                self.VE(ve2)
        guards[key_sq] = prev

    def ln_smalls2(self, r2, r3, rb0, rb1, width, eps, stats_mm):
        N = width
        res = {}
        entry_ve = self.cnt["ve"]
        entry_se = self.cnt["se"]
        def sc1(s):
            self.w(s, "ve", entry_ve)
            self.w(s, "se", entry_se)
            self.w(s, "mm", stats_mm)
            ins = s.activation(r2[0:1, 0:N], self.PS[6][0:1, 0:N], AF.Square)
            res["s1"] = self.inc(ins, "se")
        self.SC(sc1)
        def ve(v):
            self.w(v, "se", res["s1"])
            ins = v.scalar_tensor_tensor(r3[0:1, 0:N], self.PS[7][0:1, 0:N], float(eps),
                                         r2[0:1, 0:N], ALU.add, ALU.subtract)
            res["v_stt"] = self.inc(ins, "ve")
            ins = v.tensor_copy(rb0[0:1, 0:N], self.PS[6][0:1, 0:N])
            res["v1"] = self.inc(ins, "ve")
            self.ps_release(6, "ve")
            self.ps_release(7, "ve")
        self.VE(ve)
        def sc2(s):
            self.w(s, "ve", res["v_stt"])
            ins = s.activation(r2[0:1, 0:N], r3[0:1, 0:N], AF.Ln)
            t = self.inc(ins, "se")
            self.w(s, "se", t)
            ins = s.activation(rb1[0:1, 0:N], r2[0:1, 0:N], AF.Exp, scale=-0.5)
            self.inc(ins, "se")
        self.SC(sc2)
        return ("se", self.cnt["se"], "ve", res["v1"])

    def ln_bcast2(self, rbs, c0, n, ready):
        rb0, rb1 = rbs
        res = {}
        def te(te):
            Pm, im = self.ps_acquire(te)
            Pr, ir = self.ps_acquire(te)
            self.ps_held.add(im)
            self.ps_held.add(ir)
            for j in range(0, len(ready), 2):
                self.w(te, ready[j], ready[j + 1])
            te.matmul(Pm[:, 0:n], self.T["ones_row"][:], rb0[0:1, c0:c0 + n],
                      start=True, stop=True)
            ins = te.matmul(Pr[:, 0:n], self.T["ones_row"][:], rb1[0:1, c0:c0 + n],
                            start=True, stop=True)
            self.inc(ins, "mm")
            res["r"] = (Pm, im, Pr, ir, self.cnt["mm"])
        self.TE(te)
        return res["r"]

    def ln_smalls(self, rows, rbs, width, eps):
        r0, r1, r2, r3 = rows
        rb0, rb1 = rbs
        N = width
        res = {}
        entry_ve = self.cnt["ve"]
        entry_se = self.cnt["se"]
        def sc1(s):
            self.w(s, "ve", entry_ve)
            self.w(s, "se", entry_se)
            ins = s.activation(r2[0:1, 0:N], r0[0:1, 0:N], AF.Square)
            res["s1"] = self.inc(ins, "se")
        self.SC(sc1)
        def ve(v):
            self.w(v, "se", res["s1"])
            ins = v.scalar_tensor_tensor(r3[0:1, 0:N], r1[0:1, 0:N], float(eps),
                                         r2[0:1, 0:N], ALU.add, ALU.subtract)
            res["v_stt"] = self.inc(ins, "ve")
            self.w(v, "ve", res["v_stt"])
            ins = v.tensor_copy(rb0[0:1, 0:N], r0[0:1, 0:N])
            res["v1"] = self.inc(ins, "ve")
        self.VE(ve)
        def sc2(s):
            self.w(s, "ve", res["v_stt"])
            ins = s.activation(r2[0:1, 0:N], r3[0:1, 0:N], AF.Ln)
            t = self.inc(ins, "se")
            self.w(s, "se", t)
            ins = s.activation(rb1[0:1, 0:N], r2[0:1, 0:N], AF.Exp, scale=-0.5)
            self.inc(ins, "se")
        self.SC(sc2)
        return ("se", self.cnt["se"], "ve", res["v1"])

    def ln_bcast(self, rbs, c0, n, ready_ve):
        rb0, rb1 = rbs
        res = {}
        def te(te):
            Pm, im = self.ps_acquire(te)
            Pr, ir = self.ps_acquire(te)
            self.ps_held.add(im)
            self.ps_held.add(ir)
            self.w(te, "ve", ready_ve)
            te.matmul(Pm[:, 0:n], self.T["ones_row"][:], rb0[0:1, c0:c0 + n],
                      start=True, stop=True)
            ins = te.matmul(Pr[:, 0:n], self.T["ones_row"][:], rb1[0:1, c0:c0 + n],
                            start=True, stop=True)
            self.inc(ins, "mm")
            res["r"] = (Pm, im, Pr, ir, self.cnt["mm"])
        self.TE(te)
        return res["r"]

    def emit_ln1(self):
        T = self.T
        xv = T["x_fm"][:].rearrange("p (c w) -> p c w", w=W)
        lv = T["ln1"][:].rearrange("p (c w) -> p c w", w=W)
        sq_slots = [self._view(T["sqs0"], 512), self._view(T["sqs1"], 512)]
        cp_slots = [self._view(T["sqs0"], 0), self._view(T["sqs1"], 0)]
        def cbw(ci):
            c0, n = CBS5[ci]
            last_tile = (c0 + n - 1) // 128
            return (("ve", self.xtile_ve[last_tile]),)
        self.ln_stats(CBS5, NCH_D,
                      lambda ch, c0, n: xv[:, ch, c0:c0 + n],
                      sq_slots, copy_slots=cp_slots,
                      first_waits=(("ve", self.ones_ready),),
                      rows=(T["r0w"], T["r1w"]), key="sqs", cb_waits=cbw)
        ready = self.ln_smalls((T["r0w"], T["r1w"], T["r2w"], T["r3w"]),
                               (T["rb0w"], T["rb1w"]), TT, EPS_LN)
        self.ln1_cb_ve = {}
        for (c0, n) in CBS5:
            Pm, im, Pr, ir, mmt = self.ln_bcast2((T["rb0w"], T["rb1w"]), c0, n, ready)
            def ve(v, c0=c0, n=n, Pm=Pm, Pr=Pr, im=im, ir=ir, mmt=mmt):
                self.w(v, "mm", mmt)
                for cc in range(0, NCH_D, 2):
                    ts = []
                    for ch in (cc, cc + 1):
                        ins = v.tensor_sub(lv[:, ch, c0:c0 + n], xv[:, ch, c0:c0 + n],
                                           Pm[:, 0:n])
                        ts.append(self.inc(ins, "ve"))
                    for ch, t in zip((cc, cc + 1), ts):
                        self.w(v, "ve", t)
                        ins = v.tensor_mul(lv[:, ch, c0:c0 + n], lv[:, ch, c0:c0 + n],
                                           Pr[:, 0:n])
                        self.inc(ins, "ve")
                self.ps_release(im, "ve")
                self.ps_release(ir, "ve")
            self.VE(ve)
            self.ln1_cb_ve[c0] = self.cnt["ve"]
        self.ln1_ready = self.cnt["ve"]
        if self.debug:
            def gp(g):
                self.w(g, "ve", self.ln1_ready)
                self.w(g, "dbg", self.cnt["dbg"])
                ins = g.dma_start(out=self.dbg["dbg_ln1"],
                                  in_=T["ln1"][:].rearrange("p (c w) -> p c w", w=W)[:, :, 0:TT])
                self.inc(ins, "dbg", 16)
            self.GP(gp)

    def emit_mix(self):
        T = self.T
        lv = T["ln1"][:].rearrange("p (c w) -> p c w", w=W)
        cv = T["xcyz"][:].rearrange("p (c w) -> p c w", w=W)
        win = [T["win0"], T["win1"]]
        slot_user = [None, None]
        self.xc_se = {}
        self.pz_ve = {}
        rot_save = self.rot
        self.rot = [0, 1, 2]
        xp_started = False
        prev_conv_mm = 0
        dma_i = 0

        for ch in range(16):
            dws = {}
            for mi, m in enumerate((ch, 16 + ch)):
                sl = dma_i % 2
                ssem = "win0" if sl == 0 else "win1"
                def gp(g, m=m, sl=sl, ssem=ssem):
                    if slot_user[sl] is not None:
                        self.w(g, "mm", slot_user[sl])
                    ins = g.dma_start(out=win[sl][:].rearrange("p (k j) -> p k j", j=128),
                                      in_=self.dr["w_in_t"][m])
                    self.inc(ins, ssem, 16)
                self.GP(gp)
                dws[m] = (sl, ssem, self.cnt[ssem])
                dma_i += 1
            if ch == 0:
                def gp_d(g):
                    self.w(g, "ve", self.ln1_ready)
                    ins = g.dma_start(out=T["diag_sb"][:].rearrange(
                        "p (c t j) -> p c t j", c=16, t=4),
                                      in_=self.dr["diag_t"][:])
                    self.inc(ins, "diag", 16)
                self.GP(gp_d)
                self.diag_dw = self.cnt["diag"]

            sl_xs, ssem_xs, dw_xs = dws[ch]
            evacs = []
            V0 = self.cnt["ve"]
            def te_xs(te, sl_xs=sl_xs, ssem_xs=ssem_xs, dw_xs=dw_xs, V0=V0):
                self.w(te, ssem_xs, dw_xs)
                for j, (c0, n) in enumerate(CBS5):
                    self.w(te, "ve", self.ln1_cb_ve[c0])
                    P, pi = self.ps_acquire(te)
                    for k in range(8):
                        ins = te.matmul(P[:, 0:n], win[sl_xs][:, k * 128:(k + 1) * 128],
                                        lv[:, k, c0:c0 + n], start=(k == 0), stop=(k == 7))
                    self.inc(ins, "mm")
                    evacs.append((pi, self.cnt["mm"], c0, n))
                    self.psg[pi] = ("ve", V0 + j + 1)
            self.TE(te_xs)
            slot_user[sl_xs] = self.cnt["mm"]
            def ve_xs(v, evacs=evacs, prev_conv_mm=prev_conv_mm, V0=V0):
                self.w(v, "mm", prev_conv_mm)
                for pi, mmt, c0, n in evacs:
                    self.w(v, "mm", mmt)
                    ins = v.tensor_copy(T["xs_tmp"][:, c0:c0 + n], self.PS[pi][:, 0:n])
                    self.inc(ins, "ve")
                assert self.cnt["ve"] == V0 + len(evacs)
            self.VE(ve_xs)
            xs_ve = self.cnt["ve"]

            cevacs = []
            S0 = self.cnt["se"]
            def te_cv(te, ch=ch, xs_ve=xs_ve, S0=S0):
                self.w(te, "ve", xs_ve)
                self.w(te, "diag", self.diag_dw)
                for j, (c0, n) in enumerate(CBS4):
                    P, pi = self.ps_acquire(te)
                    for tap in range(4):
                        dcol = (ch * 4 + tap) * 128
                        ins = te.matmul(P[:, 0:n], T["diag_sb"][:, dcol:dcol + 128],
                                        T["xs_tmp"][:, c0 - 3 + tap:c0 - 3 + tap + n],
                                        start=(tap == 0), stop=(tap == 3))
                    self.inc(ins, "mm")
                    cevacs.append((pi, self.cnt["mm"], c0, n))
                    self.psg[pi] = ("se", S0 + j + 1)
            self.TE(te_cv)
            prev_conv_mm = self.cnt["mm"]
            self.conv_last_mm = prev_conv_mm
            def sc_cv(s, ch=ch, cevacs=cevacs, S0=S0):
                for pi, mmt, c0, n in cevacs:
                    self.w(s, "mm", mmt)
                    ins = s.activation(cv[:, ch, c0:c0 + n], self.PS[pi][:, 0:n], AF.Silu,
                                       bias=T["conv_b_sb"][:, ch:ch + 1])
                    self.inc(ins, "se")
                assert self.cnt["se"] == S0 + len(cevacs)
            self.SC(sc_cv)
            xc_se = self.cnt["se"]
            self.xc_se[ch] = xc_se

            def te_xp(te, ch=ch, xc_se=xc_se, xp_started=xp_started):
                self.w(te, "se", xc_se)
                for i, (c0, n) in enumerate(CBS4):
                    pi = 3 + i
                    if not xp_started:
                        self.ps_guard_wait(te, pi)
                    wcol = ch * 32
                    ins = te.matmul(self.PS[pi][0:32, 0:n], T["wx_sb"][:, wcol:wcol + 32],
                                    cv[:, ch, c0:c0 + n], start=(ch == 0), stop=(ch == 15),
                                    skip_group_check=True)
                    if ch == 15:
                        self.inc(ins, "mm")
            if ch == 0:
                def te_xp0(te, f=te_xp):
                    self.w(te, "smallw", self.small_w_dw)
                    f(te)
                self.TE(te_xp0)
            else:
                self.TE(te_xp)
            xp_started = True
            if ch == 15:
                self.xp_done_mm = self.cnt["mm"]

            sl_z, ssem_z, dw_z = dws[16 + ch]
            zevacs = []
            S0z = self.cnt["se"]
            def te_z(te, sl_z=sl_z, ssem_z=ssem_z, dw_z=dw_z, S0z=S0z):
                self.w(te, ssem_z, dw_z)
                for j, (c0, n) in enumerate(CBS4):
                    self.w(te, "ve", self.ln1_cb_ve[c0 - 3 + 512])
                    P, pi = self.ps_acquire(te)
                    for k in range(8):
                        ins = te.matmul(P[:, 0:n], win[sl_z][:, k * 128:(k + 1) * 128],
                                        lv[:, k, c0:c0 + n], start=(k == 0), stop=(k == 7))
                    self.inc(ins, "mm")
                    zevacs.append((pi, self.cnt["mm"], c0, n))
                    self.psg[pi] = ("se", S0z + j + 1)
            self.TE(te_z)
            slot_user[sl_z] = self.cnt["mm"]
            prev_pz = self.pz_ve.get(ch - 1, 0)
            def sc_z(s, zevacs=zevacs, prev_pz=prev_pz, S0z=S0z):
                self.w(s, "ve", prev_pz)
                for pi, mmt, c0, n in zevacs:
                    self.w(s, "mm", mmt)
                    ins = s.activation(T["sz_tmp"][:, c0 - 3:c0 - 3 + n],
                                       self.PS[pi][:, 0:n], AF.Silu)
                    self.inc(ins, "se")
                assert self.cnt["se"] == S0z + len(zevacs)
            self.SC(sc_z)
            sz_se = self.cnt["se"]
            def ve_pz(v, ch=ch, sz_se=sz_se):
                self.w(v, "se", sz_se)
                for (c0, n) in CBS4:
                    ins = v.tensor_mul(cv[:, ch, c0:c0 + n], cv[:, ch, c0:c0 + n],
                                       T["sz_tmp"][:, c0 - 3:c0 - 3 + n])
                    self.inc(ins, "ve")
            self.VE(ve_pz)
            self.pz_ve[ch] = self.cnt["ve"]
        self.rot = rot_save
        self.mix_done_mm = self.cnt["mm"]
        self.mix_done_ve = self.cnt["ve"]
        self._dbg_xc_t = 0
        if self.debug:
            def gp(g):
                self.w(g, "ve", self.pz_ve[15])
                self.w(g, "dbg", self.cnt["dbg"])
                ins = g.dma_start(out=self.dbg["dbg_xc"],
                                  in_=T["xcyz"][:].rearrange("p (c w) -> p c w", w=W)[:, :, HALO:TT])
                self._dbg_xc_t = self.inc(ins, "dbg", 16)
            self.GP(gp)

    def emit_dt_a(self):
        T = self.T
        cv = T["xcyz"][:].rearrange("p (c w) -> p c w", w=W)
        def gp(g):
            self.w(g, "mm", self.conv_last_mm)
            ins = g.dma_start(out=T["wdt_sb"][:].rearrange("p (m j) -> p m j", j=128),
                              in_=self.dr["w_dt_t"][:])
            self.inc(ins, "wdt", 16)
        self.GP(gp)
        self.wdt_dw = self.cnt["wdt"]

        def ve_dt(v):
            self.w(v, "mm", self.xp_done_mm)
            for i in range(4):
                ins = v.tensor_copy(T["dt_sb"][0:32, i * 512:(i + 1) * 512],
                                    self.PS[3 + i][0:32, 0:512])
                self.inc(ins, "ve")
                self.ps_release(3 + i, "ve")
        self.VE(ve_dt)
        self.dt_ve = self.cnt["ve"]
        self._a_slot_ve = [0, 0]
        self._a_si = 0

    def emit_a_pair(self, i, c0, n, ch0):
        T = self.T
        cv = T["xcyz"][:].rearrange("p (c w) -> p c w", w=W)
        at = [T["a0"], T["a1"]]
        info = []
        for ch in (ch0, ch0 + 1):
            sl = self._a_si % 2
            self._a_si += 1
            res = {}
            def te(te, ch=ch, i=i, n=n, res=res):
                self.w(te, "ve", self.dt_ve)
                self.w(te, "wdt", self.wdt_dw)
                P, pi = self.ps_acquire(te)
                ins = te.matmul(P[:, 0:n], T["wdt_sb"][:, ch * 128:(ch + 1) * 128],
                                T["dt_sb"][0:32, i * 512:i * 512 + n],
                                start=True, stop=True)
                self.inc(ins, "mm")
                res["r"] = (pi, self.cnt["mm"])
            self.TE(te)
            info.append((ch, sl, res["r"]))
        ln_t = {}
        def sc(s, info=info, n=n):
            pes = []
            for ch, sl, (pi, mmt) in info:
                self.w(s, "mm", mmt)
                self.w(s, "ve", self._a_slot_ve[sl])
                ins = s.activation(self.PS[pi][:, 0:n], self.PS[pi][:, 0:n], AF.Exp,
                                   bias=T["b_dt_sb"][:, ch:ch + 1])
                t = self.inc(ins, "se")
                pes.append((pi, t))
            for (ch, sl, (pi, mmt)), (_, t) in zip(info, pes):
                self.w(s, "se", t)
                ins = s.activation(at[sl][:, 0:n], self.PS[pi][:, 0:n], AF.Ln, bias=1.0)
                ln_t[sl] = self.inc(ins, "se")
                self.ps_release(pi, "se")
        self.SC(sc)
        def ve(v, info=info, c0=c0, n=n):
            if self.debug and self._dbg_xc_t:
                self.w(v, "dbg", self._dbg_xc_t)
            for ch, sl, _ in info:
                self.w(v, "se", ln_t[sl])
                ins = v.tensor_mul(cv[:, ch, c0:c0 + n], cv[:, ch, c0:c0 + n],
                                   at[sl][:, 0:n])
                self._a_slot_ve[sl] = self.inc(ins, "ve")
        self.VE(ve)
        return self.cnt["ve"]

    def emit_op_evac(self, info):
        T = self.T
        lv = T["ln1"][:].rearrange("p (c w) -> p c w", w=W)
        arv = T["arcb"][:].rearrange("p (c w) -> p c w", w=512)
        pi, mmt, m, c0, n = info
        def ve(v):
            self.w(v, "mm", mmt)
            ins = v.tensor_add(arv[:, m, 0:n], self.PS[pi][:, 0:n],
                               lv[:, m, c0:c0 + n])
            self.inc(ins, "ve")
            self.ps_release(pi, "ve")
        self.VE(ve)
        return self.cnt["ve"]

    def emit_op_m(self, ci, c0, n, m, yz_cb_ve, defer_evac=False):
        T = self.T
        cv = T["xcyz"][:].rearrange("p (c w) -> p c w", w=W)
        lv = T["ln1"][:].rearrange("p (c w) -> p c w", w=W)
        arv = T["arcb"][:].rearrange("p (c w) -> p c w", w=512)
        wout = [T["wout0"], T["wout1"]]
        pair = (ci * 8 + m) // 2
        sl = pair % 2
        half = m % 2
        osem = "wout0" if sl == 0 else "wout1"
        if half == 0:
            def gp(g, sl=sl, m=m, osem=osem):
                if self.wout_slot_user[sl] is None:
                    self.w(g, "mm", self.mix_done_mm)
                    self.w(g, "ve", self.mix_done_ve)
                else:
                    self.w(g, "mm", self.wout_slot_user[sl])
                m2 = min(m + 2, 8)
                ins = g.dma_start(
                    out=wout[sl][:].rearrange("p (q k j) -> p q k j", q=2, j=128)[
                        :, 0:m2 - m],
                    in_=self.dr["w_out_t"][m:m2].rearrange("q p k j -> p q k j"))
                self.inc(ins, osem, 16)
            self.GP(gp)
        dw_t = self.cnt[osem]
        res = {}
        def te(te, sl=sl, half=half, c0=c0, n=n, dw_t=dw_t, osem=osem,
               yz_cb_ve=yz_cb_ve):
            self.w(te, osem, dw_t)
            self.w(te, "ve", yz_cb_ve)
            P, pi = self.ps_acquire(te)
            for k in range(16):
                ins = te.matmul(P[:, 0:n],
                                wout[sl][:, half * 2048 + k * 128:half * 2048 + (k + 1) * 128],
                                cv[:, k, c0:c0 + n], start=(k == 0), stop=(k == 15))
            self.inc(ins, "mm")
            res["r"] = (pi, self.cnt["mm"])
        self.TE(te)
        if half == 1:
            self.wout_slot_user[sl] = self.cnt["mm"]
        pi, mmt = res["r"]
        if defer_evac:
            self.ps_held.add(pi)
            return (pi, mmt, m, c0, n)
        return self.emit_op_evac((pi, mmt, m, c0, n))

    def emit_lnmid_stats(self, ci, c0, n, ar_ve):
        T = self.T
        arv = T["arcb"][:].rearrange("p (c w) -> p c w", w=512)
        sq_slots = [self._view(T["sqo0"]), self._view(T["sqo1"])]
        self.ln_stats([(0, n)], NCH_D,
                      lambda ch, cc0, nn: arv[:, ch, 0:nn],
                      sq_slots, copy_slots=None,
                      first_waits=(("ve", ar_ve),),
                      rows=None, key="sqo", sq_vector=True, keep_psum=True)
        stats_done_mm = self.cnt["mm"]
        return self.ln_smalls2(T["r2o"], T["r3o"], T["rb0o"], T["rb1o"], n, EPS_LN,
                               stats_done_mm)

    def emit_lnmid_apply(self, ci, c0, n, ready):
        T = self.T
        xv = T["x_fm"][:].rearrange("p (c w) -> p c w", w=W)
        x2v = T["x2"][:].rearrange("p (c w) -> p c w", w=W)
        arv = T["arcb"][:].rearrange("p (c w) -> p c w", w=512)
        Pm, im, Pr, ir, mmt = self.ln_bcast2((T["rb0o"], T["rb1o"]), 0, n, ready)
        def ve2(v, ci=ci, c0=c0, n=n, Pm=Pm, Pr=Pr, im=im, ir=ir, mmt=mmt):
            self.w(v, "mm", mmt)
            for cc in range(0, NCH_D, 2):
                x2a = x2v[:, cc, ci * 512:ci * 512 + n]
                x2b = x2v[:, cc + 1, ci * 512:ci * 512 + n]
                t0_ = self.inc(v.tensor_sub(x2a, arv[:, cc, 0:n], Pm[:, 0:n]), "ve")
                t1_ = self.inc(v.tensor_sub(x2b, arv[:, cc + 1, 0:n], Pm[:, 0:n]), "ve")
                self.w(v, "ve", t0_)
                t0_ = self.inc(v.tensor_mul(x2a, x2a, Pr[:, 0:n]), "ve")
                self.w(v, "ve", t1_)
                t1_ = self.inc(v.tensor_mul(x2b, x2b, Pr[:, 0:n]), "ve")
                self.w(v, "ve", t0_)
                self.inc(v.tensor_add(x2a, xv[:, cc, c0:c0 + n], x2a), "ve")
                self.w(v, "ve", t1_)
                self.inc(v.tensor_add(x2b, xv[:, cc + 1, c0:c0 + n], x2b), "ve")
            self.ps_release(im, "ve")
            self.ps_release(ir, "ve")
        self.VE(ve2)
        return self.cnt["ve"]

    def emit_out_mid(self):
        T = self.T
        self.wout_slot_user = [None, None]
        self._tap_prev = 0
        dbg_yz_t = getattr(self, "_dbg_yz_t", 0) if self.debug else 0
        if dbg_yz_t:
            def vew(v):
                self.w(v, "dbg", dbg_yz_t)
            self.VE(vew)
        ar_ve = {}
        yz_ve = {}
        rp = {}
        for ci in range(5):
            pends = []
            for m in range(8):
                if ci < 4:
                    c0, n = CBS4[ci]
                    yz_ve[ci] = self.emit_a_pair(ci, c0, n, 2 * m)
                if ci >= 1:
                    pc0, pn = CBS4[ci - 1]
                    if m == 0 and ci >= 2:
                        rp[ci - 2] = self.emit_lnmid_stats(ci - 2, *CBS4[ci - 2],
                                                           ar_ve[ci - 2])
                    if m == 1 and ci >= 2:
                        self.emit_lnmid_apply(ci - 2, *CBS4[ci - 2], rp[ci - 2])
                        for pend in pends:
                            self.ps_held.discard(pend[0])
                            self.emit_op_evac(pend)
                        pends = []
                    r = self.emit_op_m(ci - 1, pc0, pn, m, yz_ve[ci - 1],
                                       defer_evac=(m == 0 and ci >= 2))
                    if m == 0 and ci >= 2:
                        pends.append(r)
                    else:
                        ar_ve[ci - 1] = r
        rp[3] = self.emit_lnmid_stats(3, CBS4[3][0], CBS4[3][1], ar_ve[3])
        self.emit_lnmid_apply(3, CBS4[3][0], CBS4[3][1], rp[3])
        self.x2_ready = self.cnt["ve"]
        if self.debug:
            def gp(g):
                self.w(g, "ve", self.x2_ready)
                self.w(g, "dbg", self.cnt["dbg"])
                ins = g.dma_start(out=self.dbg["dbg_x2"],
                                  in_=T["x2"][:].rearrange("p (c w) -> p c w", w=W)[:, :, 0:TOK])
                self.inc(ins, "dbg", 16)
            self.GP(gp)

    @staticmethod
    def _view(t, base=0):
        class V:
            def __getitem__(self, key):
                p, cc = key
                return t[p, base + (cc.start or 0):base + cc.stop]
        return V()

    def emit_ffn_head(self, b):
        for p in self.make_ffn_head(b):
            p()

    def make_ffn_head(self, b):
        T = self.T
        x2v = T["x2"][:].rearrange("p (c w) -> p c w", w=W)
        l2buf = T["ln2"] if b % 2 == 0 else T["ln2b"]
        self._l2buf = getattr(self, "_l2buf", {})
        self._l2buf[b] = l2buf
        l2v = l2buf[:].rearrange("p (c w) -> p c w", w=512)
        xmv = T["xm"][:].rearrange("p (e c w) -> p e c w", e=NE, c=NCH_D)
        rbs = (T["rb0f"], T["rb1f"])
        t0 = b * 512
        N = 512
        pieces = []
        hst = {}
        if self.debug and b == 1:
            dtot = self.cnt["dbg"]
            def vew(v):
                self.w(v, "dbg", dtot)
            self.VE(vew)
            def scw(s):
                self.w(s, "dbg", dtot)
            self.SC(scw)

        def p_stats():
            sq_slots = [self._view(T["sqf0"]), self._view(T["sqf1"])]
            self.ln_stats([(0, N)], NCH_D,
                          lambda ch, cc0, nn: x2v[:, ch, t0:t0 + nn],
                          sq_slots, copy_slots=None,
                          first_waits=(("ve", self.x2_ready),),
                          rows=None, key="sqf", sq_vector=True, keep_psum=True)
            hst["stats_mm"] = self.cnt["mm"]
        pieces.append(p_stats)
        def p_smalls():
            hst["ready"] = self.ln_smalls2(T["r2f"], T["r3f"], T["rb0f"], T["rb1f"],
                                           N, EPS_LN, hst["stats_mm"])
        pieces.append(p_smalls)
        def p_bcast_apply():
            ready = hst["ready"]
            hst["ready2"] = ready
            Pm, im, Pr, ir, mmt = self.ln_bcast2(rbs, 0, N, ready)
            def ve_l2(v):
                self.w(v, "mm", mmt)
                for c0_ in range(0, NCH_D, 2):
                    ts = []
                    for ch in (c0_, c0_ + 1):
                        ins = v.tensor_sub(l2v[:, ch, 0:N], x2v[:, ch, t0:t0 + N],
                                           Pm[:, 0:N])
                        ts.append(self.inc(ins, "ve"))
                    for ch, t in zip((c0_, c0_ + 1), ts):
                        self.w(v, "ve", t)
                        ins = v.tensor_mul(l2v[:, ch, 0:N], l2v[:, ch, 0:N],
                                           Pr[:, 0:N])
                        self.inc(ins, "ve")
                self.ps_release(im, "ve")
                self.ps_release(ir, "ve")
            self.VE(ve_l2)
            hst["ln2_ve"] = self.cnt["ve"]
        pieces.append(p_bcast_apply)

        def p_router():
            ready = hst["ready2"]
            res = {}
            def te_r(te):
                self.w(te, "ve", self.x2_ready)
                self.w(te, "smallw", self.small_w_dw)
                P, pi = self.ps_acquire(te)
                for k in range(8):
                    ins = te.matmul(P[0:4, 0:N], T["wrt_sb"][:, 4 * k:4 * k + 4],
                                    x2v[:, k, t0:t0 + N], start=(k == 0), stop=(k == 7))
                self.inc(ins, "mm")
                P2, pi2 = self.ps_acquire(te)
                for j in range(0, len(ready), 2):
                    self.w(te, ready[j], ready[j + 1])
                self.w(te, "ve", self.cs_ve)
                ins = te.matmul(P2[0:4, 0:N], T["cs_row"][0:1, 0:4], rbs[0][0:1, 0:N],
                                start=True, stop=True)
                self.inc(ins, "mm")
                res["r"] = (pi, pi2, self.cnt["mm"])
            self.TE(te_r)
            pi_lg, pi_adj, mmt = res["r"]
            def ve_lg(v):
                self.w(v, "mm", mmt)
                ins = v.tensor_copy(T["lg_sb"][:], self.PS[pi_lg][0:4, 0:N])
                t = self.inc(ins, "ve")
                self.w(v, "ve", t)
                ins = v.tensor_sub(T["lg_sb"][:], T["lg_sb"][:],
                                   self.PS[pi_adj][0:4, 0:N])
                self.inc(ins, "ve")
                self.ps_release(pi_lg, "ve")
                self.ps_release(pi_adj, "ve")
            self.VE(ve_lg)
            hst["lg_ve"] = self.cnt["ve"]
        pieces.append(p_router)

        def p_cascade():
            lg_ve = hst["lg_ve"]
            res2 = {}
            def te_t(te):
                self.w(te, "ve", lg_ve)
                P, pi = self.ps_acquire(te)
                for g in range(4):
                    ins = te.transpose(P[:, 4 * g:4 * g + 4],
                                       T["lg_sb"][:, 128 * g:128 * (g + 1)],
                                       T["id32"][0:4, 0:4])
                self.inc(ins, "mm")
                res2["r"] = (pi, self.cnt["mm"])
            self.TE(te_t)
            pi_tm, mmt = res2["r"]
            def ve_c(v):
                def st(ins):
                    t = self.inc(ins, "ve")
                    self.w(v, "ve", t)
                self.w(v, "mm", mmt)
                st(v.tensor_copy(T["lt_tm"][:], self.PS[pi_tm][:, 0:16]))
                self.ps_release(pi_tm, "ve")
                lt3 = T["lt_tm"][:].rearrange("p (g e) -> p g e", e=4)
                sel3 = T["sel"][:].rearrange("p (g e) -> p g e", e=4)
                st(v.reduce_max(T["rmax"][:], lt3, axis=AX.X))
                st(v.memset(T["notac"][:], 1.0))
                for e in range(4):
                    st(v.tensor_tensor(T["tmp4"][:], lt3[:, :, e], T["rmax"][:],
                                       ALU.is_ge))
                    st(v.tensor_mul(sel3[:, :, e], T["tmp4"][:], T["notac"][:]))
                    if e < 3:
                        st(v.tensor_sub(T["notac"][:], T["notac"][:], sel3[:, :, e]))
            self.VE(ve_c)
            hst["sel_ve"] = self.cnt["ve"]
        pieces.append(p_cascade)

        def p_mask4():
            sel_ve = hst["sel_ve"]
            res3 = {}
            def te_m(te):
                self.w(te, "ve", sel_ve)
                P, pi = self.ps_acquire(te)
                for g in range(4):
                    ins = te.transpose(P[0:4, 128 * g:128 * (g + 1)],
                                       T["sel"][:, 4 * g:4 * g + 4], T["id32"][:])
                self.inc(ins, "mm")
                res3["r"] = (pi, self.cnt["mm"])
            self.TE(te_m)
            pi_m4, mmt = res3["r"]
            def ve_m4(v):
                self.w(v, "mm", mmt)
                ins = v.tensor_copy(T["mask4b"][:], self.PS[pi_m4][0:4, 0:N])
                self.inc(ins, "ve")
                self.ps_release(pi_m4, "ve")
            self.VE(ve_m4)
            hst["mask4_ve"] = self.cnt["ve"]
        pieces.append(p_mask4)

        def mk_mrow(e):
            def p_mrow():
                sel3 = T["sel"][:].rearrange("p (g e) -> p g e", e=4)
                sel_ve = hst["sel_ve"]
                res4 = {}
                def te_e(te, e=e):
                    self.w(te, "ve", sel_ve)
                    P, pi = self.ps_acquire(te)
                    for g in range(4):
                        ins = te.transpose(P[0:1, g * 128:(g + 1) * 128],
                                           sel3[:, g:g + 1, e], T["id32"][:])
                    self.inc(ins, "mm")
                    res4["r"] = (pi, self.cnt["mm"])
                self.TE(te_e)
                pi, mmt = res4["r"]
                def ve_e(v, e=e, pi=pi, mmt=mmt):
                    self.w(v, "mm", mmt)
                    ins = v.tensor_copy(T["mrow4"][0:1, e * 512:(e + 1) * 512],
                                        self.PS[pi][0:1, 0:N])
                    self.inc(ins, "ve")
                    self.ps_release(pi, "ve")
                self.VE(ve_e)
            return p_mrow
        for e in range(4):
            pieces.append(mk_mrow(e))
        def mk_mrep(e):
            def p_mrep():
                mrow_ve = self.cnt["ve"]
                res5 = {}
                def te_e(te, e=e, mrow_ve=mrow_ve):
                    self.w(te, "ve", mrow_ve)
                    P, pi = self.ps_acquire(te)
                    ins = te.matmul(P[:, 0:N], T["ones_row"][:],
                                    T["mrow4"][0:1, e * 512:(e + 1) * 512],
                                    start=True, stop=True)
                    self.inc(ins, "mm")
                    res5["r"] = (pi, self.cnt["mm"])
                self.TE(te_e)
                pi, mmt = res5["r"]
                def ve_e(v, e=e, pi=pi, mmt=mmt):
                    self.w(v, "mm", mmt)
                    ins = v.tensor_copy(T["mrep"][:, e * 512:(e + 1) * 512],
                                        self.PS[pi][:, 0:N])
                    self.inc(ins, "ve")
                    self.ps_release(pi, "ve")
                self.VE(ve_e)
            return p_mrep
        for e in range(4):
            pieces.append(mk_mrep(e))

        def mk_xm(half):
            def p_xm():
                mrep_ve = self.cnt["ve"]
                def ve_xm(v):
                    self.w(v, "ve", mrep_ve)
                    self.w(v, "ve", hst["ln2_ve"])
                    for e in range(2 * half, 2 * half + 2):
                        for ch in range(NCH_D):
                            ins = v.tensor_mul(xmv[:, e, ch, 0:N], l2v[:, ch, 0:N],
                                               T["mrep"][:, e * 512:(e + 1) * 512])
                            self._xm_tile_t[(b, e, ch)] = self.inc(ins, "ve")
                self.VE(ve_xm)
            return p_xm
        pieces.append(mk_xm(0))
        pieces.append(mk_xm(1))
        def p_fin():
            self._ffn_head_t[b] = (self.cnt["ve"], hst["mask4_ve"], 0)
            if self.debug and b == 0:
                def gp(g):
                    self.w(g, "ve", self.cnt["ve"])
                    self.w(g, "dbg", self.cnt["dbg"])
                    ins = g.dma_start(out=self.dbg["dbg_ln2"], in_=l2buf[:])
                    self.inc(ins, "dbg", 16)
                    ins = g.dma_start(out=self.dbg["dbg_mask"], in_=T["mask4b"][:])
                    self.inc(ins, "dbg", 16)
                self.GP(gp)
        pieces.append(p_fin)
        return pieces

    def issue_w1_dma(self, b, m, sl_override=None):
        wf = [self.T["wf0"], self.T["wf1"], self.T["wf2"]]
        if sl_override is None:
            sl = self.ffn_slot_i % 3
            self.ffn_slot_i += 1
        else:
            sl = sl_override
        fsem = f"wf{sl}"
        def gp(g, sl=sl, m=m, fsem=fsem):
            if self.ffn_slot_user[sl] is None:
                self.w(g, "ve", self.x2_ready)
            else:
                self.w(g, "mm", self.ffn_slot_user[sl])
            wv = wf[sl][:].rearrange("p (j c) -> p j c", c=128)
            ins = g.dma_start(out=wv[:, 0:17], in_=self.dr["w1b_t"][m][:, 0:17])
            self.inc(ins, fsem, 16)
            ins = g.dma_start(out=wv[:, 17:33], in_=self.dr["w1b_t"][m][:, 17:33])
            self.inc(ins, fsem, 16)
        self.GP(gp)
        return sl, fsem, self.cnt[fsem]

    def emit_ffn_expert_m(self, b, m):
        T = self.T
        xmv = T["xm"][:].rearrange("p (e c w) -> p e c w", e=NE, c=NCH_D)
        hv = T["hsel"][:].rearrange("p (c w) -> p c w", w=512)
        wf = [T["wf0"], T["wf1"], T["wf2"]]
        N = 512
        xm_ve, mask4_ve, xm_gx = self._ffn_head_t[b]
        if (b, m) in self._wf_pre:
            sl, fsem, dw_t = self._wf_pre.pop((b, m))
        else:
            sl, fsem, dw_t = self.issue_w1_dma(b, m)
        res = {}
        def te(te, sl=sl, dw_t=dw_t, fsem=fsem, m=m, b=b):
            self.w(te, fsem, dw_t)
            if m == 0:
                self.w(te, "ve", mask4_ve)
            else:
                self.w(te, "ve", max(xm_ve, mask4_ve))
            self.w(te, "gx", xm_gx)
            P, pi = self.ps_acquire(te)
            first = True
            last_w = mask4_ve
            for e in range(4):
                for k in range(8):
                    j = e * 8 + k
                    if m == 0:
                        t = self._xm_tile_t[(b, e, k)]
                        if t > last_w:
                            self.w(te, "ve", t)
                            last_w = t
                    te.matmul(P[:, 0:N], wf[sl][:, j * 128:(j + 1) * 128],
                              xmv[:, e, k, 0:N], start=first, stop=False)
                    first = False
            ins = te.matmul(P[:, 0:N], wf[sl][0:4, 32 * 128:32 * 128 + 128],
                            T["mask4b"][0:4, 0:N], start=False, stop=True)
            self.inc(ins, "mm")
            res["r"] = (pi, self.cnt["mm"])
        self.TE(te)
        self.ffn_slot_user[sl] = self.cnt["mm"]
        pi, mmt = res["r"]
        def sc(s, pi=pi, mmt=mmt, m=m):
            self.w(s, "mm", mmt)
            ins = s.activation(hv[:, m, 0:N], self.PS[pi][:, 0:N], AF.Gelu)
            self.inc(ins, "se")
            self.ps_release(pi, "se")
        self.SC(sc)

    def emit_ffn_wo(self, b, head_pieces=None):
        T = self.T
        hv = T["hsel"][:].rearrange("p (c w) -> p c w", w=512)
        yv = T["yffn"][:].rearrange("p (c w) -> p c w", w=512)
        l2v = self._l2buf[b][:].rearrange("p (c w) -> p c w", w=512)
        wf = [T["wf0"], T["wf1"], T["wf2"]]
        N = 512
        hsel_se = self.cnt["se"]
        if self.debug and b == 0:
            def gpd(g):
                self.w(g, "se", hsel_se)
                self.w(g, "dbg", self.cnt["dbg"])
                ins = g.dma_start(out=self.dbg["dbg_hsel"], in_=T["hsel"][:])
                self.inc(ins, "dbg", 16)
            self.GP(gpd)
        i0 = self.ffn_slot_i
        OFF = [0, 1, 2, 0, 1, 2, 1, 2]
        for mo in range(8):
            sl = (i0 + OFF[mo]) % 3
            fsem = f"wf{sl}"
            def gp(g, sl=sl, mo=mo, fsem=fsem):
                self.w(g, "mm", self.ffn_slot_user[sl])
                wv = wf[sl][:, 0:32 * 128].rearrange("p (j c) -> p j c", c=128)
                ins = g.dma_start(out=wv[:, 0:16], in_=self.dr["wo_t"][mo][:, 0:16])
                self.inc(ins, fsem, 16)
                ins = g.dma_start(out=wv[:, 16:32], in_=self.dr["wo_t"][mo][:, 16:32])
                self.inc(ins, fsem, 16)
            self.GP(gp)
            dw_t = self.cnt[fsem]
            res = {}
            def te(te, sl=sl, dw_t=dw_t, fsem=fsem):
                self.w(te, fsem, dw_t)
                self.w(te, "se", hsel_se)
                P, pi = self.ps_acquire(te)
                for k in range(32):
                    ins = te.matmul(P[:, 0:N], wf[sl][:, k * 128:(k + 1) * 128],
                                    hv[:, k, 0:N], start=(k == 0), stop=(k == 31))
                self.inc(ins, "mm")
                res["r"] = (pi, self.cnt["mm"])
            self.TE(te)
            self.ffn_slot_user[sl] = self.cnt["mm"]
            pi, mmt = res["r"]
            dbg_now = self.cnt["dbg"] if self.debug else 0
            def ve(v, pi=pi, mmt=mmt, mo=mo, dbg_now=dbg_now):
                self.w(v, "mm", mmt)
                if mo == 0 and dbg_now:
                    self.w(v, "dbg", dbg_now)
                ins = v.scalar_tensor_tensor(yv[:, mo, 0:N], self.PS[pi][:, 0:N],
                                             T["bo_sb"][:, mo:mo + 1], l2v[:, mo, 0:N],
                                             ALU.add, ALU.add)
                self.inc(ins, "ve")
                self.ps_release(pi, "ve")
            self.VE(ve)
            if head_pieces:
                head_pieces.pop(0)()
            if mo >= 1 and head_pieces:
                head_pieces.pop(0)()
        if b + 1 < NBLK:
            for m in range(3):
                self._wf_pre[(b + 1, m)] = self.issue_w1_dma(
                    b + 1, m, sl_override=(i0 + m) % 3)
        self.ffn_slot_i = i0 + 9
        self._ffn_y_ve[b] = self.cnt["ve"]
        if self.debug and b == 0:
            def gpd2(g):
                self.w(g, "ve", self._ffn_y_ve[b])
                self.w(g, "dbg", self.cnt["dbg"])
                ins = g.dma_start(out=self.dbg["dbg_y"], in_=T["yffn"][:])
                self.inc(ins, "dbg", 16)
            self.GP(gpd2)

    def make_ffn_tail(self, b):
        T = self.T
        yv = T["yffn"][:].rearrange("p (c w) -> p c w", w=512)
        frv = T["fret"][:].rearrange("p (c w) -> p c w", w=512)
        x2v = T["x2"][:].rearrange("p (c w) -> p c w", w=W)
        ov = T["out_tm"][:].rearrange("p (g f) -> p g f", f=1024)
        rows = (T["r0f"], T["r1f"], T["r2f"], T["r3f"])
        rbs = (T["rb0f"], T["rb1f"])
        t0 = b * 512
        N = 512
        y_ve = self._ffn_y_ve[b]
        st = {}
        pieces = []

        sqf = [T["sqf0"], T["sqf1"]]
        st["aff_sq_t"] = {}
        st["aff_sqmm"] = {}
        def mk_aff_feed(cc):
            def p(cc=cc):
                prev = self._slot_guard.get("sqf", [0, 0])
                def ve_f(v):
                    for ch in (cc, cc + 1):
                        g = prev[ch % 2] if ch < 2 else st["aff_sqmm"][ch - 2]
                        self.w(v, "mm", g)
                        if ch == 0:
                            self.w(v, "ve", y_ve)
                        ins = v.tensor_mul(sqf[ch % 2][:, 0:N], yv[:, ch, 0:N],
                                           yv[:, ch, 0:N])
                        st["aff_sq_t"][ch] = self.inc(ins, "ve")
                self.VE(ve_f)
            return p
        def mk_aff_mm(cc):
            def p(cc=cc):
                def te_a(te):
                    for ch in (cc, cc + 1):
                        self.w(te, "ve", st["aff_sq_t"][ch])
                        if ch == 0:
                            self.ps_guard_wait(te, 6)
                            self.ps_guard_wait(te, 7)
                        te.matmul(self.PS[6][0:1, 0:N], T["inv_colf"][:],
                                  yv[:, ch, 0:N],
                                  start=(ch == 0), stop=(ch == NCH_D - 1),
                                  skip_group_check=True)
                        ins = te.matmul(self.PS[7][0:1, 0:N], T["inv_col"][:],
                                        sqf[ch % 2][:, 0:N],
                                        start=(ch == 0), stop=(ch == NCH_D - 1),
                                        skip_group_check=True)
                        st["aff_sqmm"][ch] = self.inc(ins, "mm")
                self.TE(te_a)
                if cc == NCH_D - 2:
                    self._slot_guard["sqf"] = [st["aff_sqmm"][NCH_D - 2],
                                               st["aff_sqmm"][NCH_D - 1]]
                    st["aff_stats_mm"] = self.cnt["mm"]
            return p
        for cc in range(0, NCH_D, 2):
            pieces.append(mk_aff_feed(cc))
            pieces.append(mk_aff_mm(cc))

        def p_aff_smalls():
            st["aff_ready"] = self.ln_smalls2(T["r2f"], T["r3f"], T["rb0f"], T["rb1f"],
                                              N, EPS_AFF, st["aff_stats_mm"])
        pieces.append(p_aff_smalls)

        def p_aff_bcast():
            st["aff_bc"] = self.ln_bcast2(rbs, 0, N, st["aff_ready"])
        pieces.append(p_aff_bcast)

        def p_aff_apply():
            Pm2, im2, Pr2, ir2, mmt2 = st["aff_bc"]
            mul_t = [0] * NCH_D
            def ve_all(v):
                self.w(v, "mm", mmt2)
                for c0_ in range(0, NCH_D, 2):
                    ts = []
                    for ch in (c0_, c0_ + 1):
                        ins = v.tensor_sub(frv[:, ch, 0:N], yv[:, ch, 0:N],
                                           Pm2[:, 0:N])
                        ts.append(self.inc(ins, "ve"))
                    for ch, t in zip((c0_, c0_ + 1), ts):
                        self.w(v, "ve", t)
                        ins = v.tensor_mul(frv[:, ch, 0:N], frv[:, ch, 0:N],
                                           Pr2[:, 0:N])
                        mul_t[ch] = self.inc(ins, "ve")
                self.ps_release(im2, "ve")
                self.ps_release(ir2, "ve")
            self.VE(ve_all)
            def sc_all(s):
                for ch in range(NCH_D):
                    self.w(s, "ve", mul_t[ch])
                    ins = s.activation(frv[:, ch, 0:N], frv[:, ch, 0:N], AF.Identity,
                                       scale=T["gam_sb"][:, ch:ch + 1],
                                       bias=T["bet_sb"][:, ch:ch + 1])
                    self.inc(ins, "se")
            self.SC(sc_all)
            st["fr_se"] = self.cnt["se"]
        pieces.append(p_aff_apply)

        st["out_sq_t"] = {}
        st["out_sqmm"] = {}
        def mk_out_feed(cc):
            def p(cc=cc):
                prev = self._slot_guard.get("sqf", [0, 0])
                def ve_f(v):
                    for ch in (cc, cc + 1):
                        g = prev[ch % 2] if ch < 2 else st["out_sqmm"][ch - 2]
                        self.w(v, "mm", g)
                        if ch == 0:
                            self.w(v, "se", st["fr_se"])
                        ins = v.tensor_mul(sqf[ch % 2][:, 0:N], frv[:, ch, 0:N],
                                           frv[:, ch, 0:N])
                        st["out_sq_t"][ch] = self.inc(ins, "ve")
                self.VE(ve_f)
            return p
        def mk_out_mm(cc):
            def p(cc=cc):
                def te_o(te):
                    for ch in (cc, cc + 1):
                        self.w(te, "ve", st["out_sq_t"][ch])
                        if ch == 0:
                            self.ps_guard_wait(te, 6)
                            self.ps_guard_wait(te, 7)
                        te.matmul(self.PS[6][0:1, 0:N], T["inv_col"][:],
                                  frv[:, ch, 0:N],
                                  start=(ch == 0), stop=(ch == NCH_D - 1),
                                  skip_group_check=True)
                        ins = te.matmul(self.PS[7][0:1, 0:N], T["inv_col"][:],
                                        sqf[ch % 2][:, 0:N],
                                        start=(ch == 0), stop=(ch == NCH_D - 1),
                                        skip_group_check=True)
                        st["out_sqmm"][ch] = self.inc(ins, "mm")
                self.TE(te_o)
                if cc == NCH_D - 2:
                    self._slot_guard["sqf"] = [st["out_sqmm"][NCH_D - 2],
                                               st["out_sqmm"][NCH_D - 1]]
                    st["out_stats_mm"] = self.cnt["mm"]
            return p
        for cc in range(0, NCH_D, 2):
            pieces.append(mk_out_feed(cc))
            pieces.append(mk_out_mm(cc))

        def p_out_smalls():
            st["out_ready"] = self.ln_smalls2(T["r2f"], T["r3f"], T["rb0f"], T["rb1f"],
                                              N, EPS_LN, st["out_stats_mm"])
        pieces.append(p_out_smalls)

        def p_out_bcast():
            st["out_bc"] = self.ln_bcast2(rbs, 0, N, st["out_ready"])
        pieces.append(p_out_bcast)

        prev_tp = [0, 0]
        def mk_out_pair(cc):
            def p_out_pair():
                Pm3, im3, Pr3, ir3, mmt4 = st["out_bc"]
                bufs = (T["x3t"], T["aff_tap"])
                res9 = {}
                def ve(v, cc=cc):
                    if cc == 0:
                        self.w(v, "mm", mmt4)
                        if b > 0:
                            self.w(v, "dx", self.out_dx_prev)
                    if prev_tp[0]:
                        self.w(v, "mm", prev_tp[0])
                    if prev_tp[1]:
                        self.w(v, "mm", prev_tp[1])
                    ts = []
                    for j, ch in enumerate((cc, cc + 1)):
                        ins = v.tensor_sub(bufs[j][:, 0:N], frv[:, ch, 0:N],
                                           Pm3[:, 0:N])
                        ts.append(self.inc(ins, "ve"))
                    for j in range(2):
                        self.w(v, "ve", ts[j])
                        ins = v.tensor_mul(bufs[j][:, 0:N], bufs[j][:, 0:N],
                                           Pr3[:, 0:N])
                        ts[j] = self.inc(ins, "ve")
                    for j, ch in enumerate((cc, cc + 1)):
                        self.w(v, "ve", ts[j])
                        ins = v.tensor_add(bufs[j][:, 0:N], x2v[:, ch, t0:t0 + N],
                                           bufs[j][:, 0:N])
                        ts[j] = self.inc(ins, "ve")
                    res9["ve"] = tuple(ts)
                    if cc == NCH_D - 2:
                        self.ps_release(im3, "ve")
                        self.ps_release(ir3, "ve")
                self.VE(ve)
                tps = []
                for j, ch in enumerate((cc, cc + 1)):
                    res10 = {}
                    def te(te, j=j, res10=res10):
                        self.w(te, "ve", res9["ve"][j])
                        P, pi = self.ps_acquire(te)
                        for g in range(4):
                            ins = te.transpose(P[:, g * 128:(g + 1) * 128],
                                               bufs[j][:, g * 128:(g + 1) * 128],
                                               T["id32"][:])
                        self.inc(ins, "mm")
                        res10["mm"] = (pi, self.cnt["mm"])
                    self.TE(te)
                    pi, mmt5 = res10["mm"]
                    prev_tp[j] = mmt5
                    def ve2(v, ch=ch, pi=pi, mmt5=mmt5):
                        self.w(v, "mm", mmt5)
                        src = self.PS[pi][:, 0:512].rearrange("p (g f) -> p g f", g=4)
                        ins = v.tensor_copy(ov[:, :, ch * 128:(ch + 1) * 128], src)
                        self.inc(ins, "ve")
                        self.ps_release(pi, "ve")
                    self.VE(ve2)
            return p_out_pair
        for cc in range(0, NCH_D, 2):
            pieces.append(mk_out_pair(cc))

        def p_store():
            out_ve = self.cnt["ve"]
            def sy(s, b=b, out_ve=out_ve):
                self.w(s, "ve", out_ve)
                for g in range(4):
                    r0_ = b * 512 + g * 128
                    ins = s.dma_start(out=self.dr["y_out"][r0_:r0_ + 128, :],
                                      in_=self.T["out_tm"][:, g * 1024:(g + 1) * 1024])
                    self.inc(ins, "dx", 16)
            self.SY(sy)
            self.out_dx_prev = self.cnt["dx"]
        pieces.append(p_store)
        return pieces

    def emit_ffn_all(self):
        self._ffn_head_t = {}
        self._ffn_y_ve = {}
        self._wf_pre = {}
        self._xm_tile_t = {}
        for m in range(3):
            self._wf_pre[(0, m)] = self.issue_w1_dma(0, m)
        self.emit_ffn_head(0)
        tail = []
        for b in range(NBLK):
            for m in range(NCH_F):
                self.emit_ffn_expert_m(b, m)
                if tail:
                    tail.pop(0)()
            while tail:
                tail.pop(0)()
            heads = self.make_ffn_head(b + 1) if b + 1 < NBLK else []
            self.emit_ffn_wo(b, heads)
            while heads:
                heads.pop(0)()
            tail = self.make_ffn_tail(b)
        while tail:
            tail.pop(0)()

    _slot_guard = None


Kern._slot_guard = {}


def _run(debug, inputs):
    packed = pack_weights(inputs)
    x = np.asarray(inputs["L0_attn_x"], np.float32).reshape(NCORE * TOK, D)
    in_maps = []
    for c in range(NCORE):
        lo = c * TOK
        xl = np.zeros((TT, D), np.float32)
        if lo % 8192 != 0:
            xl[:HALO] = x[lo - HALO:lo]
        xl[HALO:] = x[lo:lo + TOK]
        m = {"x_loc": np.ascontiguousarray(xl)}
        m.update(packed)
        in_maps.append(m)
    kb = Kern(debug=debug)
    kb._slot_guard = {}
    nc = kb.build()
    res = run_bass_kernel_spmd(nc, in_maps, core_ids=list(range(NCORE)))
    return res


def kernel(**inputs):
    res = _run(False, inputs)
    out = np.concatenate([r["y_out"] for r in res.results], axis=0)
    return out.reshape(2, 8192, D).astype(np.float32)


# version 35
# speedup vs baseline: 1.0134x; 1.0036x over previous
import contextlib
import numpy as np
import concourse.bass as bass
import concourse.mybir as mybir
from concourse.bass_utils import run_bass_kernel_spmd

F32 = mybir.dt.float32
BF16 = mybir.dt.bfloat16
AF = mybir.ActivationFunctionType
ALU = mybir.AluOpType
AX = mybir.AxisListType

NCORE = 8
TOK = 2048
HALO = 3
TT = TOK + HALO
W = 2052
D = 1024
NCH_D = 8
NCH_I = 16
NCH_F = 32
NE = 4
EPS_LN = 1e-5
EPS_AFF = 1e-12
INVD = 1.0 / 1024.0

CBS5 = [(0, 512), (512, 512), (1024, 512), (1536, 512), (2048, 3)]
CBS4 = [(3, 512), (515, 512), (1027, 512), (1539, 512)]
NBLK = 4
SBUF_BASE = 18432
SBUF_LIMIT = 228000


def pack_weights(inp):
    W_in = np.asarray(inp["L0_attn_W_in"], np.float32)
    conv_w = np.asarray(inp["L0_attn_conv_w"], np.float32)
    conv_b = np.asarray(inp["L0_attn_conv_b"], np.float32)
    W_x = np.asarray(inp["L0_attn_W_x"], np.float32)
    W_dt = np.asarray(inp["L0_attn_W_dt"], np.float32)
    b_dt = np.asarray(inp["L0_attn_b_dt"], np.float32)
    W_out = np.asarray(inp["L0_attn_W_out"], np.float32)
    W_r = np.asarray(inp["L0_ffn_W_router"], np.float32)
    W1 = np.asarray(inp["L0_ffn_W1"], np.float32)
    b1 = np.asarray(inp["L0_ffn_b1"], np.float32)
    Wo = np.asarray(inp["L0_ffn_Wo"], np.float32)
    bo = np.asarray(inp["L0_ffn_bo"], np.float32)
    gam = np.asarray(inp["L0_ffn_ln_gamma"], np.float32)
    bet = np.asarray(inp["L0_ffn_ln_beta"], np.float32)
    C = np.ascontiguousarray
    d = {}
    d["w_in_t"] = C(W_in.reshape(32, 128, 8, 128).transpose(0, 3, 2, 1))
    dg = np.zeros((16, 4, 128, 128), np.float32)
    ii = np.arange(128)
    for c in range(16):
        for t in range(4):
            dg[c, t, ii, ii] = conv_w[c * 128:(c + 1) * 128, 0, t]
    d["diag_t"] = C(dg.transpose(2, 0, 1, 3))
    d["conv_b_p"] = C(conv_b.reshape(16, 128).T)
    d["w_x_t"] = C(W_x[:32].T.reshape(16, 128, 32).transpose(1, 0, 2))
    d["w_dt_t"] = C(W_dt.T.reshape(32, 16, 128))
    d["b_dt_p"] = C(b_dt.reshape(16, 128).T)
    d["w_out_t"] = C(W_out.reshape(8, 128, 16, 128).transpose(0, 3, 2, 1))
    d["w_r_t"] = C(W_r.reshape(8, 128, 4).transpose(1, 0, 2))
    w1t = W1.reshape(4, 8, 128, 32, 128).transpose(3, 0, 1, 2, 4).reshape(32, 32, 128, 128)
    b1pad = np.zeros((32, 1, 128, 128), np.float32)
    b1pad[:, 0, 0:4, :] = b1.reshape(4, 32, 128).transpose(1, 0, 2)
    d["w1b_t"] = C(np.concatenate([w1t, b1pad], axis=1).transpose(0, 2, 1, 3))
    d["wo_t"] = C(Wo.reshape(32, 128, 8, 128).transpose(2, 1, 0, 3))
    d["bo_p"] = C(bo.reshape(8, 128).T)
    d["gam_p"] = C(gam.reshape(8, 128).T)
    d["bet_p"] = C(bet.reshape(8, 128).T)
    d["ident"] = np.eye(128, dtype=np.float32)
    return d


class Kern:
    def __init__(self, debug=False):
        self.debug = debug
        self.nc = bass.Bass("TRN2", target_bir_lowering=False)
        self._sem = {}
        self.T = {}
        self.psg = [None] * 8
        self.ps_held = set()
        self.rot = [0, 1, 2, 3, 4, 5]
        self.rot_i = 0
        self.cnt = dict(dx=0, mm=0, ve=0, se=0)
        self.ffn_slot_i = 0
        self.ffn_slot_user = [None, None, None]

    def sb(self, name, shape, dtype, off):
        t = self.nc.alloc_sbuf_tensor_at(name, shape, dtype, offset=off)
        self.T[name] = t
        return t

    def inc(self, ins, which, k=1):
        ins.then_inc(self._sem[which], k)
        self.cnt[which] += k
        return self.cnt[which]

    def w(self, eng, which, target):
        if target:
            eng.wait_ge(self._sem[which], target)

    def ps_acquire(self, te):
        for _ in range(len(self.rot) + 1):
            i = self.rot[self.rot_i % len(self.rot)]
            self.rot_i += 1
            if i not in self.ps_held:
                break
        else:
            raise RuntimeError("all psum banks held")
        g = self.psg[i]
        if g is not None:
            self.w(te, g[0], g[1])
            self.psg[i] = None
        return self.PS[i], i

    def ps_release(self, i, which):
        self.psg[i] = (which, self.cnt[which])
        self.ps_held.discard(i)

    def ps_guard_wait(self, te, i):
        g = self.psg[i]
        if g is not None:
            self.w(te, g[0], g[1])
            self.psg[i] = None

    def TE(self, f):
        self.block.tensor(f)

    def VE(self, f):
        self.block.vector(f)

    def SC(self, f):
        self.block.scalar(f)

    def GP(self, f):
        self.block.gpsimd(f)

    def SY(self, f):
        self.block.sync(f)

    def build(self):
        nc = self.nc
        self._es = contextlib.ExitStack()
        es = self._es
        dr = {}
        dr["x_loc"] = nc.dram_tensor("x_loc", [TT, D], F32, kind="ExternalInput").ap()
        dr["w_in_t"] = nc.dram_tensor("w_in_t", [32, 128, 8, 128], F32, kind="ExternalInput").ap()
        dr["diag_t"] = nc.dram_tensor("diag_t", [128, 16, 4, 128], F32, kind="ExternalInput").ap()
        dr["conv_b_p"] = nc.dram_tensor("conv_b_p", [128, 16], F32, kind="ExternalInput").ap()
        dr["w_x_t"] = nc.dram_tensor("w_x_t", [128, 16, 32], F32, kind="ExternalInput").ap()
        dr["w_dt_t"] = nc.dram_tensor("w_dt_t", [32, 16, 128], F32, kind="ExternalInput").ap()
        dr["b_dt_p"] = nc.dram_tensor("b_dt_p", [128, 16], F32, kind="ExternalInput").ap()
        dr["w_out_t"] = nc.dram_tensor("w_out_t", [8, 128, 16, 128], F32, kind="ExternalInput").ap()
        dr["w_r_t"] = nc.dram_tensor("w_r_t", [128, 8, 4], F32, kind="ExternalInput").ap()
        dr["w1b_t"] = nc.dram_tensor("w1b_t", [32, 128, 33, 128], F32, kind="ExternalInput").ap()
        dr["wo_t"] = nc.dram_tensor("wo_t", [8, 128, 32, 128], F32, kind="ExternalInput").ap()
        dr["bo_p"] = nc.dram_tensor("bo_p", [128, 8], F32, kind="ExternalInput").ap()
        dr["gam_p"] = nc.dram_tensor("gam_p", [128, 8], F32, kind="ExternalInput").ap()
        dr["bet_p"] = nc.dram_tensor("bet_p", [128, 8], F32, kind="ExternalInput").ap()
        dr["ident"] = nc.dram_tensor("ident", [128, 128], F32, kind="ExternalInput").ap()
        dr["y_out"] = nc.dram_tensor("y_out", [TOK, D], F32, kind="ExternalOutput").ap()
        self.dr = dr
        self.dbg = {}
        if self.debug:
            for nm, shp, dt in [("dbg_ln1", [128, NCH_D, TT], F32),
                                ("dbg_xc", [128, NCH_I, TOK], F32),
                                ("dbg_x2", [128, NCH_D, TOK], F32),
                                ("dbg_ln2", [128, NCH_D * 512], F32),
                                ("dbg_mask", [4, 512], F32),
                                ("dbg_hsel", [128, NCH_F * 512], F32),
                                ("dbg_y", [128, NCH_D * 512], F32)]:
                self.dbg[nm] = nc.dram_tensor(nm, shp, dt, kind="ExternalOutput").ap()

        o = SBUF_BASE
        def take(n):
            nonlocal o
            o = (o + 31) // 32 * 32
            r = o
            o += n
            return r
        self.sb("id32", [128, 128], F32, take(512))
        self.sb("ones_row", [1, 128], BF16, take(256))
        self.sb("inv_col", [128, 1], BF16, take(32))
        self.sb("inv_colf", [128, 1], F32, take(32))
        self.sb("conv_b_sb", [128, 16], F32, take(64))
        self.sb("b_dt_sb", [128, 16], F32, take(64))
        self.sb("bo_sb", [128, 8], F32, take(32))
        self.sb("gam_sb", [128, 8], F32, take(32))
        self.sb("bet_sb", [128, 8], F32, take(32))
        self.sb("eps5_c", [1, 8], F32, take(32))
        self.sb("eps12_c", [1, 8], F32, take(32))
        self.sb("wrt_sb", [128, 32], BF16, take(64))
        self.sb("ones_col", [128, 1], BF16, take(32))
        self.sb("cs_row", [1, 4], BF16, take(32))
        self.sb("wx_sb", [128, 512], BF16, take(1024))
        off_x = take(NCH_D * W * 4)
        off_l = take(NCH_D * W * 2)
        off_c = take(NCH_I * W * 2)
        self.sb("x_fm", [128, NCH_D * W], F32, off_x)
        self.sb("ln1", [128, NCH_D * W], BF16, off_l)
        self.sb("xcyz", [128, NCH_I * W], BF16, off_c)
        SB = take(0)
        self.SB = SB
        avail = SBUF_LIMIT - SB
        self.sb("stg0", [128, D], F32, SB + 16512)
        self.sb("stg1", [128, D], F32, SB + 20608)
        self.sb("stg2", [128, D], F32, SB + 24704)
        self.sb("r0w", [1, W], F32, SB + 0)
        self.sb("r1w", [1, W], F32, SB + 8256)
        self.sb("rb0w", [1, W], BF16, SB + 8256)
        self.sb("rb1w", [1, W], BF16, SB + 8256 + 4128)
        self.sb("r2w", [1, W], F32, SB + 16512)
        self.sb("r3w", [1, W], F32, SB + 24768)
        self.sb("sqs0", [128, 1024], BF16, SB + 33024)
        self.sb("sqs1", [128, 1024], BF16, SB + 35072)
        assert SB + 37120 <= SBUF_LIMIT, (SB, avail)
        self.sb("win0", [128, 1024], BF16, SB + 38976)
        self.sb("win1", [128, 1024], BF16, SB + 41024)
        assert SB + 43072 <= SBUF_LIMIT
        self.sb("xs_tmp", [128, W], BF16, SB + 4096)
        self.sb("sz_tmp", [128, TOK], BF16, SB + 8224)
        self.sb("diag_sb", [128, 64 * 128], BF16, SB + 12320)
        self.sb("wdt_sb", [32, 2048], BF16, SB + 28704)
        self.sb("a0", [128, 512], BF16, SB + 32800)
        self.sb("a1", [128, 512], BF16, SB + 33824)
        self.sb("dt_sb", [32, 2048], BF16, SB + 34848)
        assert SB + 38944 <= SBUF_LIMIT
        self.sb("wout0", [128, 4096], BF16, SB + 0)
        self.sb("wout1", [128, 4096], BF16, SB + 8192)
        self.sb("arcb", [128, NCH_D * 512], BF16, SB + 16512)
        self.sb("r1o", [1, 512], F32, SB + 24704)
        self.sb("rb0o", [1, 512], BF16, SB + 24704)
        self.sb("rb1o", [1, 512], BF16, SB + 25728)
        self.sb("sqo0", [128, 512], BF16, SB + 26752)
        self.sb("r2o", [1, 512], F32, SB + 38976)
        self.sb("r3o", [1, 512], F32, SB + 41024)
        self.sb("sqo1", [128, 512], BF16, SB + 42048)
        self.sb("wf0", [128, 33 * 128], BF16, SB + 0)
        self.sb("wf1", [128, 33 * 128], BF16, SB + 8448)
        self.sb("wf2", [128, 33 * 128], BF16, SB + 16896)
        self.sb("r0f", [1, 512], F32, SB + 25344)
        self.sb("r1f", [1, 512], F32, SB + 27392)
        self.sb("rb0f", [1, 512], BF16, SB + 27392)
        self.sb("rb1f", [1, 512], BF16, SB + 28416)
        self.sb("r2f", [1, 512], F32, SB + 29440)
        self.sb("r3f", [1, 512], F32, SB + 31488)
        self.sb("sqf0", [128, 512], BF16, SB + 25344)
        self.sb("sqf1", [128, 512], BF16, SB + 26368)
        self.sb("ln2b", [128, NCH_D * 512], BF16, SB + 33536)
        assert SB + 41728 <= SBUF_LIMIT
        assert SB + 35584 <= SBUF_LIMIT
        self.sb("xm", [128, NE * NCH_D * 512], BF16, off_x)
        self.sb("hsel", [128, NCH_F * 512], BF16, off_x + 32768)
        self.sb("ln2", [128, NCH_D * 512], BF16, off_l)
        self.sb("mrep", [128, NE * 512], BF16, off_l + 8192)
        self.sb("out_tm", [128, 4 * 1024], F32, off_l + 12288)
        self.sb("lt_tm", [128, 16], F32, off_l + 28672)
        self.sb("rmax", [128, 4], F32, off_l + 28736)
        self.sb("sel", [128, 16], F32, off_l + 28768)
        self.sb("notac", [128, 4], F32, off_l + 28832)
        self.sb("tmp4", [128, 4], F32, off_l + 28864)
        self.sb("mask4b", [4, 512], BF16, off_l + 28896)
        self.sb("lg_sb", [4, 512], F32, off_l + 29920)
        self.sb("x2", [128, NCH_D * W], BF16, off_c)
        self.sb("yffn", [128, NCH_D * 512], F32, off_c + 32832)
        self.sb("fret", [128, NCH_D * 512], BF16, off_c + 49216)
        self.sb("x3t", [128, 512], F32, off_c + 57408)
        self.sb("mrow4", [1, NE * 512], BF16, off_c + 59456)
        self.sb("aff_tap", [128, 512], F32, off_c + 63552)

        self.PS = [es.enter_context(nc.psum_tensor(f"ps{i}", [128, 512], F32))
                   for i in range(8)]
        for s in ("dx", "mm", "ve", "se", "gx", "stgA", "stgB", "stgC", "win0", "win1",
                  "wout0", "wout1", "wf0", "wf1", "wf2", "diag", "smallw",
                  "wdt", "dbg"):
            self._sem[s] = es.enter_context(nc.semaphore(s))
            self.cnt.setdefault(s, 0)
        self.block = es.enter_context(nc.Block())
        try:
            self.emit_setup()
            self.emit_small_weights()
            self.emit_ln1()
            self.emit_mix()
            self.emit_dt_a()
            self.emit_out_mid()
            self.emit_ffn_all()
            totals = {k: self.cnt[k] for k in self._sem if self.cnt.get(k)}
            def fin(s):
                for k, v in totals.items():
                    if k in ("mm", "ve", "se"):
                        continue
                    s.wait_ge(self._sem[k], v)
            self.SY(fin)
        finally:
            self._es.close()
        return nc

    def emit_setup(self):
        T, dr = self.T, self.dr
        def sy(s):
            for dst, srcn in [("id32", "ident"), ("conv_b_sb", "conv_b_p"),
                              ("b_dt_sb", "b_dt_p"), ("bo_sb", "bo_p"),
                              ("gam_sb", "gam_p"), ("bet_sb", "bet_p")]:
                ins = s.dma_start(out=T[dst][:], in_=dr[srcn][:])
                self.inc(ins, "dx", 16)
        self.SY(sy)
        self.const_dx = self.cnt["dx"]

        def ve0(v):
            v.memset(T["ones_row"][:], 1.0)
            v.memset(T["ones_col"][:], 1.0)
            v.memset(T["eps5_c"][:], EPS_LN)
            v.memset(T["eps12_c"][:], EPS_AFF)
            v.memset(T["inv_col"][:], INVD)
            ins = v.memset(T["inv_colf"][:], INVD)
            self.inc(ins, "ve")
        self.VE(ve0)
        self.ones_ready = self.cnt["ve"]

        stg = [T["stg0"], T["stg1"], T["stg2"]]
        self.xtile_ve = {}
        xv = T["x_fm"][:].rearrange("p (c w) -> p c w", w=W)
        tile_done_mm = [0] * 17
        for t in range(17):
            rows = 128 if t < 16 else 3
            stg_sem = ("stgA", "stgB", "stgC")[t % 3]
            def sy2(s, t=t, rows=rows, stg_sem=stg_sem):
                if t >= 3:
                    self.w(s, "mm", tile_done_mm[t - 3])
                ins = s.dma_start(out=stg[t % 3][0:rows, :],
                                  in_=dr["x_loc"][t * 128:t * 128 + rows, :])
                self.inc(ins, stg_sem, 16)
            self.SY(sy2)
            dx_t = (stg_sem, self.cnt[stg_sem])
            evi = []
            def te(te, t=t, rows=rows, dx_t=dx_t):
                if t == 0:
                    self.w(te, "dx", self.const_dx)
                self.w(te, dx_t[0], dx_t[1])
                for half in range(2):
                    P, pi = self.ps_acquire(te)
                    for j in range(4):
                        ch = half * 4 + j
                        ins = te.transpose(P[:, j * rows:(j + 1) * rows],
                                           stg[t % 3][0:rows, ch * 128:(ch + 1) * 128],
                                           T["id32"][0:rows, 0:rows])
                    self.inc(ins, "mm")
                    evi.append((pi, self.cnt["mm"], half))
            self.TE(te)
            tile_done_mm[t] = self.cnt["mm"]
            def ve(v, t=t, rows=rows, evi=evi):
                for pi, mmt, half in evi:
                    self.w(v, "mm", mmt)
                    src = self.PS[pi][:, 0:4 * rows].rearrange("p (c r) -> p c r", c=4)
                    dst = xv[:, half * 4:half * 4 + 4, t * 128:t * 128 + rows]
                    ins = v.tensor_copy(dst, src)
                    self.inc(ins, "ve")
                    self.ps_release(pi, "ve")
            self.VE(ve)
            self.xtile_ve[t] = self.cnt["ve"]
        self.xfm_ready = self.cnt["ve"]

    def emit_small_weights(self):
        T, dr = self.T, self.dr
        def gp(g):
            ins = g.dma_start(out=T["wrt_sb"][:].rearrange("p (k j) -> p k j", j=4),
                              in_=dr["w_r_t"][:])
            self.inc(ins, "smallw", 16)
            ins = g.dma_start(out=T["wx_sb"][:].rearrange("p (k f) -> p k f", f=32),
                              in_=dr["w_x_t"][:])
            self.inc(ins, "smallw", 16)
        self.GP(gp)
        self.small_w_dw = self.cnt["smallw"]
        res = {}
        def te_cs(te):
            self.w(te, "smallw", self.small_w_dw)
            self.w(te, "ve", self.ones_ready)
            P, pi = self.ps_acquire(te)
            for k in range(8):
                ins = te.matmul(P[0:4, 0:1], T["wrt_sb"][:, 4 * k:4 * k + 4],
                                T["ones_col"][:], start=(k == 0), stop=(k == 7))
            self.inc(ins, "mm")
            res["a"] = (pi, self.cnt["mm"])
        self.TE(te_cs)
        pi_a, mm_a = res["a"]
        def ve_cs(v):
            self.w(v, "mm", mm_a)
            ins = v.tensor_copy(T["lg_sb"][0:4, 0:1], self.PS[pi_a][0:4, 0:1])
            self.inc(ins, "ve")
            self.ps_release(pi_a, "ve")
        self.VE(ve_cs)
        cs_ve = self.cnt["ve"]
        def te_cs2(te):
            self.w(te, "ve", cs_ve)
            P, pi = self.ps_acquire(te)
            ins = te.transpose(P[0:1, 0:4], T["lg_sb"][0:4, 0:1], T["id32"][0:4, 0:4])
            self.inc(ins, "mm")
            res["b"] = (pi, self.cnt["mm"])
        self.TE(te_cs2)
        pi_b, mm_b = res["b"]
        def ve_cs2(v):
            self.w(v, "mm", mm_b)
            ins = v.tensor_copy(T["cs_row"][0:1, 0:4], self.PS[pi_b][0:1, 0:4])
            self.inc(ins, "ve")
            self.ps_release(pi_b, "ve")
        self.VE(ve_cs2)
        self.cs_mm = self.cnt["mm"]
        self.cs_ve = self.cnt["ve"]

    def ln_stats(self, cbs, nch, src_ap, sq_slots, copy_slots=None,
                 first_waits=(), rows=None, key=None, sq_vector=False,
                 keep_psum=False, cb_waits=None, ch_waits=None):
        r0, r1 = rows if rows is not None else (None, None)
        guards = self._slot_guard
        key_sq = key or tuple(id(s) for s in sq_slots)
        prev = guards.get(key_sq, [0, 0])
        for ci, (c0, n) in enumerate(cbs):
            mm_base = self.cnt["mm"]
            pred = [mm_base + ch + 1 for ch in range(nch)]
            ve_t = [0] * nch
            se_t = [0] * nch
            if copy_slots is not None:
                def ve(v, ci=ci, c0=c0, n=n, prev=list(prev), pred=pred):
                    for ch in range(nch):
                        sl = ch % 2
                        g = prev[sl] if ch < 2 else pred[ch - 2]
                        self.w(v, "mm", g)
                        if ci == 0 and ch == 0:
                            for wk, wt in first_waits:
                                self.w(v, wk, wt)
                        if ch == 0 and cb_waits is not None:
                            for wk, wt in cb_waits(ci):
                                self.w(v, wk, wt)
                        ins = v.tensor_copy(copy_slots[sl][:, 0:n], src_ap(ch, c0, n))
                        ve_t[ch] = self.inc(ins, "ve")
                self.VE(ve)
            if sq_vector:
                def vsq(v, ci=ci, c0=c0, n=n, prev=list(prev), pred=pred):
                    for ch in range(nch):
                        sl = ch % 2
                        g = prev[sl] if ch < 2 else pred[ch - 2]
                        self.w(v, "mm", g)
                        if ci == 0 and ch == 0:
                            for wk, wt in first_waits:
                                self.w(v, wk, wt)
                        if ch_waits is not None:
                            for wk, wt in ch_waits(ch):
                                self.w(v, wk, wt)
                        sap = src_ap(ch, c0, n)
                        ins = v.tensor_mul(sq_slots[sl][:, 0:n], sap, sap)
                        se_t[ch] = 0
                        ve_t[ch] = self.inc(ins, "ve")
                self.VE(vsq)
            else:
                def sc(s, ci=ci, c0=c0, n=n, prev=list(prev), pred=pred):
                    for ch in range(nch):
                        sl = ch % 2
                        g = prev[sl] if ch < 2 else pred[ch - 2]
                        self.w(s, "mm", g)
                        if ci == 0 and ch == 0:
                            for wk, wt in first_waits:
                                self.w(s, wk, wt)
                        if ch == 0 and cb_waits is not None:
                            for wk, wt in cb_waits(ci):
                                self.w(s, wk, wt)
                        ins = s.activation(sq_slots[sl][:, 0:n], src_ap(ch, c0, n),
                                           AF.Square)
                        se_t[ch] = self.inc(ins, "se")
                self.SC(sc)
            def te(te, ci=ci, c0=c0, n=n, pred=pred):
                for ch in range(nch):
                    sl = ch % 2
                    if se_t[ch]:
                        self.w(te, "se", se_t[ch])
                    if ve_t[ch]:
                        self.w(te, "ve", ve_t[ch])
                    if ch == 0:
                        self.ps_guard_wait(te, 6)
                        self.ps_guard_wait(te, 7)
                    mean_rhs = (copy_slots[ch % 2][:, 0:n] if copy_slots is not None
                                else src_ap(ch, c0, n))
                    te.matmul(self.PS[6][0:1, 0:n], self.T["inv_col"][:], mean_rhs,
                              start=(ch == 0), stop=(ch == nch - 1), skip_group_check=True)
                    ins = te.matmul(self.PS[7][0:1, 0:n], self.T["inv_col"][:],
                                    sq_slots[sl][:, 0:n],
                                    start=(ch == 0), stop=(ch == nch - 1),
                                    skip_group_check=True)
                    self.inc(ins, "mm")
                assert self.cnt["mm"] == pred[-1], (self.cnt["mm"], pred[-1])
            self.TE(te)
            prev = [pred[nch - 2] if nch >= 2 else pred[-1], pred[nch - 1]]
            if keep_psum:
                self.psg[6] = ("mm", self.cnt["mm"])
                self.psg[7] = ("mm", self.cnt["mm"])
            else:
                mmt = self.cnt["mm"]
                def ve2(v, c0=c0, n=n, mmt=mmt):
                    self.w(v, "mm", mmt)
                    v.tensor_copy(r0[0:1, c0:c0 + n], self.PS[6][0:1, 0:n])
                    ins = v.tensor_copy(r1[0:1, c0:c0 + n], self.PS[7][0:1, 0:n])
                    self.inc(ins, "ve")
                    self.ps_release(6, "ve")
                    self.ps_release(7, "ve")
                self.VE(ve2)
        guards[key_sq] = prev

    def ln_smalls2(self, r2, r3, rb0, rb1, width, eps, stats_mm):
        N = width
        res = {}
        entry_ve = self.cnt["ve"]
        entry_se = self.cnt["se"]
        def sc1(s):
            self.w(s, "ve", entry_ve)
            self.w(s, "se", entry_se)
            self.w(s, "mm", stats_mm)
            ins = s.activation(r2[0:1, 0:N], self.PS[6][0:1, 0:N], AF.Square)
            res["s1"] = self.inc(ins, "se")
        self.SC(sc1)
        def ve(v):
            self.w(v, "se", res["s1"])
            ins = v.scalar_tensor_tensor(r3[0:1, 0:N], self.PS[7][0:1, 0:N], float(eps),
                                         r2[0:1, 0:N], ALU.add, ALU.subtract)
            res["v_stt"] = self.inc(ins, "ve")
            ins = v.tensor_copy(rb0[0:1, 0:N], self.PS[6][0:1, 0:N])
            res["v1"] = self.inc(ins, "ve")
            self.ps_release(6, "ve")
            self.ps_release(7, "ve")
        self.VE(ve)
        def sc2(s):
            self.w(s, "ve", res["v_stt"])
            ins = s.activation(r2[0:1, 0:N], r3[0:1, 0:N], AF.Ln)
            t = self.inc(ins, "se")
            self.w(s, "se", t)
            ins = s.activation(rb1[0:1, 0:N], r2[0:1, 0:N], AF.Exp, scale=-0.5)
            self.inc(ins, "se")
        self.SC(sc2)
        return ("se", self.cnt["se"], "ve", res["v1"])

    def ln_bcast2(self, rbs, c0, n, ready):
        rb0, rb1 = rbs
        res = {}
        def te(te):
            Pm, im = self.ps_acquire(te)
            Pr, ir = self.ps_acquire(te)
            self.ps_held.add(im)
            self.ps_held.add(ir)
            for j in range(0, len(ready), 2):
                self.w(te, ready[j], ready[j + 1])
            te.matmul(Pm[:, 0:n], self.T["ones_row"][:], rb0[0:1, c0:c0 + n],
                      start=True, stop=True)
            ins = te.matmul(Pr[:, 0:n], self.T["ones_row"][:], rb1[0:1, c0:c0 + n],
                            start=True, stop=True)
            self.inc(ins, "mm")
            res["r"] = (Pm, im, Pr, ir, self.cnt["mm"])
        self.TE(te)
        return res["r"]

    def ln_smalls(self, rows, rbs, width, eps):
        r0, r1, r2, r3 = rows
        rb0, rb1 = rbs
        N = width
        res = {}
        entry_ve = self.cnt["ve"]
        entry_se = self.cnt["se"]
        def sc1(s):
            self.w(s, "ve", entry_ve)
            self.w(s, "se", entry_se)
            ins = s.activation(r2[0:1, 0:N], r0[0:1, 0:N], AF.Square)
            res["s1"] = self.inc(ins, "se")
        self.SC(sc1)
        def ve(v):
            self.w(v, "se", res["s1"])
            ins = v.scalar_tensor_tensor(r3[0:1, 0:N], r1[0:1, 0:N], float(eps),
                                         r2[0:1, 0:N], ALU.add, ALU.subtract)
            res["v_stt"] = self.inc(ins, "ve")
            self.w(v, "ve", res["v_stt"])
            ins = v.tensor_copy(rb0[0:1, 0:N], r0[0:1, 0:N])
            res["v1"] = self.inc(ins, "ve")
        self.VE(ve)
        def sc2(s):
            self.w(s, "ve", res["v_stt"])
            ins = s.activation(r2[0:1, 0:N], r3[0:1, 0:N], AF.Ln)
            t = self.inc(ins, "se")
            self.w(s, "se", t)
            ins = s.activation(rb1[0:1, 0:N], r2[0:1, 0:N], AF.Exp, scale=-0.5)
            self.inc(ins, "se")
        self.SC(sc2)
        return ("se", self.cnt["se"], "ve", res["v1"])

    def ln_bcast(self, rbs, c0, n, ready_ve):
        rb0, rb1 = rbs
        res = {}
        def te(te):
            Pm, im = self.ps_acquire(te)
            Pr, ir = self.ps_acquire(te)
            self.ps_held.add(im)
            self.ps_held.add(ir)
            self.w(te, "ve", ready_ve)
            te.matmul(Pm[:, 0:n], self.T["ones_row"][:], rb0[0:1, c0:c0 + n],
                      start=True, stop=True)
            ins = te.matmul(Pr[:, 0:n], self.T["ones_row"][:], rb1[0:1, c0:c0 + n],
                            start=True, stop=True)
            self.inc(ins, "mm")
            res["r"] = (Pm, im, Pr, ir, self.cnt["mm"])
        self.TE(te)
        return res["r"]

    def emit_ln1(self):
        T = self.T
        xv = T["x_fm"][:].rearrange("p (c w) -> p c w", w=W)
        lv = T["ln1"][:].rearrange("p (c w) -> p c w", w=W)
        sq_slots = [self._view(T["sqs0"], 512), self._view(T["sqs1"], 512)]
        cp_slots = [self._view(T["sqs0"], 0), self._view(T["sqs1"], 0)]
        def cbw(ci):
            c0, n = CBS5[ci]
            last_tile = (c0 + n - 1) // 128
            return (("ve", self.xtile_ve[last_tile]),)
        self.ln_stats(CBS5, NCH_D,
                      lambda ch, c0, n: xv[:, ch, c0:c0 + n],
                      sq_slots, copy_slots=cp_slots,
                      first_waits=(("ve", self.ones_ready),),
                      rows=(T["r0w"], T["r1w"]), key="sqs", cb_waits=cbw)
        ready = self.ln_smalls((T["r0w"], T["r1w"], T["r2w"], T["r3w"]),
                               (T["rb0w"], T["rb1w"]), TT, EPS_LN)
        self.ln1_cb_ve = {}
        for (c0, n) in CBS5:
            Pm, im, Pr, ir, mmt = self.ln_bcast2((T["rb0w"], T["rb1w"]), c0, n, ready)
            def ve(v, c0=c0, n=n, Pm=Pm, Pr=Pr, im=im, ir=ir, mmt=mmt):
                self.w(v, "mm", mmt)
                for cc in range(0, NCH_D, 2):
                    ts = []
                    for ch in (cc, cc + 1):
                        ins = v.tensor_sub(lv[:, ch, c0:c0 + n], xv[:, ch, c0:c0 + n],
                                           Pm[:, 0:n])
                        ts.append(self.inc(ins, "ve"))
                    for ch, t in zip((cc, cc + 1), ts):
                        self.w(v, "ve", t)
                        ins = v.tensor_mul(lv[:, ch, c0:c0 + n], lv[:, ch, c0:c0 + n],
                                           Pr[:, 0:n])
                        self.inc(ins, "ve")
                self.ps_release(im, "ve")
                self.ps_release(ir, "ve")
            self.VE(ve)
            self.ln1_cb_ve[c0] = self.cnt["ve"]
        self.ln1_ready = self.cnt["ve"]
        if self.debug:
            def gp(g):
                self.w(g, "ve", self.ln1_ready)
                self.w(g, "dbg", self.cnt["dbg"])
                ins = g.dma_start(out=self.dbg["dbg_ln1"],
                                  in_=T["ln1"][:].rearrange("p (c w) -> p c w", w=W)[:, :, 0:TT])
                self.inc(ins, "dbg", 16)
            self.GP(gp)

    def emit_mix(self):
        T = self.T
        lv = T["ln1"][:].rearrange("p (c w) -> p c w", w=W)
        cv = T["xcyz"][:].rearrange("p (c w) -> p c w", w=W)
        win = [T["win0"], T["win1"]]
        slot_user = [None, None]
        self.xc_se = {}
        self.pz_ve = {}
        rot_save = self.rot
        self.rot = [0, 1, 2]
        xp_started = False
        prev_conv_mm = 0
        dma_i = 0

        for ch in range(16):
            dws = {}
            for mi, m in enumerate((ch, 16 + ch)):
                sl = dma_i % 2
                ssem = "win0" if sl == 0 else "win1"
                def gp(g, m=m, sl=sl, ssem=ssem):
                    if slot_user[sl] is not None:
                        self.w(g, "mm", slot_user[sl])
                    ins = g.dma_start(out=win[sl][:].rearrange("p (k j) -> p k j", j=128),
                                      in_=self.dr["w_in_t"][m])
                    self.inc(ins, ssem, 16)
                self.GP(gp)
                dws[m] = (sl, ssem, self.cnt[ssem])
                dma_i += 1
            if ch == 0:
                def gp_d(g):
                    self.w(g, "ve", self.ln1_ready)
                    ins = g.dma_start(out=T["diag_sb"][:].rearrange(
                        "p (c t j) -> p c t j", c=16, t=4),
                                      in_=self.dr["diag_t"][:])
                    self.inc(ins, "diag", 16)
                self.GP(gp_d)
                self.diag_dw = self.cnt["diag"]

            sl_xs, ssem_xs, dw_xs = dws[ch]
            evacs = []
            V0 = self.cnt["ve"]
            def te_xs(te, sl_xs=sl_xs, ssem_xs=ssem_xs, dw_xs=dw_xs, V0=V0):
                self.w(te, ssem_xs, dw_xs)
                for j, (c0, n) in enumerate(CBS5):
                    self.w(te, "ve", self.ln1_cb_ve[c0])
                    P, pi = self.ps_acquire(te)
                    for k in range(8):
                        ins = te.matmul(P[:, 0:n], win[sl_xs][:, k * 128:(k + 1) * 128],
                                        lv[:, k, c0:c0 + n], start=(k == 0), stop=(k == 7))
                    self.inc(ins, "mm")
                    evacs.append((pi, self.cnt["mm"], c0, n))
                    self.psg[pi] = ("ve", V0 + j + 1)
            self.TE(te_xs)
            slot_user[sl_xs] = self.cnt["mm"]
            def ve_xs(v, evacs=evacs, prev_conv_mm=prev_conv_mm, V0=V0):
                self.w(v, "mm", prev_conv_mm)
                for pi, mmt, c0, n in evacs:
                    self.w(v, "mm", mmt)
                    ins = v.tensor_copy(T["xs_tmp"][:, c0:c0 + n], self.PS[pi][:, 0:n])
                    self.inc(ins, "ve")
                assert self.cnt["ve"] == V0 + len(evacs)
            self.VE(ve_xs)
            xs_ve = self.cnt["ve"]

            cevacs = []
            S0 = self.cnt["se"]
            def te_cv(te, ch=ch, xs_ve=xs_ve, S0=S0):
                self.w(te, "ve", xs_ve)
                self.w(te, "diag", self.diag_dw)
                for j, (c0, n) in enumerate(CBS4):
                    P, pi = self.ps_acquire(te)
                    for tap in range(4):
                        dcol = (ch * 4 + tap) * 128
                        ins = te.matmul(P[:, 0:n], T["diag_sb"][:, dcol:dcol + 128],
                                        T["xs_tmp"][:, c0 - 3 + tap:c0 - 3 + tap + n],
                                        start=(tap == 0), stop=(tap == 3))
                    self.inc(ins, "mm")
                    cevacs.append((pi, self.cnt["mm"], c0, n))
                    self.psg[pi] = ("se", S0 + j + 1)
            self.TE(te_cv)
            prev_conv_mm = self.cnt["mm"]
            self.conv_last_mm = prev_conv_mm
            def sc_cv(s, ch=ch, cevacs=cevacs, S0=S0):
                for pi, mmt, c0, n in cevacs:
                    self.w(s, "mm", mmt)
                    ins = s.activation(cv[:, ch, c0:c0 + n], self.PS[pi][:, 0:n], AF.Silu,
                                       bias=T["conv_b_sb"][:, ch:ch + 1])
                    self.inc(ins, "se")
                assert self.cnt["se"] == S0 + len(cevacs)
            self.SC(sc_cv)
            xc_se = self.cnt["se"]
            self.xc_se[ch] = xc_se

            def te_xp(te, ch=ch, xc_se=xc_se, xp_started=xp_started):
                self.w(te, "se", xc_se)
                for i, (c0, n) in enumerate(CBS4):
                    pi = 3 + i
                    if not xp_started:
                        self.ps_guard_wait(te, pi)
                    wcol = ch * 32
                    ins = te.matmul(self.PS[pi][0:32, 0:n], T["wx_sb"][:, wcol:wcol + 32],
                                    cv[:, ch, c0:c0 + n], start=(ch == 0), stop=(ch == 15),
                                    skip_group_check=True)
                    if ch == 15:
                        self.inc(ins, "mm")
            if ch == 0:
                def te_xp0(te, f=te_xp):
                    self.w(te, "smallw", self.small_w_dw)
                    f(te)
                self.TE(te_xp0)
            else:
                self.TE(te_xp)
            xp_started = True
            if ch == 15:
                self.xp_done_mm = self.cnt["mm"]

            sl_z, ssem_z, dw_z = dws[16 + ch]
            zevacs = []
            S0z = self.cnt["se"]
            def te_z(te, sl_z=sl_z, ssem_z=ssem_z, dw_z=dw_z, S0z=S0z):
                self.w(te, ssem_z, dw_z)
                for j, (c0, n) in enumerate(CBS4):
                    self.w(te, "ve", self.ln1_cb_ve[c0 - 3 + 512])
                    P, pi = self.ps_acquire(te)
                    for k in range(8):
                        ins = te.matmul(P[:, 0:n], win[sl_z][:, k * 128:(k + 1) * 128],
                                        lv[:, k, c0:c0 + n], start=(k == 0), stop=(k == 7))
                    self.inc(ins, "mm")
                    zevacs.append((pi, self.cnt["mm"], c0, n))
                    self.psg[pi] = ("se", S0z + j + 1)
            self.TE(te_z)
            slot_user[sl_z] = self.cnt["mm"]
            prev_pz = self.pz_ve.get(ch - 1, 0)
            def sc_z(s, zevacs=zevacs, prev_pz=prev_pz, S0z=S0z):
                self.w(s, "ve", prev_pz)
                for pi, mmt, c0, n in zevacs:
                    self.w(s, "mm", mmt)
                    ins = s.activation(T["sz_tmp"][:, c0 - 3:c0 - 3 + n],
                                       self.PS[pi][:, 0:n], AF.Silu)
                    self.inc(ins, "se")
                assert self.cnt["se"] == S0z + len(zevacs)
            self.SC(sc_z)
            sz_se = self.cnt["se"]
            def ve_pz(v, ch=ch, sz_se=sz_se):
                self.w(v, "se", sz_se)
                for (c0, n) in CBS4:
                    ins = v.tensor_mul(cv[:, ch, c0:c0 + n], cv[:, ch, c0:c0 + n],
                                       T["sz_tmp"][:, c0 - 3:c0 - 3 + n])
                    self.inc(ins, "ve")
            self.VE(ve_pz)
            self.pz_ve[ch] = self.cnt["ve"]
        self.rot = rot_save
        self.mix_done_mm = self.cnt["mm"]
        self.mix_done_ve = self.cnt["ve"]
        self._dbg_xc_t = 0
        if self.debug:
            def gp(g):
                self.w(g, "ve", self.pz_ve[15])
                self.w(g, "dbg", self.cnt["dbg"])
                ins = g.dma_start(out=self.dbg["dbg_xc"],
                                  in_=T["xcyz"][:].rearrange("p (c w) -> p c w", w=W)[:, :, HALO:TT])
                self._dbg_xc_t = self.inc(ins, "dbg", 16)
            self.GP(gp)

    def emit_dt_a(self):
        T = self.T
        cv = T["xcyz"][:].rearrange("p (c w) -> p c w", w=W)
        def gp(g):
            self.w(g, "mm", self.conv_last_mm)
            ins = g.dma_start(out=T["wdt_sb"][:].rearrange("p (m j) -> p m j", j=128),
                              in_=self.dr["w_dt_t"][:])
            self.inc(ins, "wdt", 16)
        self.GP(gp)
        self.wdt_dw = self.cnt["wdt"]

        def ve_dt(v):
            self.w(v, "mm", self.xp_done_mm)
            for i in range(4):
                ins = v.tensor_copy(T["dt_sb"][0:32, i * 512:(i + 1) * 512],
                                    self.PS[3 + i][0:32, 0:512])
                self.inc(ins, "ve")
                self.ps_release(3 + i, "ve")
        self.VE(ve_dt)
        self.dt_ve = self.cnt["ve"]
        self._a_slot_ve = [0, 0]
        self._a_si = 0

    def emit_a_pair(self, i, c0, n, ch0):
        T = self.T
        cv = T["xcyz"][:].rearrange("p (c w) -> p c w", w=W)
        at = [T["a0"], T["a1"]]
        info = []
        for ch in (ch0, ch0 + 1):
            sl = self._a_si % 2
            self._a_si += 1
            res = {}
            def te(te, ch=ch, i=i, n=n, res=res):
                self.w(te, "ve", self.dt_ve)
                self.w(te, "wdt", self.wdt_dw)
                P, pi = self.ps_acquire(te)
                ins = te.matmul(P[:, 0:n], T["wdt_sb"][:, ch * 128:(ch + 1) * 128],
                                T["dt_sb"][0:32, i * 512:i * 512 + n],
                                start=True, stop=True)
                self.inc(ins, "mm")
                res["r"] = (pi, self.cnt["mm"])
            self.TE(te)
            info.append((ch, sl, res["r"]))
        ln_t = {}
        def sc(s, info=info, n=n):
            pes = []
            for ch, sl, (pi, mmt) in info:
                self.w(s, "mm", mmt)
                self.w(s, "ve", self._a_slot_ve[sl])
                ins = s.activation(self.PS[pi][:, 0:n], self.PS[pi][:, 0:n], AF.Exp,
                                   bias=T["b_dt_sb"][:, ch:ch + 1])
                t = self.inc(ins, "se")
                pes.append((pi, t))
            for (ch, sl, (pi, mmt)), (_, t) in zip(info, pes):
                self.w(s, "se", t)
                ins = s.activation(at[sl][:, 0:n], self.PS[pi][:, 0:n], AF.Ln, bias=1.0)
                ln_t[sl] = self.inc(ins, "se")
                self.ps_release(pi, "se")
        self.SC(sc)
        def ve(v, info=info, c0=c0, n=n):
            if self.debug and self._dbg_xc_t:
                self.w(v, "dbg", self._dbg_xc_t)
            for ch, sl, _ in info:
                self.w(v, "se", ln_t[sl])
                ins = v.tensor_mul(cv[:, ch, c0:c0 + n], cv[:, ch, c0:c0 + n],
                                   at[sl][:, 0:n])
                self._a_slot_ve[sl] = self.inc(ins, "ve")
        self.VE(ve)
        return self.cnt["ve"]

    def emit_op_evac(self, info):
        T = self.T
        lv = T["ln1"][:].rearrange("p (c w) -> p c w", w=W)
        arv = T["arcb"][:].rearrange("p (c w) -> p c w", w=512)
        pi, mmt, m, c0, n = info
        def ve(v):
            self.w(v, "mm", mmt)
            ins = v.tensor_add(arv[:, m, 0:n], self.PS[pi][:, 0:n],
                               lv[:, m, c0:c0 + n])
            self.inc(ins, "ve")
            self.ps_release(pi, "ve")
        self.VE(ve)
        return self.cnt["ve"]

    def emit_op_m(self, ci, c0, n, m, yz_cb_ve, defer_evac=False):
        T = self.T
        cv = T["xcyz"][:].rearrange("p (c w) -> p c w", w=W)
        lv = T["ln1"][:].rearrange("p (c w) -> p c w", w=W)
        arv = T["arcb"][:].rearrange("p (c w) -> p c w", w=512)
        wout = [T["wout0"], T["wout1"]]
        pair = (ci * 8 + m) // 2
        sl = pair % 2
        half = m % 2
        osem = "wout0" if sl == 0 else "wout1"
        if half == 0:
            def gp(g, sl=sl, m=m, osem=osem):
                if self.wout_slot_user[sl] is None:
                    self.w(g, "mm", self.mix_done_mm)
                    self.w(g, "ve", self.mix_done_ve)
                else:
                    self.w(g, "mm", self.wout_slot_user[sl])
                m2 = min(m + 2, 8)
                ins = g.dma_start(
                    out=wout[sl][:].rearrange("p (q k j) -> p q k j", q=2, j=128)[
                        :, 0:m2 - m],
                    in_=self.dr["w_out_t"][m:m2].rearrange("q p k j -> p q k j"))
                self.inc(ins, osem, 16)
            self.GP(gp)
        dw_t = self.cnt[osem]
        res = {}
        def te(te, sl=sl, half=half, c0=c0, n=n, dw_t=dw_t, osem=osem,
               yz_cb_ve=yz_cb_ve):
            self.w(te, osem, dw_t)
            self.w(te, "ve", yz_cb_ve)
            P, pi = self.ps_acquire(te)
            for k in range(16):
                ins = te.matmul(P[:, 0:n],
                                wout[sl][:, half * 2048 + k * 128:half * 2048 + (k + 1) * 128],
                                cv[:, k, c0:c0 + n], start=(k == 0), stop=(k == 15))
            self.inc(ins, "mm")
            res["r"] = (pi, self.cnt["mm"])
        self.TE(te)
        if half == 1:
            self.wout_slot_user[sl] = self.cnt["mm"]
        pi, mmt = res["r"]
        if defer_evac:
            self.ps_held.add(pi)
            return (pi, mmt, m, c0, n)
        return self.emit_op_evac((pi, mmt, m, c0, n))

    def emit_lnmid_stats(self, ci, c0, n, ar_ve, arm_ch=None):
        T = self.T
        arv = T["arcb"][:].rearrange("p (c w) -> p c w", w=512)
        sq_slots = [self._view(T["sqo0"]), self._view(T["sqo1"])]
        if arm_ch is not None:
            self.ln_stats([(0, n)], NCH_D,
                          lambda ch, cc0, nn: arv[:, ch, 0:nn],
                          sq_slots, copy_slots=None,
                          rows=None, key="sqo", sq_vector=True, keep_psum=True,
                          ch_waits=lambda ch: (("ve", arm_ch[ch]),))
        else:
            self.ln_stats([(0, n)], NCH_D,
                          lambda ch, cc0, nn: arv[:, ch, 0:nn],
                          sq_slots, copy_slots=None,
                          first_waits=(("ve", ar_ve),),
                          rows=None, key="sqo", sq_vector=True, keep_psum=True)
        stats_done_mm = self.cnt["mm"]
        return self.ln_smalls2(T["r2o"], T["r3o"], T["rb0o"], T["rb1o"], n, EPS_LN,
                               stats_done_mm)

    def emit_lnmid_apply(self, ci, c0, n, ready):
        T = self.T
        xv = T["x_fm"][:].rearrange("p (c w) -> p c w", w=W)
        x2v = T["x2"][:].rearrange("p (c w) -> p c w", w=W)
        arv = T["arcb"][:].rearrange("p (c w) -> p c w", w=512)
        Pm, im, Pr, ir, mmt = self.ln_bcast2((T["rb0o"], T["rb1o"]), 0, n, ready)
        def ve2(v, ci=ci, c0=c0, n=n, Pm=Pm, Pr=Pr, im=im, ir=ir, mmt=mmt):
            self.w(v, "mm", mmt)
            for cc in range(0, NCH_D, 2):
                x2a = x2v[:, cc, ci * 512:ci * 512 + n]
                x2b = x2v[:, cc + 1, ci * 512:ci * 512 + n]
                t0_ = self.inc(v.tensor_sub(x2a, arv[:, cc, 0:n], Pm[:, 0:n]), "ve")
                t1_ = self.inc(v.tensor_sub(x2b, arv[:, cc + 1, 0:n], Pm[:, 0:n]), "ve")
                self.w(v, "ve", t0_)
                t0_ = self.inc(v.tensor_mul(x2a, x2a, Pr[:, 0:n]), "ve")
                self.w(v, "ve", t1_)
                t1_ = self.inc(v.tensor_mul(x2b, x2b, Pr[:, 0:n]), "ve")
                self.w(v, "ve", t0_)
                self.inc(v.tensor_add(x2a, xv[:, cc, c0:c0 + n], x2a), "ve")
                self.w(v, "ve", t1_)
                self.inc(v.tensor_add(x2b, xv[:, cc + 1, c0:c0 + n], x2b), "ve")
            self.ps_release(im, "ve")
            self.ps_release(ir, "ve")
        self.VE(ve2)
        return self.cnt["ve"]

    def emit_out_mid(self):
        T = self.T
        self.wout_slot_user = [None, None]
        self._tap_prev = 0
        dbg_yz_t = getattr(self, "_dbg_yz_t", 0) if self.debug else 0
        if dbg_yz_t:
            def vew(v):
                self.w(v, "dbg", dbg_yz_t)
            self.VE(vew)
        ar_ve = {}
        yz_ve = {}
        rp = {}
        arm = {}
        for ci in range(5):
            pends = []
            for m in range(8):
                if ci < 4:
                    c0, n = CBS4[ci]
                    yz_ve[ci] = self.emit_a_pair(ci, c0, n, 2 * m)
                if ci >= 1:
                    pc0, pn = CBS4[ci - 1]
                    if m == 0 and ci >= 2:
                        rp[ci - 2] = self.emit_lnmid_stats(ci - 2, *CBS4[ci - 2],
                                                           ar_ve[ci - 2],
                                                           arm_ch=arm[ci - 2])
                    if m == 1 and ci >= 2:
                        self.emit_lnmid_apply(ci - 2, *CBS4[ci - 2], rp[ci - 2])
                        for pend in pends:
                            self.ps_held.discard(pend[0])
                            arm.setdefault(ci - 1, {})[pend[2]] =                                 self.emit_op_evac(pend)
                        pends = []
                    r = self.emit_op_m(ci - 1, pc0, pn, m, yz_ve[ci - 1],
                                       defer_evac=(m == 0 and ci >= 2))
                    if m == 0 and ci >= 2:
                        pends.append(r)
                    else:
                        arm.setdefault(ci - 1, {})[m] = r
                        ar_ve[ci - 1] = r
        rp[3] = self.emit_lnmid_stats(3, CBS4[3][0], CBS4[3][1], ar_ve[3],
                                      arm_ch=arm[3])
        self.emit_lnmid_apply(3, CBS4[3][0], CBS4[3][1], rp[3])
        self.x2_ready = self.cnt["ve"]
        if self.debug:
            def gp(g):
                self.w(g, "ve", self.x2_ready)
                self.w(g, "dbg", self.cnt["dbg"])
                ins = g.dma_start(out=self.dbg["dbg_x2"],
                                  in_=T["x2"][:].rearrange("p (c w) -> p c w", w=W)[:, :, 0:TOK])
                self.inc(ins, "dbg", 16)
            self.GP(gp)

    @staticmethod
    def _view(t, base=0):
        class V:
            def __getitem__(self, key):
                p, cc = key
                return t[p, base + (cc.start or 0):base + cc.stop]
        return V()

    def emit_ffn_head(self, b):
        for p in self.make_ffn_head(b):
            p()

    def make_ffn_head(self, b):
        T = self.T
        x2v = T["x2"][:].rearrange("p (c w) -> p c w", w=W)
        l2buf = T["ln2"] if b % 2 == 0 else T["ln2b"]
        self._l2buf = getattr(self, "_l2buf", {})
        self._l2buf[b] = l2buf
        l2v = l2buf[:].rearrange("p (c w) -> p c w", w=512)
        xmv = T["xm"][:].rearrange("p (e c w) -> p e c w", e=NE, c=NCH_D)
        rbs = (T["rb0f"], T["rb1f"])
        t0 = b * 512
        N = 512
        pieces = []
        hst = {}
        if self.debug and b == 1:
            dtot = self.cnt["dbg"]
            def vew(v):
                self.w(v, "dbg", dtot)
            self.VE(vew)
            def scw(s):
                self.w(s, "dbg", dtot)
            self.SC(scw)

        def p_stats():
            sq_slots = [self._view(T["sqf0"]), self._view(T["sqf1"])]
            self.ln_stats([(0, N)], NCH_D,
                          lambda ch, cc0, nn: x2v[:, ch, t0:t0 + nn],
                          sq_slots, copy_slots=None,
                          first_waits=(("ve", self.x2_ready),),
                          rows=None, key="sqf", sq_vector=True, keep_psum=True)
            hst["stats_mm"] = self.cnt["mm"]
        pieces.append(p_stats)
        def p_smalls():
            hst["ready"] = self.ln_smalls2(T["r2f"], T["r3f"], T["rb0f"], T["rb1f"],
                                           N, EPS_LN, hst["stats_mm"])
        pieces.append(p_smalls)
        def p_bcast_apply():
            ready = hst["ready"]
            hst["ready2"] = ready
            Pm, im, Pr, ir, mmt = self.ln_bcast2(rbs, 0, N, ready)
            def ve_l2(v):
                self.w(v, "mm", mmt)
                for c0_ in range(0, NCH_D, 2):
                    ts = []
                    for ch in (c0_, c0_ + 1):
                        ins = v.tensor_sub(l2v[:, ch, 0:N], x2v[:, ch, t0:t0 + N],
                                           Pm[:, 0:N])
                        ts.append(self.inc(ins, "ve"))
                    for ch, t in zip((c0_, c0_ + 1), ts):
                        self.w(v, "ve", t)
                        ins = v.tensor_mul(l2v[:, ch, 0:N], l2v[:, ch, 0:N],
                                           Pr[:, 0:N])
                        self.inc(ins, "ve")
                self.ps_release(im, "ve")
                self.ps_release(ir, "ve")
            self.VE(ve_l2)
            hst["ln2_ve"] = self.cnt["ve"]
        pieces.append(p_bcast_apply)

        def p_router():
            ready = hst["ready2"]
            res = {}
            def te_r(te):
                self.w(te, "ve", self.x2_ready)
                self.w(te, "smallw", self.small_w_dw)
                P, pi = self.ps_acquire(te)
                for k in range(8):
                    ins = te.matmul(P[0:4, 0:N], T["wrt_sb"][:, 4 * k:4 * k + 4],
                                    x2v[:, k, t0:t0 + N], start=(k == 0), stop=(k == 7))
                self.inc(ins, "mm")
                P2, pi2 = self.ps_acquire(te)
                for j in range(0, len(ready), 2):
                    self.w(te, ready[j], ready[j + 1])
                self.w(te, "ve", self.cs_ve)
                ins = te.matmul(P2[0:4, 0:N], T["cs_row"][0:1, 0:4], rbs[0][0:1, 0:N],
                                start=True, stop=True)
                self.inc(ins, "mm")
                res["r"] = (pi, pi2, self.cnt["mm"])
            self.TE(te_r)
            pi_lg, pi_adj, mmt = res["r"]
            def ve_lg(v):
                self.w(v, "mm", mmt)
                ins = v.tensor_copy(T["lg_sb"][:], self.PS[pi_lg][0:4, 0:N])
                t = self.inc(ins, "ve")
                self.w(v, "ve", t)
                ins = v.tensor_sub(T["lg_sb"][:], T["lg_sb"][:],
                                   self.PS[pi_adj][0:4, 0:N])
                self.inc(ins, "ve")
                self.ps_release(pi_lg, "ve")
                self.ps_release(pi_adj, "ve")
            self.VE(ve_lg)
            hst["lg_ve"] = self.cnt["ve"]
        pieces.append(p_router)

        def p_cascade():
            lg_ve = hst["lg_ve"]
            res2 = {}
            def te_t(te):
                self.w(te, "ve", lg_ve)
                P, pi = self.ps_acquire(te)
                for g in range(4):
                    ins = te.transpose(P[:, 4 * g:4 * g + 4],
                                       T["lg_sb"][:, 128 * g:128 * (g + 1)],
                                       T["id32"][0:4, 0:4])
                self.inc(ins, "mm")
                res2["r"] = (pi, self.cnt["mm"])
            self.TE(te_t)
            pi_tm, mmt = res2["r"]
            def ve_c(v):
                def st(ins):
                    t = self.inc(ins, "ve")
                    self.w(v, "ve", t)
                self.w(v, "mm", mmt)
                st(v.tensor_copy(T["lt_tm"][:], self.PS[pi_tm][:, 0:16]))
                self.ps_release(pi_tm, "ve")
                lt3 = T["lt_tm"][:].rearrange("p (g e) -> p g e", e=4)
                sel3 = T["sel"][:].rearrange("p (g e) -> p g e", e=4)
                st(v.reduce_max(T["rmax"][:], lt3, axis=AX.X))
                st(v.memset(T["notac"][:], 1.0))
                for e in range(4):
                    st(v.tensor_tensor(T["tmp4"][:], lt3[:, :, e], T["rmax"][:],
                                       ALU.is_ge))
                    st(v.tensor_mul(sel3[:, :, e], T["tmp4"][:], T["notac"][:]))
                    if e < 3:
                        st(v.tensor_sub(T["notac"][:], T["notac"][:], sel3[:, :, e]))
            self.VE(ve_c)
            hst["sel_ve"] = self.cnt["ve"]
        pieces.append(p_cascade)

        def p_mask4():
            sel_ve = hst["sel_ve"]
            res3 = {}
            def te_m(te):
                self.w(te, "ve", sel_ve)
                P, pi = self.ps_acquire(te)
                for g in range(4):
                    ins = te.transpose(P[0:4, 128 * g:128 * (g + 1)],
                                       T["sel"][:, 4 * g:4 * g + 4], T["id32"][:])
                self.inc(ins, "mm")
                res3["r"] = (pi, self.cnt["mm"])
            self.TE(te_m)
            pi_m4, mmt = res3["r"]
            def ve_m4(v):
                self.w(v, "mm", mmt)
                ins = v.tensor_copy(T["mask4b"][:], self.PS[pi_m4][0:4, 0:N])
                self.inc(ins, "ve")
                self.ps_release(pi_m4, "ve")
            self.VE(ve_m4)
            hst["mask4_ve"] = self.cnt["ve"]
        pieces.append(p_mask4)

        def mk_mrow(e):
            def p_mrow():
                sel3 = T["sel"][:].rearrange("p (g e) -> p g e", e=4)
                sel_ve = hst["sel_ve"]
                res4 = {}
                def te_e(te, e=e):
                    self.w(te, "ve", sel_ve)
                    P, pi = self.ps_acquire(te)
                    for g in range(4):
                        ins = te.transpose(P[0:1, g * 128:(g + 1) * 128],
                                           sel3[:, g:g + 1, e], T["id32"][:])
                    self.inc(ins, "mm")
                    res4["r"] = (pi, self.cnt["mm"])
                self.TE(te_e)
                pi, mmt = res4["r"]
                def ve_e(v, e=e, pi=pi, mmt=mmt):
                    self.w(v, "mm", mmt)
                    ins = v.tensor_copy(T["mrow4"][0:1, e * 512:(e + 1) * 512],
                                        self.PS[pi][0:1, 0:N])
                    self.inc(ins, "ve")
                    self.ps_release(pi, "ve")
                self.VE(ve_e)
            return p_mrow
        for e in range(4):
            pieces.append(mk_mrow(e))
        def mk_mrep(e):
            def p_mrep():
                mrow_ve = self.cnt["ve"]
                res5 = {}
                def te_e(te, e=e, mrow_ve=mrow_ve):
                    self.w(te, "ve", mrow_ve)
                    P, pi = self.ps_acquire(te)
                    ins = te.matmul(P[:, 0:N], T["ones_row"][:],
                                    T["mrow4"][0:1, e * 512:(e + 1) * 512],
                                    start=True, stop=True)
                    self.inc(ins, "mm")
                    res5["r"] = (pi, self.cnt["mm"])
                self.TE(te_e)
                pi, mmt = res5["r"]
                def ve_e(v, e=e, pi=pi, mmt=mmt):
                    self.w(v, "mm", mmt)
                    ins = v.tensor_copy(T["mrep"][:, e * 512:(e + 1) * 512],
                                        self.PS[pi][:, 0:N])
                    self.inc(ins, "ve")
                    self.ps_release(pi, "ve")
                self.VE(ve_e)
            return p_mrep
        for e in range(4):
            pieces.append(mk_mrep(e))

        def mk_xm(half):
            def p_xm():
                mrep_ve = self.cnt["ve"]
                def ve_xm(v):
                    self.w(v, "ve", mrep_ve)
                    self.w(v, "ve", hst["ln2_ve"])
                    for e in range(2 * half, 2 * half + 2):
                        for ch in range(NCH_D):
                            ins = v.tensor_mul(xmv[:, e, ch, 0:N], l2v[:, ch, 0:N],
                                               T["mrep"][:, e * 512:(e + 1) * 512])
                            self._xm_tile_t[(b, e, ch)] = self.inc(ins, "ve")
                self.VE(ve_xm)
            return p_xm
        pieces.append(mk_xm(0))
        pieces.append(mk_xm(1))
        def p_fin():
            self._ffn_head_t[b] = (self.cnt["ve"], hst["mask4_ve"], 0)
            if self.debug and b == 0:
                def gp(g):
                    self.w(g, "ve", self.cnt["ve"])
                    self.w(g, "dbg", self.cnt["dbg"])
                    ins = g.dma_start(out=self.dbg["dbg_ln2"], in_=l2buf[:])
                    self.inc(ins, "dbg", 16)
                    ins = g.dma_start(out=self.dbg["dbg_mask"], in_=T["mask4b"][:])
                    self.inc(ins, "dbg", 16)
                self.GP(gp)
        pieces.append(p_fin)
        return pieces

    def issue_w1_dma(self, b, m, sl_override=None):
        wf = [self.T["wf0"], self.T["wf1"], self.T["wf2"]]
        if sl_override is None:
            sl = self.ffn_slot_i % 3
            self.ffn_slot_i += 1
        else:
            sl = sl_override
        fsem = f"wf{sl}"
        def gp(g, sl=sl, m=m, fsem=fsem):
            if self.ffn_slot_user[sl] is None:
                self.w(g, "ve", self.x2_ready)
            else:
                self.w(g, "mm", self.ffn_slot_user[sl])
            wv = wf[sl][:].rearrange("p (j c) -> p j c", c=128)
            ins = g.dma_start(out=wv[:, 0:17], in_=self.dr["w1b_t"][m][:, 0:17])
            self.inc(ins, fsem, 16)
            ins = g.dma_start(out=wv[:, 17:33], in_=self.dr["w1b_t"][m][:, 17:33])
            self.inc(ins, fsem, 16)
        self.GP(gp)
        return sl, fsem, self.cnt[fsem]

    def emit_ffn_expert_m(self, b, m):
        T = self.T
        xmv = T["xm"][:].rearrange("p (e c w) -> p e c w", e=NE, c=NCH_D)
        hv = T["hsel"][:].rearrange("p (c w) -> p c w", w=512)
        wf = [T["wf0"], T["wf1"], T["wf2"]]
        N = 512
        xm_ve, mask4_ve, xm_gx = self._ffn_head_t[b]
        if (b, m) in self._wf_pre:
            sl, fsem, dw_t = self._wf_pre.pop((b, m))
        else:
            sl, fsem, dw_t = self.issue_w1_dma(b, m)
        res = {}
        def te(te, sl=sl, dw_t=dw_t, fsem=fsem, m=m, b=b):
            self.w(te, fsem, dw_t)
            if m == 0:
                self.w(te, "ve", mask4_ve)
            else:
                self.w(te, "ve", max(xm_ve, mask4_ve))
            self.w(te, "gx", xm_gx)
            P, pi = self.ps_acquire(te)
            first = True
            last_w = mask4_ve
            for e in range(4):
                for k in range(8):
                    j = e * 8 + k
                    if m == 0:
                        t = self._xm_tile_t[(b, e, k)]
                        if t > last_w:
                            self.w(te, "ve", t)
                            last_w = t
                    te.matmul(P[:, 0:N], wf[sl][:, j * 128:(j + 1) * 128],
                              xmv[:, e, k, 0:N], start=first, stop=False)
                    first = False
            ins = te.matmul(P[:, 0:N], wf[sl][0:4, 32 * 128:32 * 128 + 128],
                            T["mask4b"][0:4, 0:N], start=False, stop=True)
            self.inc(ins, "mm")
            res["r"] = (pi, self.cnt["mm"])
        self.TE(te)
        self.ffn_slot_user[sl] = self.cnt["mm"]
        pi, mmt = res["r"]
        def sc(s, pi=pi, mmt=mmt, m=m):
            self.w(s, "mm", mmt)
            ins = s.activation(hv[:, m, 0:N], self.PS[pi][:, 0:N], AF.Gelu)
            self.inc(ins, "se")
            self.ps_release(pi, "se")
        self.SC(sc)

    def emit_ffn_wo(self, b, head_pieces=None):
        T = self.T
        hv = T["hsel"][:].rearrange("p (c w) -> p c w", w=512)
        yv = T["yffn"][:].rearrange("p (c w) -> p c w", w=512)
        l2v = self._l2buf[b][:].rearrange("p (c w) -> p c w", w=512)
        wf = [T["wf0"], T["wf1"], T["wf2"]]
        N = 512
        hsel_se = self.cnt["se"]
        if self.debug and b == 0:
            def gpd(g):
                self.w(g, "se", hsel_se)
                self.w(g, "dbg", self.cnt["dbg"])
                ins = g.dma_start(out=self.dbg["dbg_hsel"], in_=T["hsel"][:])
                self.inc(ins, "dbg", 16)
            self.GP(gpd)
        i0 = self.ffn_slot_i
        OFF = [0, 1, 2, 0, 1, 2, 1, 2]
        for mo in range(8):
            sl = (i0 + OFF[mo]) % 3
            fsem = f"wf{sl}"
            def gp(g, sl=sl, mo=mo, fsem=fsem):
                self.w(g, "mm", self.ffn_slot_user[sl])
                wv = wf[sl][:, 0:32 * 128].rearrange("p (j c) -> p j c", c=128)
                ins = g.dma_start(out=wv[:, 0:16], in_=self.dr["wo_t"][mo][:, 0:16])
                self.inc(ins, fsem, 16)
                ins = g.dma_start(out=wv[:, 16:32], in_=self.dr["wo_t"][mo][:, 16:32])
                self.inc(ins, fsem, 16)
            self.GP(gp)
            dw_t = self.cnt[fsem]
            res = {}
            def te(te, sl=sl, dw_t=dw_t, fsem=fsem):
                self.w(te, fsem, dw_t)
                self.w(te, "se", hsel_se)
                P, pi = self.ps_acquire(te)
                for k in range(32):
                    ins = te.matmul(P[:, 0:N], wf[sl][:, k * 128:(k + 1) * 128],
                                    hv[:, k, 0:N], start=(k == 0), stop=(k == 31))
                self.inc(ins, "mm")
                res["r"] = (pi, self.cnt["mm"])
            self.TE(te)
            self.ffn_slot_user[sl] = self.cnt["mm"]
            pi, mmt = res["r"]
            dbg_now = self.cnt["dbg"] if self.debug else 0
            def ve(v, pi=pi, mmt=mmt, mo=mo, dbg_now=dbg_now):
                self.w(v, "mm", mmt)
                if mo == 0 and dbg_now:
                    self.w(v, "dbg", dbg_now)
                ins = v.scalar_tensor_tensor(yv[:, mo, 0:N], self.PS[pi][:, 0:N],
                                             T["bo_sb"][:, mo:mo + 1], l2v[:, mo, 0:N],
                                             ALU.add, ALU.add)
                self.inc(ins, "ve")
                self.ps_release(pi, "ve")
            self.VE(ve)
            if head_pieces:
                head_pieces.pop(0)()
            if mo >= 1 and head_pieces:
                head_pieces.pop(0)()
        if b + 1 < NBLK:
            for m in range(3):
                self._wf_pre[(b + 1, m)] = self.issue_w1_dma(
                    b + 1, m, sl_override=(i0 + m) % 3)
        self.ffn_slot_i = i0 + 9
        self._ffn_y_ve[b] = self.cnt["ve"]
        if self.debug and b == 0:
            def gpd2(g):
                self.w(g, "ve", self._ffn_y_ve[b])
                self.w(g, "dbg", self.cnt["dbg"])
                ins = g.dma_start(out=self.dbg["dbg_y"], in_=T["yffn"][:])
                self.inc(ins, "dbg", 16)
            self.GP(gpd2)

    def make_ffn_tail(self, b):
        T = self.T
        yv = T["yffn"][:].rearrange("p (c w) -> p c w", w=512)
        frv = T["fret"][:].rearrange("p (c w) -> p c w", w=512)
        x2v = T["x2"][:].rearrange("p (c w) -> p c w", w=W)
        ov = T["out_tm"][:].rearrange("p (g f) -> p g f", f=1024)
        rows = (T["r0f"], T["r1f"], T["r2f"], T["r3f"])
        rbs = (T["rb0f"], T["rb1f"])
        t0 = b * 512
        N = 512
        y_ve = self._ffn_y_ve[b]
        st = {}
        pieces = []

        sqf = [T["sqf0"], T["sqf1"]]
        st["aff_sq_t"] = {}
        st["aff_sqmm"] = {}
        def mk_aff_feed(cc):
            def p(cc=cc):
                prev = self._slot_guard.get("sqf", [0, 0])
                def ve_f(v):
                    for ch in (cc, cc + 1):
                        g = prev[ch % 2] if ch < 2 else st["aff_sqmm"][ch - 2]
                        self.w(v, "mm", g)
                        if ch == 0:
                            self.w(v, "ve", y_ve)
                        ins = v.tensor_mul(sqf[ch % 2][:, 0:N], yv[:, ch, 0:N],
                                           yv[:, ch, 0:N])
                        st["aff_sq_t"][ch] = self.inc(ins, "ve")
                self.VE(ve_f)
            return p
        def mk_aff_mm(cc):
            def p(cc=cc):
                def te_a(te):
                    for ch in (cc, cc + 1):
                        self.w(te, "ve", st["aff_sq_t"][ch])
                        if ch == 0:
                            self.ps_guard_wait(te, 6)
                            self.ps_guard_wait(te, 7)
                        te.matmul(self.PS[6][0:1, 0:N], T["inv_colf"][:],
                                  yv[:, ch, 0:N],
                                  start=(ch == 0), stop=(ch == NCH_D - 1),
                                  skip_group_check=True)
                        ins = te.matmul(self.PS[7][0:1, 0:N], T["inv_col"][:],
                                        sqf[ch % 2][:, 0:N],
                                        start=(ch == 0), stop=(ch == NCH_D - 1),
                                        skip_group_check=True)
                        st["aff_sqmm"][ch] = self.inc(ins, "mm")
                self.TE(te_a)
                if cc == NCH_D - 2:
                    self._slot_guard["sqf"] = [st["aff_sqmm"][NCH_D - 2],
                                               st["aff_sqmm"][NCH_D - 1]]
                    st["aff_stats_mm"] = self.cnt["mm"]
            return p
        for cc in range(0, NCH_D, 2):
            pieces.append(mk_aff_feed(cc))
            pieces.append(mk_aff_mm(cc))

        def p_aff_smalls():
            st["aff_ready"] = self.ln_smalls2(T["r2f"], T["r3f"], T["rb0f"], T["rb1f"],
                                              N, EPS_AFF, st["aff_stats_mm"])
        pieces.append(p_aff_smalls)

        def p_aff_bcast():
            st["aff_bc"] = self.ln_bcast2(rbs, 0, N, st["aff_ready"])
        pieces.append(p_aff_bcast)

        def p_aff_apply():
            Pm2, im2, Pr2, ir2, mmt2 = st["aff_bc"]
            mul_t = [0] * NCH_D
            def ve_all(v):
                self.w(v, "mm", mmt2)
                for c0_ in range(0, NCH_D, 2):
                    ts = []
                    for ch in (c0_, c0_ + 1):
                        ins = v.tensor_sub(frv[:, ch, 0:N], yv[:, ch, 0:N],
                                           Pm2[:, 0:N])
                        ts.append(self.inc(ins, "ve"))
                    for ch, t in zip((c0_, c0_ + 1), ts):
                        self.w(v, "ve", t)
                        ins = v.tensor_mul(frv[:, ch, 0:N], frv[:, ch, 0:N],
                                           Pr2[:, 0:N])
                        mul_t[ch] = self.inc(ins, "ve")
                self.ps_release(im2, "ve")
                self.ps_release(ir2, "ve")
            self.VE(ve_all)
            def sc_all(s):
                for ch in range(NCH_D):
                    self.w(s, "ve", mul_t[ch])
                    ins = s.activation(frv[:, ch, 0:N], frv[:, ch, 0:N], AF.Identity,
                                       scale=T["gam_sb"][:, ch:ch + 1],
                                       bias=T["bet_sb"][:, ch:ch + 1])
                    self.inc(ins, "se")
            self.SC(sc_all)
            st["fr_se"] = self.cnt["se"]
        pieces.append(p_aff_apply)

        st["out_sq_t"] = {}
        st["out_sqmm"] = {}
        def mk_out_feed(cc):
            def p(cc=cc):
                prev = self._slot_guard.get("sqf", [0, 0])
                def ve_f(v):
                    for ch in (cc, cc + 1):
                        g = prev[ch % 2] if ch < 2 else st["out_sqmm"][ch - 2]
                        self.w(v, "mm", g)
                        if ch == 0:
                            self.w(v, "se", st["fr_se"])
                        ins = v.tensor_mul(sqf[ch % 2][:, 0:N], frv[:, ch, 0:N],
                                           frv[:, ch, 0:N])
                        st["out_sq_t"][ch] = self.inc(ins, "ve")
                self.VE(ve_f)
            return p
        def mk_out_mm(cc):
            def p(cc=cc):
                def te_o(te):
                    for ch in (cc, cc + 1):
                        self.w(te, "ve", st["out_sq_t"][ch])
                        if ch == 0:
                            self.ps_guard_wait(te, 6)
                            self.ps_guard_wait(te, 7)
                        te.matmul(self.PS[6][0:1, 0:N], T["inv_col"][:],
                                  frv[:, ch, 0:N],
                                  start=(ch == 0), stop=(ch == NCH_D - 1),
                                  skip_group_check=True)
                        ins = te.matmul(self.PS[7][0:1, 0:N], T["inv_col"][:],
                                        sqf[ch % 2][:, 0:N],
                                        start=(ch == 0), stop=(ch == NCH_D - 1),
                                        skip_group_check=True)
                        st["out_sqmm"][ch] = self.inc(ins, "mm")
                self.TE(te_o)
                if cc == NCH_D - 2:
                    self._slot_guard["sqf"] = [st["out_sqmm"][NCH_D - 2],
                                               st["out_sqmm"][NCH_D - 1]]
                    st["out_stats_mm"] = self.cnt["mm"]
            return p
        for cc in range(0, NCH_D, 2):
            pieces.append(mk_out_feed(cc))
            pieces.append(mk_out_mm(cc))

        def p_out_smalls():
            st["out_ready"] = self.ln_smalls2(T["r2f"], T["r3f"], T["rb0f"], T["rb1f"],
                                              N, EPS_LN, st["out_stats_mm"])
        pieces.append(p_out_smalls)

        def p_out_bcast():
            st["out_bc"] = self.ln_bcast2(rbs, 0, N, st["out_ready"])
        pieces.append(p_out_bcast)

        prev_tp = [0, 0]
        def mk_out_pair(cc):
            def p_out_pair():
                Pm3, im3, Pr3, ir3, mmt4 = st["out_bc"]
                bufs = (T["x3t"], T["aff_tap"])
                res9 = {}
                def ve(v, cc=cc):
                    if cc == 0:
                        self.w(v, "mm", mmt4)
                        if b > 0:
                            self.w(v, "dx", self.out_dx_prev)
                    if prev_tp[0]:
                        self.w(v, "mm", prev_tp[0])
                    if prev_tp[1]:
                        self.w(v, "mm", prev_tp[1])
                    ts = []
                    for j, ch in enumerate((cc, cc + 1)):
                        ins = v.tensor_sub(bufs[j][:, 0:N], frv[:, ch, 0:N],
                                           Pm3[:, 0:N])
                        ts.append(self.inc(ins, "ve"))
                    for j in range(2):
                        self.w(v, "ve", ts[j])
                        ins = v.tensor_mul(bufs[j][:, 0:N], bufs[j][:, 0:N],
                                           Pr3[:, 0:N])
                        ts[j] = self.inc(ins, "ve")
                    for j, ch in enumerate((cc, cc + 1)):
                        self.w(v, "ve", ts[j])
                        ins = v.tensor_add(bufs[j][:, 0:N], x2v[:, ch, t0:t0 + N],
                                           bufs[j][:, 0:N])
                        ts[j] = self.inc(ins, "ve")
                    res9["ve"] = tuple(ts)
                    if cc == NCH_D - 2:
                        self.ps_release(im3, "ve")
                        self.ps_release(ir3, "ve")
                self.VE(ve)
                tps = []
                for j, ch in enumerate((cc, cc + 1)):
                    res10 = {}
                    def te(te, j=j, res10=res10):
                        self.w(te, "ve", res9["ve"][j])
                        P, pi = self.ps_acquire(te)
                        for g in range(4):
                            ins = te.transpose(P[:, g * 128:(g + 1) * 128],
                                               bufs[j][:, g * 128:(g + 1) * 128],
                                               T["id32"][:])
                        self.inc(ins, "mm")
                        res10["mm"] = (pi, self.cnt["mm"])
                    self.TE(te)
                    pi, mmt5 = res10["mm"]
                    prev_tp[j] = mmt5
                    def ve2(v, ch=ch, pi=pi, mmt5=mmt5):
                        self.w(v, "mm", mmt5)
                        src = self.PS[pi][:, 0:512].rearrange("p (g f) -> p g f", g=4)
                        ins = v.tensor_copy(ov[:, :, ch * 128:(ch + 1) * 128], src)
                        self.inc(ins, "ve")
                        self.ps_release(pi, "ve")
                    self.VE(ve2)
            return p_out_pair
        for cc in range(0, NCH_D, 2):
            pieces.append(mk_out_pair(cc))

        def p_store():
            out_ve = self.cnt["ve"]
            def sy(s, b=b, out_ve=out_ve):
                self.w(s, "ve", out_ve)
                for g in range(4):
                    r0_ = b * 512 + g * 128
                    ins = s.dma_start(out=self.dr["y_out"][r0_:r0_ + 128, :],
                                      in_=self.T["out_tm"][:, g * 1024:(g + 1) * 1024])
                    self.inc(ins, "dx", 16)
            self.SY(sy)
            self.out_dx_prev = self.cnt["dx"]
        pieces.append(p_store)
        return pieces

    def emit_ffn_all(self):
        self._ffn_head_t = {}
        self._ffn_y_ve = {}
        self._wf_pre = {}
        self._xm_tile_t = {}
        for m in range(3):
            self._wf_pre[(0, m)] = self.issue_w1_dma(0, m)
        self.emit_ffn_head(0)
        tail = []
        for b in range(NBLK):
            for m in range(NCH_F):
                self.emit_ffn_expert_m(b, m)
                if tail:
                    tail.pop(0)()
            while tail:
                tail.pop(0)()
            heads = self.make_ffn_head(b + 1) if b + 1 < NBLK else []
            self.emit_ffn_wo(b, heads)
            while heads:
                heads.pop(0)()
            tail = self.make_ffn_tail(b)
        while tail:
            tail.pop(0)()

    _slot_guard = None


Kern._slot_guard = {}


def _run(debug, inputs):
    packed = pack_weights(inputs)
    x = np.asarray(inputs["L0_attn_x"], np.float32).reshape(NCORE * TOK, D)
    in_maps = []
    for c in range(NCORE):
        lo = c * TOK
        xl = np.zeros((TT, D), np.float32)
        if lo % 8192 != 0:
            xl[:HALO] = x[lo - HALO:lo]
        xl[HALO:] = x[lo:lo + TOK]
        m = {"x_loc": np.ascontiguousarray(xl)}
        m.update(packed)
        in_maps.append(m)
    kb = Kern(debug=debug)
    kb._slot_guard = {}
    nc = kb.build()
    res = run_bass_kernel_spmd(nc, in_maps, core_ids=list(range(NCORE)))
    return res


def kernel(**inputs):
    res = _run(False, inputs)
    out = np.concatenate([r["y_out"] for r in res.results], axis=0)
    return out.reshape(2, 8192, D).astype(np.float32)
